# version 02
# baseline (speedup 1.0000x reference)
import numpy as np
import concourse.bass as bass
import concourse.mybir as mybir
from concourse.bass_utils import run_bass_kernel_spmd

F32 = mybir.dt.float32
BF16 = mybir.dt.bfloat16
I32 = mybir.dt.int32
AF = mybir.ActivationFunctionType
ALU = mybir.AluOpType
AX = mybir.AxisListType

D = 1024
T = 4096
KD = 8
NSLAB = 8
SLAB = 512
NTT = 32
NCORES = 8

N_BLOCKS = 4
LAST_PROG = None
DEV_DUMP = False
DEV_TAP = None

BIGNEG = -3.0e38
RC1 = 98304.0
RC2 = RC1 + 384.0


class V:
    __slots__ = ("ap", "keys")

    def __init__(self, ap, keys):
        self.ap = ap
        self.keys = frozenset(keys)

    def __getitem__(self, sl):
        return V(self.ap.__getitem__(sl), self.keys)


class Arena:
    def __init__(self, nc, name, ncols, dtype=F32, gran=512):
        self.name = name
        self.gran = gran
        self.ncols = ncols
        self.t = nc.alloc_sbuf_tensor(name, [128, ncols], dtype)

    def v(self, col0, ncols, r0=0, r1=128):
        assert col0 + ncols <= self.ncols, (self.name, col0, ncols)
        ap = self.t.ap()[r0:r1, col0:col0 + ncols]
        g0 = col0 // self.gran
        g1 = (col0 + ncols - 1) // self.gran
        return V(ap, ((self.name, g) for g in range(g0, g1 + 1)))

    def v3(self, col0, nch, ns):
        vv = self.v(col0, nch * ns)
        return V(vv.ap.rearrange("p (c t) -> p c t", t=ns), vv.keys)


class Psum:
    def __init__(self, nc, name):
        self.name = name
        self.t = nc.alloc_psum_tensor(name, [128, 512], F32)

    def v(self, col0=0, ncols=512, r0=0, r1=128):
        return V(self.t.ap()[r0:r1, col0:col0 + ncols], [(self.name,)])


class Dram:
    def __init__(self, nc, name, shape, dtype=F32, kind="Internal"):
        self.name = name
        self.shape = shape
        self.t = nc.dram_tensor(name, list(shape), dtype, kind=kind)

    def v(self, *slices, key=None):
        ap = self.t.ap()
        if slices:
            ap = ap.__getitem__(tuple(slices))
        if key is None:
            keys = [(self.name,)]
        elif isinstance(key, list):
            keys = [(self.name,) + (k if isinstance(k, tuple) else (k,)) for k in key]
        else:
            keys = [(self.name,) + (key if isinstance(key, tuple) else (key,))]
        return V(ap, keys)

    def cs(self, c, s, nc_=128, ns=SLAB):
        return self.v(slice(c * nc_, (c + 1) * nc_),
                      slice(s * ns, (s + 1) * ns), key=(c, s))

    def slab3(self, c0, nch, s, ns=SLAB, key_s=None):
        ap = self.t.ap().rearrange("(c p) t -> p c t", p=128)
        ap = ap[:, c0:c0 + nch, s * ns:(s + 1) * ns]
        ks = s if key_s is None else key_s
        return V(ap, [(self.name, c, ks) for c in range(c0, c0 + nch)])

    def whole(self):
        return self.v()


class Prog:
    CE = ("te", "ve", "se", "ge", "sy")

    def __init__(self, nc):
        self.nc = nc
        self.ops = []
        self.state = {}

    def op(self, eng, fn, reads=(), writes=(), kind="c"):
        rkeys = [r.keys for r in reads]
        wkeys = [w.keys for w in writes]
        deps = set()
        for kset in rkeys:
            for k in kset:
                st = self.state.get(k)
                if st and st[0] is not None:
                    deps.add(st[0])
        for kset in wkeys:
            for k in kset:
                st = self.state.get(k)
                if st:
                    if st[0] is not None:
                        deps.add(st[0])
                    deps.update(st[1])
        oid = len(self.ops)
        deps.discard(oid)
        self.ops.append(dict(eng=eng, kind=kind, fn=fn, deps=deps))
        for kset in rkeys:
            for k in kset:
                self.state.setdefault(k, [None, set()])[1].add(oid)
        for kset in wkeys:
            for k in kset:
                self.state[k] = [oid, set()]
        return oid

    def dma(self, queue, dst, src, **kw):
        nc = self.nc
        eng = {"sy": nc.sync, "se": nc.scalar, "ge": nc.gpsimd}[queue]
        return self.op(queue, lambda: eng.dma_start(out=dst.ap, in_=src.ap, **kw),
                       reads=[src], writes=[dst], kind="d")

    def mm(self, out, lhsT, rhs, start, stop):
        nc = self.nc
        return self.op("te", lambda: nc.tensor.matmul(
            out.ap, lhsT.ap, rhs.ap, start=start, stop=stop),
            reads=[lhsT, rhs], writes=[out])

    def transp(self, out, in_, ident):
        nc = self.nc
        return self.op("te", lambda: nc.tensor.transpose(out.ap, in_.ap, ident.ap),
                       reads=[in_, ident], writes=[out])

    def tt(self, eng, out, a, b, alu):
        nc = self.nc
        e = {"ve": nc.vector, "ge": nc.gpsimd}[eng]
        return self.op(eng, lambda: e.tensor_tensor(out.ap, a.ap, b.ap, alu),
                       reads=[a, b], writes=[out])

    def ts(self, eng, out, a, s1, s2, op1, op2=None):
        nc = self.nc
        e = {"ve": nc.vector, "ge": nc.gpsimd}[eng]
        reads = [a] + [x for x in (s1, s2) if isinstance(x, V)]
        s1a = s1.ap if isinstance(s1, V) else s1
        s2a = s2.ap if isinstance(s2, V) else s2
        if op2 is None:
            return self.op(eng, lambda: e.tensor_scalar(out.ap, a.ap, s1a, None, op1),
                           reads=reads, writes=[out])
        return self.op(eng, lambda: e.tensor_scalar(out.ap, a.ap, s1a, s2a, op1, op2),
                       reads=reads, writes=[out])

    def stt(self, eng, out, in0, scalar, in1, op0, op1):
        nc = self.nc
        e = {"ve": nc.vector, "ge": nc.gpsimd}[eng]
        sa = scalar.ap if isinstance(scalar, V) else scalar
        reads = [in0, in1] + ([scalar] if isinstance(scalar, V) else [])
        return self.op(eng, lambda: e.scalar_tensor_tensor(
            out.ap, in0.ap, sa, in1.ap, op0, op1), reads=reads, writes=[out])

    def act(self, out, in_, func, bias=0.0, scale=1.0, accum=None):
        nc = self.nc
        reads = [in_] + [x for x in (bias, scale) if isinstance(x, V)]
        ba = bias.ap if isinstance(bias, V) else bias
        sa = scale.ap if isinstance(scale, V) else scale
        writes = [out] + ([accum] if accum is not None else [])
        if accum is None:
            return self.op("se", lambda: nc.scalar.activation(
                out.ap, in_.ap, func, bias=ba, scale=sa), reads=reads, writes=writes)
        return self.op("se", lambda: nc.scalar.activation(
            out.ap, in_.ap, func, bias=ba, scale=sa, accum_out=accum.ap),
            reads=reads, writes=writes)

    def copy(self, eng, out, in_):
        nc = self.nc
        if eng == "se":
            return self.op("se", lambda: nc.scalar.copy(out.ap, in_.ap),
                           reads=[in_], writes=[out])
        e = {"ve": nc.vector, "ge": nc.gpsimd}[eng]
        return self.op(eng, lambda: e.tensor_copy(out.ap, in_.ap),
                       reads=[in_], writes=[out])

    def red(self, out, in_, alu, negate=False, absval=False):
        nc = self.nc
        return self.op("ve", lambda: nc.vector.tensor_reduce(
            out.ap, in_.ap, AX.X, alu,
            apply_absolute_value=absval, negate=negate),
            reads=[in_], writes=[out])

    def memset(self, eng, out, val):
        nc = self.nc
        e = {"ve": nc.vector, "ge": nc.gpsimd}[eng]
        return self.op(eng, lambda: e.memset(out.ap, val), reads=[], writes=[out])

    def recip(self, out, in_):
        nc = self.nc
        return self.op("ve", lambda: nc.vector.reciprocal(out.ap, in_.ap),
                       reads=[in_], writes=[out])

    def scan(self, out, data0, data1, initial):
        nc = self.nc
        init = initial.ap if isinstance(initial, V) else initial
        reads = [data0, data1] + ([initial] if isinstance(initial, V) else [])
        return self.op("ve", lambda: nc.vector.tensor_tensor_scan(
            out.ap, data0.ap, data1.ap, init, ALU.mult, ALU.add),
            reads=reads, writes=[out])

    def max8(self, out, in_):
        nc = self.nc
        return self.op("ve", lambda: nc.vector.max(out=out.ap, in_=in_.ap),
                       reads=[in_], writes=[out])

    def match_replace(self, out, to_replace, values, imm):
        nc = self.nc
        return self.op("ve", lambda: nc.vector.match_replace(
            out=out.ap, in_to_replace=to_replace.ap, in_values=values.ap,
            imm_value=imm), reads=[to_replace, values], writes=[out])

    def collective_max(self, ins, outs):
        nc = self.nc
        return self.op("ge", lambda: nc.gpsimd.collective_compute(
            "AllReduce", ALU.max, replica_groups=[list(range(NCORES))],
            ins=[ins.ap.opt()], outs=[outs.ap.opt()]),
            reads=[ins], writes=[outs], kind="cc")

    def finalize(self, block):
        nc = self.nc
        streams = {e: [] for e in self.CE}
        for oid, o in enumerate(self.ops):
            streams[o["eng"]].append(oid)
        need_sig = set()
        for o in self.ops:
            need_sig.update(o["deps"])
        sems = {}
        for e in self.CE:
            sems[e + "_c"] = nc.alloc_semaphore(name="s_" + e)
            sems[e + "_d"] = nc.alloc_semaphore(name="d_" + e)
        sigval = {}
        dcount = {e: 0 for e in self.CE}
        ccount = {e: 0 for e in self.CE}
        for e in self.CE:
            for oid in streams[e]:
                o = self.ops[oid]
                if o["kind"] == "d":
                    dcount[e] += 1
                    sigval[oid] = (e + "_d", 16 * dcount[e])
                elif oid in need_sig:
                    ccount[e] += 1
                    sigval[oid] = (e + "_c", ccount[e])

        def mk_body(e):
            def body(eng):
                know = {}
                for oid in streams[e]:
                    o = self.ops[oid]
                    for dd in sorted(o["deps"]):
                        sname, val = sigval[dd]
                        if know.get(sname, 0) < val:
                            eng.wait_ge(sems[sname], val)
                            know[sname] = val
                    ins = o["fn"]()
                    if o["kind"] == "d":
                        sname, val = sigval[oid]
                        ins.then_inc(sems[sname], 16)
                        if oid in need_sig:
                            eng.wait_ge(sems[sname], val)
                            know[sname] = val
                    elif oid in need_sig:
                        if e == "te":
                            eng.drain()
                            eng.nop().then_inc(sems[sigval[oid][0]], 1)
                        else:
                            ins.then_inc(sems[sigval[oid][0]], 1)
                if dcount[e]:
                    eng.wait_ge(sems[e + "_d"], 16 * dcount[e])
            return body

        block.tensor(mk_body("te"))
        block.vector(mk_body("ve"))
        block.scalar(mk_body("se"))
        block.gpsimd(mk_body("ge"))
        block.sync(mk_body("sy"))


CONSTS = {
    "eps_ln": 1e-5,
    "one": 1.0,
}
CIDX = {k: i for i, k in enumerate(CONSTS)}


def _bias_cols(b):
    b = np.asarray(b, np.float32).reshape(-1)
    n = b.size // 128
    return np.ascontiguousarray(b.reshape(n, 128).T)


def _rope_tables():
    half = 32
    theta = 1.0 / (10000.0 ** (np.arange(half, dtype=np.float64) / half))
    ang = np.arange(T, dtype=np.float64)[:, None] * theta[None, :]
    sin = np.sin(ang).astype(np.float32).T
    cos = np.cos(ang).astype(np.float32).T
    C = np.tile(cos, (4, 1))
    S = np.tile(np.concatenate([-sin, sin], axis=0), (2, 1))
    return np.ascontiguousarray(C), np.ascontiguousarray(S)


def _diag_masks():
    out = []
    for off in range(0, 512, 128):
        q = off + np.arange(128)[:, None]
        k = np.arange(512)[None, :]
        out.append(np.where(k <= q, np.float32(0.0), np.float32(BIGNEG)))
    return np.ascontiguousarray(np.stack(out).astype(np.float32))


def host_prepare(inputs):
    x = np.asarray(inputs["L0_attn_x"], np.float32)
    h = {}
    perm = np.zeros(D, np.int64)
    for hh in range(16):
        b0 = hh * 64
        perm[b0:b0 + 32] = b0 + np.arange(0, 64, 2)
        perm[b0 + 32:b0 + 64] = b0 + np.arange(1, 64, 2)
    h["l0_wqT"] = np.ascontiguousarray(np.asarray(inputs["L0_attn_W_q"], np.float32).T[:, perm])
    h["l0_wkT"] = np.ascontiguousarray(np.asarray(inputs["L0_attn_W_k"], np.float32).T[:, perm])
    h["l0_wvT"] = np.ascontiguousarray(np.asarray(inputs["L0_attn_W_v"], np.float32).T)
    h["l0_woT"] = np.ascontiguousarray(np.asarray(inputs["L0_attn_W_o"], np.float32).T)
    h["l0_bq"] = _bias_cols(np.asarray(inputs["L0_attn_b_q"], np.float32)[perm])
    h["l0_bk"] = _bias_cols(np.asarray(inputs["L0_attn_b_k"], np.float32)[perm])
    h["l0_bv_row"] = np.asarray(inputs["L0_attn_b_v"], np.float32).reshape(1, D).copy()
    h["l0_bo"] = _bias_cols(inputs["L0_attn_b_o"])
    C, S = _rope_tables()
    h["ropeCq"] = np.ascontiguousarray(C * np.float32(0.125))
    h["ropeSq"] = np.ascontiguousarray(S * np.float32(0.125))
    h["ropeCk"] = C
    h["ropeSk"] = S
    h["dmask"] = _diag_masks()
    h["l0_win"] = np.ascontiguousarray(np.asarray(inputs["L0_ffn_W_in"], np.float32))
    h["l0_bin"] = _bias_cols(inputs["L0_ffn_b_in"])
    h["l0_wout"] = np.ascontiguousarray(np.asarray(inputs["L0_ffn_W_out"], np.float32))
    h["l0_bout"] = _bias_cols(inputs["L0_ffn_b_out"])
    h["ident"] = np.eye(128, dtype=np.float32)
    h["ones"] = np.ones((128, 1), np.float32)
    cv = np.zeros((128, 8), np.float32)
    for k, i in CIDX.items():
        cv[:, i] = CONSTS[k]
    h["cvec"] = np.ascontiguousarray(cv)
    h["zeros64"] = np.zeros((1, 64), np.float32)
    for d_ in ("fwd", "bwd"):
        wz = np.asarray(inputs[f"L1_attn_{d_}_Wz"], np.float32)
        wh = np.asarray(inputs[f"L1_attn_{d_}_Wh"], np.float32)
        h[f"l1a_wz_{d_}"] = np.ascontiguousarray(
            np.concatenate([wz[i].T for i in range(4)], axis=0))
        h[f"l1a_wh_{d_}"] = np.ascontiguousarray(
            np.concatenate([wh[i].T for i in range(4)], axis=0))
        bz = np.asarray(inputs[f"L1_attn_{d_}_bz"], np.float32)
        bh = np.asarray(inputs[f"L1_attn_{d_}_bh"], np.float32)
        h[f"l1a_bz_{d_}"] = np.ascontiguousarray(
            np.concatenate([_bias_cols(bz[i]) for i in range(4)], axis=1))
        h[f"l1a_bh_{d_}"] = np.ascontiguousarray(
            np.concatenate([_bias_cols(bh[i]) for i in range(4)], axis=1))
    h["l1a_fusionT"] = np.ascontiguousarray(
        np.asarray(inputs["L1_attn_fusion_W"], np.float32).T)
    h["l1a_fb"] = _bias_cols(inputs["L1_attn_fusion_b"])
    h["l1a_g"] = _bias_cols(inputs["L1_attn_ln_gamma"])
    h["l1a_b"] = _bias_cols(inputs["L1_attn_ln_beta"])

    w1 = np.asarray(inputs["L1_ffn_w1"], np.float32)
    h["l1f_w1a"] = np.ascontiguousarray(w1[:D])
    h["l1f_w1b"] = np.ascontiguousarray(w1[D:])
    h["l1f_b1"] = _bias_cols(inputs["L1_ffn_b1"])
    h["l1f_w2"] = np.ascontiguousarray(np.asarray(inputs["L1_ffn_w2"], np.float32))
    h["l1f_b2row"] = np.asarray(inputs["L1_ffn_b2"], np.float32).reshape(1, 64).copy()
    h["l1f_tau"] = np.asarray(inputs["L1_ffn_tau"], np.float32).reshape(1, 1).copy()
    h["l1f_wihT"] = np.ascontiguousarray(np.asarray(inputs["L1_ffn_Wih"], np.float32).T)
    h["l1f_whhT"] = np.ascontiguousarray(np.asarray(inputs["L1_ffn_Whh"], np.float32).T)
    h["l1f_bih"] = _bias_cols(inputs["L1_ffn_bih"])
    h["l1f_bhh"] = _bias_cols(inputs["L1_ffn_bhh"])
    h["l1f_wq"] = np.ascontiguousarray(np.asarray(inputs["L1_ffn_wq"], np.float32))
    h["l1f_bq"] = _bias_cols(inputs["L1_ffn_bq"])
    h["l1f_wk"] = np.ascontiguousarray(np.asarray(inputs["L1_ffn_wk"], np.float32))
    h["l1f_bk"] = _bias_cols(inputs["L1_ffn_bk"])
    h["l1f_wv"] = np.ascontiguousarray(np.asarray(inputs["L1_ffn_wv"], np.float32))
    h["l1f_bvrow"] = np.asarray(inputs["L1_ffn_bv"], np.float32).reshape(1, D).copy()
    h["l1f_lng"] = _bias_cols(inputs["L1_ffn_ln_g"])
    h["l1f_lnb"] = _bias_cols(inputs["L1_ffn_ln_b"])
    h["l1f_wvp"] = np.ascontiguousarray(np.asarray(inputs["L1_ffn_wvp"], np.float32))
    h["l1f_bvprow"] = np.asarray(inputs["L1_ffn_bvp"], np.float32).reshape(1, D).copy()
    h["l1f_wo"] = np.ascontiguousarray(np.asarray(inputs["L1_ffn_wo"], np.float32))
    h["l1f_bo"] = _bias_cols(inputs["L1_ffn_bo"])
    h["l1f_slot"] = np.ascontiguousarray(np.asarray(inputs["L1_ffn_slot_init"], np.float32))

    h["l2a_ln1g"] = _bias_cols(inputs["L2_attn_ln1_g"])
    h["l2a_ln1b"] = _bias_cols(inputs["L2_attn_ln1_b"])
    h["l2a_ln2g"] = _bias_cols(inputs["L2_attn_ln2_g"])
    h["l2a_ln2b"] = _bias_cols(inputs["L2_attn_ln2_b"])
    h["l2a_inW"] = np.ascontiguousarray(np.asarray(inputs["L2_attn_in_W"], np.float32))
    h["l2a_inb"] = _bias_cols(inputs["L2_attn_in_b"])
    dw = np.asarray(inputs["L2_attn_dw_W"], np.float32)
    h["l2a_dw"] = np.ascontiguousarray(
        np.stack([_bias_cols(dw[:, 0, k]) for k in range(3)]))
    h["l2a_dwb"] = _bias_cols(inputs["L2_attn_dw_b"])
    h["l2a_dtW"] = np.ascontiguousarray(np.asarray(inputs["L2_attn_dt_W"], np.float32))
    h["l2a_dtb"] = _bias_cols(inputs["L2_attn_dt_b"])
    h["l2a_bW"] = np.ascontiguousarray(np.asarray(inputs["L2_attn_b_W"], np.float32))
    h["l2a_bb"] = _bias_cols(inputs["L2_attn_b_b"])
    h["l2a_cW"] = np.ascontiguousarray(np.asarray(inputs["L2_attn_c_W"], np.float32))
    h["l2a_cb"] = _bias_cols(inputs["L2_attn_c_b"])
    h["l2a_alog"] = _bias_cols(inputs["L2_attn_a_log"])
    h["l2a_dskip"] = _bias_cols(inputs["L2_attn_d_skip"])
    h["l2a_outW"] = np.ascontiguousarray(np.asarray(inputs["L2_attn_out_W"], np.float32))
    h["l2a_outb"] = _bias_cols(inputs["L2_attn_out_b"])
    h["l2a_f1W"] = np.ascontiguousarray(np.asarray(inputs["L2_attn_f1_W"], np.float32))
    h["l2a_f1b"] = _bias_cols(inputs["L2_attn_f1_b"])
    h["l2a_f2W"] = np.ascontiguousarray(np.asarray(inputs["L2_attn_f2_W"], np.float32))
    h["l2a_f2b"] = _bias_cols(inputs["L2_attn_f2_b"])

    SC2 = np.float32(1.0 / np.sqrt(128.0))
    h["l2f_qT"] = np.ascontiguousarray(
        np.asarray(inputs["L2_ffn_q_w"], np.float32).T * SC2)
    h["l2f_kT"] = np.ascontiguousarray(np.asarray(inputs["L2_ffn_k_w"], np.float32).T)
    vw = np.asarray(inputs["L2_ffn_v_w"], np.float32)
    h["l2f_vwT"] = np.ascontiguousarray(vw.transpose(1, 0, 2).reshape(D, 64 * 128))
    h["l2f_ow"] = np.ascontiguousarray(
        np.asarray(inputs["L2_ffn_o_w"], np.float32).reshape(64 * 128, D))
    h["l2f_svT"] = np.ascontiguousarray(np.asarray(inputs["L2_ffn_sel_v_w"], np.float32).T)
    h["l2f_soT"] = np.ascontiguousarray(np.asarray(inputs["L2_ffn_sel_o_w"], np.float32).T)
    h["l2f_rs"] = np.asarray(inputs["L2_ffn_route_scale"], np.float32).reshape(1, 1).copy()

    h["l3a_wqT"] = np.ascontiguousarray(np.asarray(inputs["L3_attn_Wq"], np.float32).T)
    h["l3a_wkdT"] = np.ascontiguousarray(np.asarray(inputs["L3_attn_Wk_down"], np.float32).T)
    h["l3a_wvdT"] = np.ascontiguousarray(np.asarray(inputs["L3_attn_Wv_down"], np.float32).T)
    h["l3a_q2l"] = np.ascontiguousarray(
        (np.asarray(inputs["L3_attn_q_to_latent"], np.float32) *
         np.float32(0.125)).reshape(16 * 64, 64))
    h["l3a_vup"] = np.ascontiguousarray(
        np.asarray(inputs["L3_attn_v_up"], np.float32).reshape(16 * 64, 64))
    h["l3a_woT"] = np.ascontiguousarray(np.asarray(inputs["L3_attn_Wout"], np.float32).T)

    h["l3f_wgT"] = np.ascontiguousarray(np.asarray(inputs["L3_ffn_w_gate"], np.int32).T)
    h["l3f_wuT"] = np.ascontiguousarray(np.asarray(inputs["L3_ffn_w_up"], np.int32).T)
    h["l3f_wdT"] = np.ascontiguousarray(np.asarray(inputs["L3_ffn_w_down"], np.int32).T)

    cores = [{"x_f": np.ascontiguousarray(x[c].T)} for c in range(NCORES)]
    return h, cores


class Ctx:
    pass


def build_program():
    nc = bass.Bass("TRN2", target_bir_lowering=False)
    g = Ctx()
    g.nc = nc

    def inp(name, shape, dtype=F32):
        return Dram(nc, name, shape, dtype, kind="ExternalInput")

    g.x_f = inp("x_f", [D, T])
    g.l0_wqT = inp("l0_wqT", [D, D]); g.l0_wkT = inp("l0_wkT", [D, D])
    g.l0_wvT = inp("l0_wvT", [D, D]); g.l0_woT = inp("l0_woT", [D, D])
    g.l0_bq = inp("l0_bq", [128, 8]); g.l0_bk = inp("l0_bk", [128, 8])
    g.l0_bv_row = inp("l0_bv_row", [1, D]); g.l0_bo = inp("l0_bo", [128, 8])
    g.ropeCq = inp("ropeCq", [128, T]); g.ropeSq = inp("ropeSq", [128, T])
    g.ropeCk = inp("ropeCk", [128, T]); g.ropeSk = inp("ropeSk", [128, T])
    g.dmask = inp("dmask", [4, 128, 512])
    g.l0_win = inp("l0_win", [D, 4 * D]); g.l0_bin = inp("l0_bin", [128, 32])
    g.l0_wout = inp("l0_wout", [4 * D, D]); g.l0_bout = inp("l0_bout", [128, 8])
    g.ident = inp("ident", [128, 128])
    g.ones = inp("ones", [128, 1])
    g.cvec = inp("cvec", [128, 8])
    g.zeros64 = inp("zeros64", [1, 64])
    if N_BLOCKS >= 2:
        for d_ in ("fwd", "bwd"):
            setattr(g, f"l1a_wz_{d_}", inp(f"l1a_wz_{d_}", [4 * D, D]))
            setattr(g, f"l1a_wh_{d_}", inp(f"l1a_wh_{d_}", [4 * D, D]))
            setattr(g, f"l1a_bz_{d_}", inp(f"l1a_bz_{d_}", [128, 32]))
            setattr(g, f"l1a_bh_{d_}", inp(f"l1a_bh_{d_}", [128, 32]))
        g.l1a_fusionT = inp("l1a_fusionT", [2 * D, D])
        g.l1a_fb = inp("l1a_fb", [128, 8])
        g.l1a_g = inp("l1a_g", [128, 8]); g.l1a_b = inp("l1a_b", [128, 8])
        g.l1f_w1a = inp("l1f_w1a", [D, 512]); g.l1f_w1b = inp("l1f_w1b", [D, 512])
        g.l1f_b1 = inp("l1f_b1", [128, 4]); g.l1f_w2 = inp("l1f_w2", [512, 64])
        g.l1f_b2row = inp("l1f_b2row", [1, 64]); g.l1f_tau = inp("l1f_tau", [1, 1])
        g.l1f_wihT = inp("l1f_wihT", [D, 3 * D]); g.l1f_whhT = inp("l1f_whhT", [D, 3 * D])
        g.l1f_bih = inp("l1f_bih", [128, 24]); g.l1f_bhh = inp("l1f_bhh", [128, 24])
        g.l1f_wq = inp("l1f_wq", [D, 256]); g.l1f_bq = inp("l1f_bq", [128, 2])
        g.l1f_wk = inp("l1f_wk", [D, 256]); g.l1f_bk = inp("l1f_bk", [128, 2])
        g.l1f_wv = inp("l1f_wv", [D, D]); g.l1f_bvrow = inp("l1f_bvrow", [1, D])
        g.l1f_lng = inp("l1f_lng", [128, 8]); g.l1f_lnb = inp("l1f_lnb", [128, 8])
        g.l1f_wvp = inp("l1f_wvp", [D, D]); g.l1f_bvprow = inp("l1f_bvprow", [1, D])
        g.l1f_wo = inp("l1f_wo", [D, D]); g.l1f_bo = inp("l1f_bo", [128, 8])
        g.l1f_slot = inp("l1f_slot", [64, D])
    if N_BLOCKS >= 3:
        for nm, shp in [("l2a_ln1g", [128, 8]), ("l2a_ln1b", [128, 8]),
                        ("l2a_ln2g", [128, 8]), ("l2a_ln2b", [128, 8]),
                        ("l2a_inW", [D, 2 * D]), ("l2a_inb", [128, 16]),
                        ("l2a_dw", [3, 128, 8]), ("l2a_dwb", [128, 8]),
                        ("l2a_dtW", [D, D]), ("l2a_dtb", [128, 8]),
                        ("l2a_bW", [D, D]), ("l2a_bb", [128, 8]),
                        ("l2a_cW", [D, D]), ("l2a_cb", [128, 8]),
                        ("l2a_alog", [128, 8]), ("l2a_dskip", [128, 8]),
                        ("l2a_outW", [D, D]), ("l2a_outb", [128, 8]),
                        ("l2a_f1W", [D, 2 * D]), ("l2a_f1b", [128, 16]),
                        ("l2a_f2W", [2 * D, D]), ("l2a_f2b", [128, 8]),
                        ("l2f_qT", [D, D]), ("l2f_kT", [D, D]),
                        ("l2f_vwT", [D, 64 * 128]), ("l2f_ow", [64 * 128, D]),
                        ("l2f_svT", [D, 64]), ("l2f_soT", [D, 64]),
                        ("l2f_rs", [1, 1])]:
            setattr(g, nm, inp(nm, shp))
    if N_BLOCKS >= 4:
        for nm, shp, dt in [("l3a_wqT", [D, D], F32), ("l3a_wkdT", [D, 64], F32),
                            ("l3a_wvdT", [D, 64], F32), ("l3a_q2l", [16 * 64, 64], F32),
                            ("l3a_vup", [16 * 64, 64], F32), ("l3a_woT", [D, D], F32),
                            ("l3f_wgT", [D, 3 * D], I32), ("l3f_wuT", [D, 3 * D], I32),
                            ("l3f_wdT", [3 * D, D], I32)]:
            setattr(g, nm, inp(nm, shp, dt))

    g.out_f = Dram(nc, "out_f", [D, T], kind="ExternalOutput")

    g.xf = [Dram(nc, f"xf{i}", [D, T]) for i in range(2)]
    g.xnf = Dram(nc, "xnf", [D, T])
    g.sf = Dram(nc, "sf", [D, T])
    g.gf = Dram(nc, "gf", [D, T])
    g.qrope = Dram(nc, "qrope", [D, T])
    g.krope = Dram(nc, "krope", [D, T])
    g.vtok = Dram(nc, "vtok", [T, D])
    g.attnf = Dram(nc, "attnf", [D, T])
    g.xsimf = Dram(nc, "xsimf", [D, T])
    g.hidf = Dram(nc, "hidf", [4 * D, T])
    g.w_in_sim = Dram(nc, "w_in_sim", [D, 4 * D])
    g.w_out_sim = Dram(nc, "w_out_sim", [4 * D, D])
    g.rows = Dram(nc, "rows", [6, 4 * D])
    g.cc_in = Dram(nc, "cc_in", [1, 64])
    g.cc_out = Dram(nc, "cc_out", [1, 64])
    for i in range(8):
        setattr(g, f"sc{i}", Dram(nc, f"sc{i}", [D, T]))
    g.alpha_t = Dram(nc, "alpha_t", [T, 64])
    g.gv_t = Dram(nc, "gv_t", [T, 64])
    g.go_t = Dram(nc, "go_t", [T, 64])
    g.lv_t = Dram(nc, "lv_t", [T, 64])

    with nc.Block() as block:
        p = Prog(nc)
        g.p = p
        global LAST_PROG
        LAST_PROG = p
        g.WA = Arena(nc, "WA", 16384)
        g.WB = Arena(nc, "WB", 8192)
        g.XA = Arena(nc, "XA", 8192)
        g.XB = Arena(nc, "XB", 8192)
        g.YA = Arena(nc, "YA", 4096)
        g.SM = Arena(nc, "SM", 1024, gran=8)
        g.BB = Arena(nc, "BB", 4096, BF16)
        g.BI = Arena(nc, "BI", 2048, I32)
        g.ps = [Psum(nc, f"ps{i}") for i in range(8)]

        g.sb_ident = g.SM.v(0, 128)
        g.sb_ones = g.SM.v(128, 1)
        g.sb_cvec = g.SM.v(136, 8)
        p.dma("sy", g.sb_ident, g.ident.whole())
        p.dma("sy", g.sb_ones, g.ones.whole())
        p.dma("sy", g.sb_cvec, g.cvec.whole())

        emit_all(g)
        p.finalize(block)
    return nc


def dve_abs(g, eng, out, in_, tmp):
    p = g.p
    p.ts(eng, tmp, in_, -1.0, None, ALU.mult)
    p.tt(eng, out, tmp, in_, ALU.max)


def ident64(g):
    return V(g.sb_ident.ap[0:64, 0:64], g.sb_ident.keys)


def ccv(g, name, rows=128):
    v = g.SM.v(136 + CIDX[name], 1)
    return V(v.ap[0:rows, :], v.keys)


def round_even(g, eng, buf, tmp_f, tmp_f2, tmp_b):
    p = g.p
    C1 = 98304.0
    p.ts(eng, tmp_f, buf, C1, None, ALU.add)
    p.copy(eng, tmp_b, tmp_f)
    p.copy(eng, tmp_f2, tmp_b)
    p.ts(eng, tmp_f, tmp_b, -(C1 + 768.0), None, ALU.add)
    p.tt(eng, buf, buf, tmp_f, ALU.subtract)
    p.copy(eng, tmp_b, buf)
    p.tt(eng, tmp_f2, tmp_f2, tmp_b, ALU.add)
    p.ts(eng, tmp_f, tmp_b, -192.0, None, ALU.add)
    p.tt(eng, buf, buf, tmp_f, ALU.subtract)
    p.copy(eng, tmp_b, buf)
    p.tt(eng, tmp_f2, tmp_f2, tmp_b, ALU.add)
    p.ts(eng, buf, tmp_f2, -(C1 + 960.0), None, ALU.add)


def quantize_tile(g, eng, buf, scl, inv, lo, hi, tmp_f, tmp_f2, tmp_b):
    p = g.p
    p.ts(eng, buf, buf, scl, None, ALU.mult)
    round_even(g, eng, buf, tmp_f, tmp_f2, tmp_b)
    p.ts(eng, buf, buf, lo, hi, ALU.max, ALU.min)
    p.ts(eng, buf, buf, inv, None, ALU.mult)


def fold_partitions(g, buf, tmp, alu=ALU.max, width=128):
    p = g.p
    while width > 1:
        half = width // 2
        p.dma("ge", tmp[0:half, :], buf[half:width, :])
        p.tt("ve", buf[0:half, :], buf[0:half, :], tmp[0:half, :], alu)
        width = half
    return buf[0:1, :]


def bounce_rows(g, rowV, row_idx, col0, n, out128):
    p = g.p
    dst = g.rows.v(slice(row_idx, row_idx + 1), slice(col0, col0 + n),
                   key=(row_idx, col0))
    p.dma("se", dst, rowV)
    p.dma("sy", out128, V(dst.ap.broadcast_to([128, n]), dst.keys))


def emit_ln(g, xin, xout=None, gamma=None, beta=None, add=None):
    p = g.p
    if xout is None:
        xout = g.xnf
    for s in range(NSLAB):
        p.dma("sy" if s % 2 else "se", g.XA.v3(0, KD, SLAB), xin.slab3(0, KD, s))
        xs = [g.XA.v(c * SLAB, SLAB) for c in range(KD)]
        if add is not None:
            p.dma("ge", g.XB.v3(4096, KD, SLAB), add.slab3(0, KD, s))
            for c in range(KD):
                p.tt("ve" if c % 2 else "ge", xs[c], xs[c],
                     g.XB.v(4096 + c * SLAB, SLAB), ALU.add)
        ps_s = g.ps[0].v(0, SLAB, 0, 1)
        ps_q = g.ps[1].v(0, SLAB, 0, 1)
        sq = g.XB.v(0, SLAB)
        for c in range(KD):
            p.mm(ps_s, g.sb_ones, xs[c], start=(c == 0), stop=(c == KD - 1))
        for c in range(KD):
            p.tt("ve", sq, xs[c], xs[c], ALU.mult)
            p.mm(ps_q, g.sb_ones, sq, start=(c == 0), stop=(c == KD - 1))
        mean = g.XB.v(512, SLAB, 0, 1)
        msq = g.XB.v(1024, SLAB, 0, 1)
        var = g.XB.v(1536, SLAB, 0, 1)
        rstd = g.XB.v(2048, SLAB, 0, 1)
        mr = g.XB.v(2560, SLAB, 0, 1)
        p.ts("ve", mean, ps_s, 1.0 / D, None, ALU.mult)
        p.tt("ve", msq, mean, mean, ALU.mult)
        p.stt("ve", var, ps_q, 1.0 / D, msq, ALU.mult, ALU.subtract)
        sdt = g.XB.v(3584, SLAB, 0, 1)
        p.act(sdt, var, AF.Sqrt, bias=ccv(g, "eps_ln", 1))
        p.recip(rstd, sdt)
        p.tt("ve", mr, mean, rstd, ALU.mult)
        rstd_b = g.XB.v(3072, SLAB)
        mr_b = g.XB.v(3584, SLAB)
        bounce_rows(g, rstd, 0, s * SLAB, SLAB, rstd_b)
        bounce_rows(g, mr, 1, s * SLAB, SLAB, mr_b)
        for c in range(KD):
            t = g.XB.v(4096 + c * SLAB, SLAB)
            eng = "ve" if c % 2 else "ge"
            p.tt(eng, t, xs[c], rstd_b, ALU.mult)
            p.tt(eng, t, t, mr_b, ALU.subtract)
            if gamma is not None:
                p.ts(eng, t, t, V(gamma.ap[:, c:c + 1], gamma.keys),
                     V(beta.ap[:, c:c + 1], beta.keys), ALU.mult, ALU.add)
        p.dma("se", xout.slab3(0, KD, s), g.XB.v3(4096, KD, SLAB))


def emit_l0_attn(g):
    p = g.p

    def load_w(wdram, tag):
        out = []
        for c in range(KD):
            vdst = g.WA.v(c * D, D)
            p.dma("sy", vdst, wdram.v(slice(c * 128, c * 128 + 128),
                                      slice(0, D), key=(tag, c)))
            out.append(vdst)
        return out

    for which, wdram, bias_sb, Cd, Sd, outd in (
        ("q", g.l0_wqT, g.sb_bq, g.ropeCq, g.ropeSq, g.qrope),
        ("k", g.l0_wkT, g.sb_bk, g.ropeCk, g.ropeSk, g.krope),
    ):
        wch = load_w(wdram, which)
        for s in range(NSLAB):
            p.dma("sy", g.XA.v3(0, KD, SLAB), g.xnf.slab3(0, KD, s))
            xs = [g.XA.v(c * SLAB, SLAB) for c in range(KD)]
            ct = g.WB.v(4096, SLAB)
            st_ = g.WB.v(4608, SLAB)
            p.dma("sy", ct, Cd.v(slice(0, 128),
                                 slice(s * SLAB, (s + 1) * SLAB), key=s))
            p.dma("sy", st_, Sd.v(slice(0, 128),
                                  slice(s * SLAB, (s + 1) * SLAB), key=s))
            for j in range(KD):
                psb = g.ps[j % 4]
                for c in range(KD):
                    p.mm(psb.v(), V(wch[c].ap[:, j * 128:(j + 1) * 128],
                                    wch[c].keys),
                         xs[c], start=(c == 0), stop=(c == KD - 1))
                qt = g.XB.v(j * SLAB, SLAB)
                p.act(qt, psb.v(), AF.Identity,
                      bias=V(bias_sb.ap[:, j:j + 1], bias_sb.keys))
                sw = g.XB.v(4096 + (j % 4) * SLAB, SLAB)
                for b4 in range(4):
                    p.dma("ge", sw[b4 * 32:(b4 + 1) * 32, :],
                          qt[(b4 ^ 1) * 32:((b4 ^ 1) + 1) * 32, :])
                ro = g.YA.v(j * SLAB, SLAB)
                p.tt("ve", ro, qt, ct, ALU.mult)
                p.tt("ve", sw, sw, st_, ALU.mult)
                p.tt("ve", ro, ro, sw, ALU.add)
            p.dma("se", outd.slab3(0, KD, s), g.YA.v3(0, KD, SLAB))

    wv = load_w(g.l0_wvT, "v")
    bvb = g.WB.v(2048, D)
    bvsrc = g.l0_bv_row.whole()
    p.dma("sy", bvb, V(bvsrc.ap.broadcast_to([128, D]), bvsrc.keys))
    for tt_ in range(NTT):
        p.dma("sy", g.XA.v3(0, KD, 128),
              g.xnf.slab3(0, KD, tt_, ns=128, key_s=tt_ // 4))
        xs = [g.XA.v(c * 128, 128) for c in range(KD)]
        for js in range(2):
            psb = g.ps[4 + js]
            for c in range(KD):
                p.mm(psb.v(), xs[c],
                     V(wv[c].ap[:, js * SLAB:(js + 1) * SLAB], wv[c].keys),
                     start=(c == 0), stop=(c == KD - 1))
            vt = g.YA.v(2048 + js * SLAB, SLAB)
            p.tt("ve", vt, psb.v(),
                 V(bvb.ap[:, js * SLAB:(js + 1) * SLAB], bvb.keys), ALU.add)
        p.dma("se", g.vtok.v(slice(tt_ * 128, (tt_ + 1) * 128), slice(0, D),
                             key=(tt_,)), g.YA.v(2048, 2 * SLAB))

    mks = g.WB.v(0, 2048)
    dsrc = g.dmask.t.ap().transpose([1, 0, 2])
    p.dma("sy", V(mks.ap.rearrange("p (m c) -> p m c", c=512), mks.keys),
          V(dsrc, [("dmask",)]))
    for hp in range(8):
        qh2 = g.XA.v(0, T)
        kh2 = g.XA.v(4096, T)
        vh2 = g.XB.v(0, T)
        p.dma("sy", qh2, g.qrope.v(slice(hp * 128, (hp + 1) * 128), slice(0, T),
                                   key=[(hp, s) for s in range(NSLAB)]))
        p.dma("sy", kh2, g.krope.v(slice(hp * 128, (hp + 1) * 128), slice(0, T),
                                   key=[(hp, s) for s in range(NSLAB)]))
        vsrc = g.vtok.t.ap().rearrange("(a p) d -> p a d", p=128)
        vsrc = vsrc[:, :, hp * 128:(hp + 1) * 128]
        p.dma("sy", vh2, V(vsrc, [("vtok", t2) for t2 in range(NTT)]))
        for qt_ in range(NTT):
            hold2 = g.YA.v(0, 128)
            for h01 in range(2):
                acc = g.YA.v(512, 64)
                nm = [g.SM.v(512, 1), g.SM.v(520, 1)]
                lsum = g.SM.v(528, 1)
                alpha = g.SM.v(536, 1)
                rs = g.SM.v(544, 1)
                p.memset("ve", acc, 0.0)
                p.memset("ve", nm[0], 3.0e38)
                p.memset("ve", lsum, 0.0)
                nks = qt_ // 4 + 1
                for ks in range(nks):
                    pss = g.ps[0]
                    p.mm(pss.v(),
                         V(qh2.ap[h01 * 64:(h01 + 1) * 64,
                                  qt_ * 128:(qt_ + 1) * 128], qh2.keys),
                         V(kh2.ap[h01 * 64:(h01 + 1) * 64,
                                  ks * SLAB:(ks + 1) * SLAB], kh2.keys),
                         start=True, stop=True)
                    if ks == qt_ // 4:
                        p.tt("ve", pss.v(), pss.v(),
                             V(mks.ap[:, (qt_ % 4) * SLAB:
                                      ((qt_ % 4) + 1) * SLAB], mks.keys),
                             ALU.add)
                    nmo, nmn = nm[ks % 2], nm[(ks + 1) % 2]
                    nmt = g.SM.v(552, 1)
                    p.red(nmt, pss.v(), ALU.max, negate=True)
                    p.tt("ve", nmn, nmo, nmt, ALU.min)
                    p.act(alpha, nmo, AF.Exp, bias=nmn, scale=-1.0)
                    p_sb = g.XB.v(4096, SLAB)
                    p.act(p_sb, pss.v(), AF.Exp, bias=nmn, accum=rs)
                    p.stt("ve", lsum, lsum, alpha, rs, ALU.mult, ALU.add)
                    pst = g.ps[1]
                    for i4 in range(4):
                        p.transp(pst.v(i4 * 128, 128),
                                 V(p_sb.ap[:, i4 * 128:(i4 + 1) * 128],
                                   p_sb.keys), g.sb_ident)
                    pT = g.XB.v(4608, SLAB)
                    p.copy("se", pT, pst.v())
                    psv = g.ps[2]
                    for i4 in range(4):
                        kk = ks * 4 + i4
                        p.mm(psv.v(0, 64),
                             V(pT.ap[:, i4 * 128:(i4 + 1) * 128], pT.keys),
                             V(vh2.ap[:, kk * 128 + h01 * 64:
                                      kk * 128 + h01 * 64 + 64], vh2.keys),
                             start=(i4 == 0), stop=(i4 == 3))
                    p.stt("ve", acc, acc, alpha, psv.v(0, 64),
                          ALU.mult, ALU.add)
                linv = g.SM.v(560, 1)
                p.recip(linv, lsum)
                p.ts("ve", V(hold2.ap[:, h01 * 64:(h01 + 1) * 64], hold2.keys),
                     acc, linv, None, ALU.mult)
            ps3 = g.ps[3]
            p.transp(ps3.v(0, 128), hold2, g.sb_ident)
            outt = g.YA.v(256, 128)
            p.copy("se", outt, ps3.v(0, 128))
            p.dma("se", g.attnf.v(slice(hp * 128, (hp + 1) * 128),
                                  slice(qt_ * 128, (qt_ + 1) * 128),
                                  key=(hp, ("qt", qt_ // 4))), outt)

    wo = load_w(g.l0_woT, "o")
    for s in range(NSLAB):
        p.dma("sy", g.XA.v3(0, KD, SLAB),
              g.attnf.slab3(0, KD, s, key_s=("qt", s)))
        xs = [g.XA.v(c * SLAB, SLAB) for c in range(KD)]
        for j in range(KD):
            psb = g.ps[j % 4]
            for c in range(KD):
                p.mm(psb.v(), V(wo[c].ap[:, j * 128:(j + 1) * 128], wo[c].keys),
                     xs[c], start=(c == 0), stop=(c == KD - 1))
            ot = g.XB.v(j * SLAB, SLAB)
            p.act(ot, psb.v(), AF.Identity,
                  bias=V(g.sb_bo.ap[:, j:j + 1], g.sb_bo.keys))
        p.dma("se", g.sf.slab3(0, KD, s), g.XB.v3(0, KD, SLAB))


def emit_l0_ffn(g):
    p = g.p
    DFF = 4 * D

    def global_amax_allreduce(amx128, idx):
        col = g.SM.v(576 + idx * 64, 1)
        p.red(col, amx128, ALU.max)
        tmp = g.XB.v(3584, 1)
        one_ = fold_partitions(g, col, tmp)
        p.ts("ve", one_, one_, 1e-8, None, ALU.max)
        p.dma("ge", g.cc_in.whole(), g.zeros64.whole())
        ccw = g.cc_in.whole()
        p.dma("ge", V(ccw.ap[0:1, 0:1], ccw.keys), one_)
        p.collective_max(g.cc_in.whole(), g.cc_out.whole())
        cco = g.cc_out.whole()
        gmax = g.SM.v(584 + idx * 64, 1, 0, 1)
        p.dma("ge", gmax, V(cco.ap[0:1, 0:1], cco.keys))
        scl1 = g.SM.v(592 + idx * 64, 1, 0, 1)
        inv1 = g.SM.v(600 + idx * 64, 1, 0, 1)
        p.recip(scl1, gmax)
        p.ts("ve", scl1, scl1, 32767.0, None, ALU.mult)
        p.recip(inv1, scl1)
        sclb = g.SM.v(608 + idx * 64, 1)
        invb = g.SM.v(616 + idx * 64, 1)
        bounce_rows(g, scl1, 2, idx * 8, 1, sclb)
        bounce_rows(g, inv1, 2, idx * 8 + 4, 1, invb)
        return sclb, invb

    amx = g.XB.v(3072, SLAB)
    for s in range(NSLAB):
        p.dma("sy", g.XA.v3(0, KD, SLAB), g.xnf.slab3(0, KD, s))
        for c in range(KD):
            xc = g.XA.v(c * SLAB, SLAB)
            ab = g.XB.v(2560, SLAB)
            dve_abs(g, "ve", ab, xc, g.XB.v(2048, SLAB))
            if s == 0 and c == 0:
                p.copy("ve", amx, ab)
            else:
                p.tt("ve", amx, amx, ab, ALU.max)
    sclb, invb = global_amax_allreduce(amx, 0)
    for s in range(NSLAB):
        p.dma("sy", g.XA.v3(0, KD, SLAB), g.xnf.slab3(0, KD, s))
        for c in range(KD):
            xc = g.XA.v(c * SLAB, SLAB)
            eng = "ve" if c % 2 else "ge"
            tmp_f = g.XB.v((c % 4) * SLAB, SLAB)
            tmp_f2 = g.XB.v(2048 + (c % 4) * SLAB, SLAB)
            tmp_b = g.BB.v((c % 4) * SLAB, SLAB)
            quantize_tile(g, eng, xc, sclb, invb, -32768.0, 32767.0,
                          tmp_f, tmp_f2, tmp_b)
        p.dma("se", g.xsimf.slab3(0, KD, s), g.XA.v3(0, KD, SLAB))

    wamx = g.XB.v(0, DFF)
    for c in range(KD):
        wc = g.XA.v((c % 2) * DFF, DFF)
        p.dma("sy", wc, g.l0_win.v(slice(c * 128, c * 128 + 128),
                                   slice(0, DFF), key=("w", c)))
        ab = g.XB.v(DFF, DFF)
        dve_abs(g, "ve", ab, wc, g.YA.v(0, DFF))
        if c == 0:
            p.copy("ve", wamx, ab)
        else:
            p.tt("ve", wamx, wamx, ab, ALU.max)
    wtmp = g.YA.v(0, DFF)
    arow = fold_partitions(g, wamx, wtmp)
    p.ts("ve", arow, arow, 1e-8, None, ALU.max)
    srow = g.XB.v(DFF, DFF, 0, 1)
    p.recip(srow, arow)
    p.ts("ve", srow, srow, 127.0, None, ALU.mult)
    irow = g.YA.v(0, DFF, 0, 1)
    p.recip(irow, srow)
    dstS = g.rows.v(slice(3, 4), slice(0, DFF), key=(3, 0))
    p.dma("se", dstS, srow)
    dstI = g.rows.v(slice(4, 5), slice(0, DFF), key=(4, 0))
    p.dma("se", dstI, irow)
    sclwb = g.XB.v(0, DFF)
    invwb = g.XB.v(DFF, DFF)
    p.dma("sy", sclwb, V(dstS.ap.broadcast_to([128, DFF]), dstS.keys))
    p.dma("sy", invwb, V(dstI.ap.broadcast_to([128, DFF]), dstI.keys))
    for c in range(KD):
        wc = g.XA.v((c % 2) * DFF, DFF)
        p.dma("sy", wc, g.l0_win.v(slice(c * 128, c * 128 + 128),
                                   slice(0, DFF), key=("w", c)))
        eng = "ve" if c % 2 else "ge"
        p.tt(eng, wc, wc, sclwb, ALU.mult)
        tmp_f = g.YA.v(0, DFF)
        tmp_f2 = g.WB.v(4096, DFF)
        tmp_b = g.BB.v(0, DFF)
        round_even(g, eng, wc, tmp_f, tmp_f2, tmp_b)
        p.ts(eng, wc, wc, -128.0, 127.0, ALU.max, ALU.min)
        p.tt(eng, wc, wc, invwb, ALU.mult)
        p.dma("se", g.w_in_sim.v(slice(c * 128, c * 128 + 128),
                                 slice(0, DFF), key=("w", c)), wc)

    amx2 = g.WB.v(0, SLAB)
    first = [True]
    for fg in range(4):
        wg = []
        for c in range(KD):
            wgc = g.WA.v(c * D, D)
            p.dma("sy", wgc, g.w_in_sim.v(slice(c * 128, c * 128 + 128),
                                          slice(fg * D, (fg + 1) * D),
                                          key=("w", c)))
            wg.append(wgc)
        for s in range(NSLAB):
            p.dma("sy", g.XA.v3(0, KD, SLAB), g.xsimf.slab3(0, KD, s))
            xs = [g.XA.v(c * SLAB, SLAB) for c in range(KD)]
            for jf in range(KD):
                ffc = fg * 8 + jf
                psb = g.ps[jf]
                for c in range(KD):
                    p.mm(psb.v(), V(wg[c].ap[:, jf * 128:(jf + 1) * 128],
                                    wg[c].keys),
                         xs[c], start=(c == 0), stop=(c == KD - 1))
                hc = g.XB.v(jf * SLAB, SLAB)
                p.act(hc, psb.v(), AF.Gelu,
                      bias=V(g.sb_bin.ap[:, ffc:ffc + 1], g.sb_bin.keys))
                ab2 = g.WB.v(512, SLAB)
                dve_abs(g, "ve", ab2, hc, g.WB.v(1024, SLAB))
                if first[0]:
                    p.copy("ve", amx2, ab2)
                    first[0] = False
                else:
                    p.tt("ve", amx2, amx2, ab2, ALU.max)
            p.dma("se", g.hidf.slab3(fg * KD, KD, s), g.XB.v3(0, KD, SLAB))
    scl2, inv2 = global_amax_allreduce(amx2, 1)

    for c in range(4 * KD):
        wc = g.XA.v((c % 4) * D, D)
        p.dma("sy", wc, g.l0_wout.v(slice(c * 128, c * 128 + 128),
                                    slice(0, D), key=("w", c)))
        am = g.SM.v(704, 1)
        p.red(am, wc, ALU.max, absval=True)
        p.ts("ve", am, am, 1e-8, None, ALU.max)
        sc = g.SM.v(712, 1)
        iv = g.SM.v(720, 1)
        p.recip(sc, am)
        p.ts("ve", sc, sc, 127.0, None, ALU.mult)
        p.recip(iv, sc)
        eng = "ve" if c % 2 else "ge"
        p.ts(eng, wc, wc, sc, None, ALU.mult)
        tmp_f = g.XB.v((c % 4) * D, D)
        tmp_f2 = g.XB.v(4096 + (c % 4) * D, D)
        tmp_b = g.BB.v((c % 2) * D, D)
        round_even(g, eng, wc, tmp_f, tmp_f2, tmp_b)
        p.ts(eng, wc, wc, -128.0, 127.0, ALU.max, ALU.min)
        p.ts(eng, wc, wc, iv, None, ALU.mult)
        p.dma("se", g.w_out_sim.v(slice(c * 128, c * 128 + 128),
                                  slice(0, D), key=("w", c)), wc)

    for s in range(NSLAB):
        p.dma("sy", g.WA.v3(0, 4 * KD, SLAB), g.hidf.slab3(0, 4 * KD, s))
        hs = []
        for k in range(4 * KD):
            hc = g.WA.v(k * SLAB, SLAB)
            eng = "ve" if k % 2 else "ge"
            tmp_f = g.XB.v((k % 8) * SLAB, SLAB)
            tmp_f2 = g.XB.v(4096 + (k % 8) * SLAB, SLAB)
            tmp_b = g.BB.v((k % 8) * SLAB, SLAB)
            quantize_tile(g, eng, hc, scl2, inv2, -32768.0, 32767.0,
                          tmp_f, tmp_f2, tmp_b)
            hs.append(hc)
        for k in range(4 * KD):
            wk = g.XA.v((k % 2) * D, D)
            p.dma("sy", wk, g.w_out_sim.v(slice(k * 128, k * 128 + 128),
                                          slice(0, D), key=("w", k)))
            for j in range(KD):
                p.mm(g.ps[j].v(), V(wk.ap[:, j * 128:(j + 1) * 128], wk.keys),
                     hs[k], start=(k == 0), stop=(k == 4 * KD - 1))
        for j in range(KD):
            ot = g.XB.v(4096 + j * SLAB, SLAB)
            p.act(ot, g.ps[j].v(), AF.Silu,
                  bias=V(g.sb_bout.ap[:, j:j + 1], g.sb_bout.keys))
        p.dma("se", g.gf.slab3(0, KD, s), g.XB.v3(4096, KD, SLAB))



def emit_proj(g, xin, wd, wrow0, nkc, njc, outd, evac, in_keys=None):
    p = g.p
    wch = []
    ncols = njc * 128
    for c in range(nkc):
        vdst = g.WA.v(c * ncols, ncols)
        p.dma("sy", vdst, wd.v(slice(wrow0 + c * 128, wrow0 + (c + 1) * 128),
                               slice(0, ncols), key=("w", wrow0, c)))
        wch.append(vdst)
    for s in range(NSLAB):
        half = (s % 2) * (nkc * SLAB) if nkc <= 8 else 0
        p.dma("sy" if s % 2 else "se", g.XA.v3(half, nkc, SLAB),
              xin.slab3(0, nkc, s))
        xs = [g.XA.v(half + c * SLAB, SLAB) for c in range(nkc)]
        for j in range(njc):
            psb = g.ps[j % 4]
            for c in range(nkc):
                p.mm(psb.v(), V(wch[c].ap[:, j * 128:(j + 1) * 128], wch[c].keys),
                     xs[c], start=(c == 0), stop=(c == nkc - 1))
            evac(j, psb.v(), g.XB.v(j * SLAB, SLAB))
        p.dma("se" if s % 2 else "ge", outd.slab3(0, njc, s), g.XB.v3(0, njc, SLAB))


def ev_act(g, func, bias_sb, scale=1.0):
    def f(j, ps, dst):
        g.p.act(dst, ps, func, bias=V(bias_sb.ap[:, j:j + 1], bias_sb.keys),
                scale=scale)
    return f


def ev_softplus(g, bias_sb):
    def f(j, ps, dst):
        p = g.p
        pre = g.YA.v(0, SLAB)
        ab = g.YA.v(512, SLAB)
        lg = g.YA.v(1024, SLAB)
        p.act(pre, ps, AF.Identity, bias=V(bias_sb.ap[:, j:j + 1], bias_sb.keys))
        dve_abs(g, "ve", ab, pre, g.YA.v(1536, SLAB))
        p.act(ab, ab, AF.Exp, scale=-1.0)
        p.act(lg, ab, AF.Ln, bias=ccv(g, "one"))
        p.ts("ve", dst, pre, 0.0, None, ALU.max)
        p.tt("ve", dst, dst, lg, ALU.add)
    return f


def emit_reverse(g, srcd, dstd):
    p = g.p
    for s in range(NSLAB):
        p.dma("sy", g.XA.v3(0, KD, SLAB), srcd.slab3(0, KD, s))
        for c in range(KD):
            xc = g.XA.v(c * SLAB, SLAB)
            t = g.XB.v(c * SLAB, SLAB)
            p.copy("ve" if c % 2 else "ge", t, V(xc.ap[:, ::-1], xc.keys))
        p.dma("se", dstd.slab3(0, KD, NSLAB - 1 - s), g.XB.v3(0, KD, SLAB))


def emit_l1_attn(g):
    p = g.p
    g.sb_l1g = g.SM.v(256, 8)
    g.sb_l1b = g.SM.v(264, 8)
    p.dma("sy", g.sb_l1g, g.l1a_g.whole())
    p.dma("sy", g.sb_l1b, g.l1a_b.whole())
    emit_reverse(g, g.xnf, g.sc0)
    hfin = {}
    for d_ in ("fwd", "bwd"):
        bz_sb = g.SM.v(272, 32)
        bh_sb = g.SM.v(304, 32)
        p.dma("sy", bz_sb, getattr(g, f"l1a_bz_{d_}").whole())
        p.dma("sy", bh_sb, getattr(g, f"l1a_bh_{d_}").whole())
        hcur = g.xnf if d_ == "fwd" else g.sc0
        ping = [g.sc1, g.sc2] if d_ == "fwd" else [g.sc3, g.sc4]
        wz = getattr(g, f"l1a_wz_{d_}")
        wh = getattr(g, f"l1a_wh_{d_}")
        for i in range(4):
            emit_proj(g, hcur, wz, i * D, KD, KD, g.sc5,
                      ev_act(g, AF.Sigmoid, V(bz_sb.ap[:, i * 8:(i + 1) * 8],
                                              bz_sb.keys)))
            emit_proj(g, hcur, wh, i * D, KD, KD, g.sc6,
                      ev_act(g, AF.Tanh, V(bh_sb.ap[:, i * 8:(i + 1) * 8],
                                           bh_sb.keys)))
            for c in range(KD):
                zc = g.XA.v(0, T)
                hc = g.XA.v(4096, T)
                p.dma("sy", zc, g.sc5.v(slice(c * 128, (c + 1) * 128), slice(0, T),
                                        key=[(c, s) for s in range(NSLAB)]))
                p.dma("se", hc, g.sc6.v(slice(c * 128, (c + 1) * 128), slice(0, T),
                                        key=[(c, s) for s in range(NSLAB)]))
                p.tt("ve", hc, zc, hc, ALU.mult)
                p.act(zc, zc, AF.Identity, bias=ccv(g, "one"), scale=-1.0)
                p.ts("ve", zc, zc, 1e-8, 1.0 - 1e-8, ALU.max, ALU.min)
                sc = g.XB.v(0, T)
                p.scan(sc, zc, hc, 0.0)
                p.dma("ge", g.sc7.v(slice(c * 128, (c + 1) * 128), slice(0, T),
                                    key=[(c, s) for s in range(NSLAB)]), sc)
            nxt = ping[i % 2]
            emit_ln(g, g.sc7, nxt, gamma=g.sb_l1g, beta=g.sb_l1b,
                    add=(hcur if i > 0 else None))
            hcur = nxt
        hfin[d_] = hcur
    emit_reverse(g, hfin["bwd"], g.sc0)
    g.sb_fb = g.SM.v(272, 8)
    p.dma("sy", g.sb_fb, g.l1a_fb.whole())
    wch = []
    for c in range(16):
        vdst = g.WA.v(c * D, D)
        p.dma("sy", vdst, g.l1a_fusionT.v(slice(c * 128, (c + 1) * 128),
                                          slice(0, D), key=("w", c)))
        wch.append(vdst)
    for s in range(NSLAB):
        p.dma("sy", g.XA.v3(0, KD, SLAB), hfin["fwd"].slab3(0, KD, s))
        p.dma("se", g.XA.v3(4096, KD, SLAB), g.sc0.slab3(0, KD, s))
        xs = [g.XA.v(c * SLAB, SLAB) for c in range(16)]
        for j in range(KD):
            psb = g.ps[j % 4]
            for c in range(16):
                p.mm(psb.v(), V(wch[c].ap[:, j * 128:(j + 1) * 128], wch[c].keys),
                     xs[c], start=(c == 0), stop=(c == 15))
            p.act(g.XB.v(j * SLAB, SLAB), psb.v(), AF.Identity,
                  bias=V(g.sb_fb.ap[:, j:j + 1], g.sb_fb.keys))
        p.dma("ge", g.sf.slab3(0, KD, s), g.XB.v3(0, KD, SLAB))


def emit_l1_ffn(g):
    p = g.p
    sl = g.XB.v(0, D, 0, 64)
    p.dma("sy", sl, g.l1f_slot.whole())
    tmp = g.XB.v(1024, D)
    msum = fold_partitions(g, g.XB.v(0, D), tmp, alu=ALU.add, width=64)
    p.ts("ve", msum, msum, 1.0 / 64.0, None, ALU.mult)
    dstm = g.rows.v(slice(5, 6), slice(0, D), key=(5, 0))
    p.dma("se", dstm, msum)
    mcol = g.SM.v(336, 8)
    p.dma("sy", mcol, V(dstm.ap.rearrange("o (c p) -> (o p) c", p=128), dstm.keys),
          allow_slow_non_contiguous=True)
    hb = g.SM.v(344, 4)
    b1sb = g.SM.v(352, 4)
    p.dma("sy", b1sb, g.l1f_b1.whole())
    for jc in range(4):
        psb = g.ps[4]
        for c in range(KD):
            wv_ = g.XB.v(2048 + c * 128, 128)
            p.dma("sy", wv_, g.l1f_w1b.v(slice(c * 128, (c + 1) * 128),
                                         slice(jc * 128, (jc + 1) * 128),
                                         key=("w", c, jc)))
            p.mm(psb.v(0, 1), wv_, V(mcol.ap[:, c:c + 1], mcol.keys),
                 start=(c == 0), stop=(c == KD - 1))
        p.act(V(hb.ap[:, jc:jc + 1], hb.keys), psb.v(0, 1), AF.Identity,
              bias=V(b1sb.ap[:, jc:jc + 1], b1sb.keys))
    w1a_sb = []
    for c in range(KD):
        vdst = g.XB.v(4096 + c * 512, 512)
        p.dma("sy", vdst, g.l1f_w1a.v(slice(c * 128, (c + 1) * 128),
                                      slice(0, 512), key=("w", c)))
        w1a_sb.append(vdst)
    for s in range(NSLAB):
        p.dma("sy", g.XA.v3(0, KD, SLAB), g.xnf.slab3(0, KD, s))
        xs = [g.XA.v(c * SLAB, SLAB) for c in range(KD)]
        for j in range(4):
            psb = g.ps[j % 2]
            for c in range(KD):
                p.mm(psb.v(), V(w1a_sb[c].ap[:, j * 128:(j + 1) * 128],
                                w1a_sb[c].keys),
                     xs[c], start=(c == 0), stop=(c == KD - 1))
            p.act(g.WA.v(j * T + s * SLAB, SLAB), psb.v(), AF.Gelu,
                  bias=V(hb.ap[:, j:j + 1], hb.keys))
    w2sb = g.SM.v(360, 256, 0, 128)
    w2sb = g.SM.v(360, 256)
    for c in range(4):
        p.dma("sy", V(w2sb.ap[:, c * 64:(c + 1) * 64], w2sb.keys),
              g.l1f_w2.v(slice(c * 128, (c + 1) * 128), slice(0, 64),
                         key=("w", c)))
    b2b = g.SM.v(616, 64)
    b2w = g.l1f_b2row.whole()
    p.dma("sy", b2b, V(b2w.ap.broadcast_to([128, 64]), b2w.keys))
    tausb = g.SM.v(680, 1, 0, 1)
    p.dma("sy", tausb, g.l1f_tau.whole())
    dve_abs(g, "ve", tausb, tausb, g.SM.v(760, 1, 0, 1))
    p.ts("ve", tausb, tausb, 0.1, None, ALU.add)
    rtau1 = g.SM.v(688, 1, 0, 1)
    p.recip(rtau1, tausb)
    rtaub = g.SM.v(696, 1)
    bounce_rows(g, rtau1, 5, 2048, 1, rtaub)
    ps_si = [g.ps[5], g.ps[6]]
    ps_dn = g.ps[7]
    for tt_ in range(NTT):
        p.dma("sy", g.XA.v3(0, KD, 128),
              g.xnf.slab3(0, KD, tt_, ns=128, key_s=tt_ // 4))
        pst = g.ps[0]
        xnt = g.XB.v(0, 1024)
        for c in range(KD):
            p.transp(pst.v(c % 4 * 128, 128), g.XA.v(c * 128, 128), g.sb_ident)
            p.copy("se", V(xnt.ap[:, c * 128:(c + 1) * 128], xnt.keys),
                   pst.v(c % 4 * 128, 128))
        ps_lg = g.ps[1]
        for c in range(4):
            hch = g.WA.v(c * T + tt_ * 128, 128)
            p.mm(ps_lg.v(0, 64), hch,
                 V(w2sb.ap[:, c * 64:(c + 1) * 64], w2sb.keys),
                 start=(c == 0), stop=(c == 3))
        lg = g.XB.v(1024, 64)
        p.tt("ve", lg, ps_lg.v(0, 64), b2b, ALU.add)
        p.ts("ve", lg, lg, rtaub, None, ALU.mult)
        mx8 = g.SM.v(704, 8)
        p.max8(mx8, lg)
        nm1 = g.SM.v(712, 1)
        p.ts("ve", nm1, V(mx8.ap[:, 0:1], mx8.keys), -1.0, None, ALU.mult)
        sel = g.XB.v(1088, 64)
        p.ts("ve", sel, lg, V(mx8.ap[:, 7:8], mx8.keys), None, ALU.is_ge)
        ex = g.XB.v(1152, 64)
        p.act(ex, lg, AF.Exp, bias=nm1)
        p.tt("ve", ex, ex, sel, ALU.mult)
        ssum = g.SM.v(720, 1)
        p.red(ssum, ex, ALU.add)
        rsum = g.SM.v(728, 1)
        p.recip(rsum, ssum)
        alph = g.XB.v(1216, 64)
        p.ts("ve", alph, ex, rsum, None, ALU.mult)
        p.dma("se", g.alpha_t.v(slice(tt_ * 128, (tt_ + 1) * 128), slice(0, 64),
                                key=(tt_,)), alph)
        ps_at = g.ps[2]
        p.transp(ps_at.v(0, 128, 0, 64), alph, g.sb_ident)
        p.copy("se", V(g.YA.v(tt_ * 128, 128).ap[0:64, :], g.YA.v(tt_ * 128, 128).keys),
               ps_at.v(0, 128, 0, 64))
        for half in range(2):
            p.mm(ps_si[half].v(0, 512, 0, 64), alph,
                 V(xnt.ap[:, half * 512:(half + 1) * 512], xnt.keys),
                 start=(tt_ == 0), stop=(tt_ == NTT - 1))
        p.mm(ps_dn.v(0, 1, 0, 64), alph, g.sb_ones,
             start=(tt_ == 0), stop=(tt_ == NTT - 1))
    ss = g.XB.v(0, 1024, 0, 64)
    p.copy("se", V(ss.ap[:, 0:512], ss.keys), ps_si[0].v(0, 512, 0, 64))
    p.copy("se", V(ss.ap[:, 512:1024], ss.keys), ps_si[1].v(0, 512, 0, 64))
    dcol = g.SM.v(736, 1, 0, 64)
    p.act(dcol, ps_dn.v(0, 1, 0, 64), AF.Identity, bias=1e-8 if False else 0.0)
    p.ts("ve", dcol, dcol, 1e-8, None, ALU.add)
    rd = g.SM.v(744, 1, 0, 64)
    p.recip(rd, dcol)
    p.ts("ve", ss, ss, rd, None, ALU.mult)
    siT = g.XB.v(1024, 512)
    slT = g.XB.v(1536, 512)
    slin = g.XB.v(2048, 1024, 0, 64)
    p.dma("sy", slin, g.l1f_slot.whole())
    pst = g.ps[0]
    for c in range(KD):
        p.transp(pst.v(0, 64), V(ss.ap[:, c * 128:(c + 1) * 128], ss.keys),
                 ident64(g))
        p.copy("se", V(siT.ap[:, c * 64:(c + 1) * 64], siT.keys),
               pst.v(0, 64))
        p.transp(pst.v(64, 64),
                 V(slin.ap[:, c * 128:(c + 1) * 128], slin.keys), ident64(g))
        p.copy("se", V(slT.ap[:, c * 64:(c + 1) * 64], slT.keys),
               pst.v(64, 64))
    bih_sb = g.SM.v(752, 24)
    bhh_sb = g.SM.v(776, 24)
    p.dma("sy", bih_sb, g.l1f_bih.whole())
    p.dma("sy", bhh_sb, g.l1f_bhh.whole())
    gi = g.XB.v(2048 + 1024, 24 * 64)
    gh = g.XB.v(2048 + 1024 + 24 * 64, 24 * 64)
    for src_sb, wdr, bias, dst in ((siT, g.l1f_wihT, bih_sb, gi),
                                   (slT, g.l1f_whhT, bhh_sb, gh)):
        for m in range(24):
            psb = g.ps[1 + (m % 2)]
            for c in range(KD):
                wv_ = g.XA.v((c % 4) * 128, 128)
                p.dma("sy", wv_, wdr.v(slice(c * 128, (c + 1) * 128),
                                       slice(m * 128, (m + 1) * 128),
                                       key=("w", c, m)))
                p.mm(psb.v(0, 64), wv_,
                     V(src_sb.ap[:, c * 64:(c + 1) * 64], src_sb.keys),
                     start=(c == 0), stop=(c == KD - 1))
            p.act(V(dst.ap[:, m * 64:(m + 1) * 64], dst.keys), psb.v(0, 64),
                  AF.Identity, bias=V(bias.ap[:, m:m + 1], bias.keys))
    Snew = g.YA.v(0, 512)
    Snew = g.XB.v(2048 + 1024 + 48 * 64, 512)
    for c in range(KD):
        ir = V(gi.ap[:, c * 64:(c + 1) * 64], gi.keys)
        iz = V(gi.ap[:, (8 + c) * 64:(9 + c) * 64], gi.keys)
        inn = V(gi.ap[:, (16 + c) * 64:(17 + c) * 64], gi.keys)
        hr = V(gh.ap[:, c * 64:(c + 1) * 64], gh.keys)
        hz = V(gh.ap[:, (8 + c) * 64:(9 + c) * 64], gh.keys)
        hn = V(gh.ap[:, (16 + c) * 64:(17 + c) * 64], gh.keys)
        r_ = g.SM.v(800, 64)
        z_ = g.SM.v(864, 64)
        n_ = g.SM.v(928, 64)
        p.tt("ve", r_, ir, hr, ALU.add)
        p.act(r_, r_, AF.Sigmoid)
        p.tt("ve", z_, iz, hz, ALU.add)
        p.act(z_, z_, AF.Sigmoid)
        p.tt("ve", n_, r_, hn, ALU.mult)
        p.tt("ve", n_, n_, inn, ALU.add)
        p.act(n_, n_, AF.Tanh)
        sc_ = V(slT.ap[:, c * 64:(c + 1) * 64], slT.keys)
        zs = V(Snew.ap[:, c * 64:(c + 1) * 64], Snew.keys)
        p.tt("ve", r_, z_, sc_, ALU.mult)
        p.act(z_, z_, AF.Identity, bias=ccv(g, "one"), scale=-1.0)
        p.tt("ve", z_, z_, n_, ALU.mult)
        p.tt("ve", zs, z_, r_, ALU.add)
    bq_sb = g.SM.v(992, 2); bk_sb = g.SM.v(1000, 2)
    p.dma("sy", bq_sb, g.l1f_bq.whole())
    p.dma("sy", bk_sb, g.l1f_bk.whole())
    qf = g.SM.v(800, 128)
    kf = g.SM.v(0, 0) if False else None
    kf = g.XB.v(2048 + 1024 + 48 * 64 + 512, 128)
    for wdr, bsb, dst in ((g.l1f_wq, bq_sb, qf), (g.l1f_wk, bk_sb, kf)):
        for m in range(2):
            psb = g.ps[3]
            for c in range(KD):
                wv_ = g.XA.v(4096 + (c % 4) * 128, 128)
                p.dma("sy", wv_, wdr.v(slice(c * 128, (c + 1) * 128),
                                       slice(m * 128, (m + 1) * 128),
                                       key=("w", c, m)))
                p.mm(psb.v(0, 64), wv_,
                     V(Snew.ap[:, c * 64:(c + 1) * 64], Snew.keys),
                     start=(c == 0), stop=(c == KD - 1))
            p.act(V(dst.ap[:, m * 64:(m + 1) * 64], dst.keys), psb.v(0, 64),
                  AF.Identity, bias=V(bsb.ap[:, m:m + 1], bsb.keys))
    ps_sc = g.ps[4]
    for m in range(2):
        p.mm(ps_sc.v(0, 64, 0, 64), V(qf.ap[:, m * 64:(m + 1) * 64], qf.keys),
             V(kf.ap[:, m * 64:(m + 1) * 64], kf.keys),
             start=(m == 0), stop=(m == 1))
    att = g.SM.v(864, 64, 0, 64)
    p.act(att, ps_sc.v(0, 64, 0, 64), AF.Identity, scale=0.0625)
    nmx = g.SM.v(928, 1, 0, 64)
    p.red(nmx, att, ALU.max, negate=True)
    rs_ = g.SM.v(936, 1, 0, 64)
    p.act(att, att, AF.Exp, bias=nmx, accum=rs_)
    rr = g.SM.v(944, 1, 0, 64)
    p.recip(rr, rs_)
    p.ts("ve", att, att, rr, None, ALU.mult)
    ps_at2 = g.ps[5]
    p.transp(ps_at2.v(0, 64, 0, 64), att, ident64(g))
    attT = g.SM.v(864, 64, 0, 64)
    p.copy("se", attT, ps_at2.v(0, 64, 0, 64))
    vsm = g.XB.v(0, 1024, 0, 64)
    bvw = g.l1f_bvrow.whole()
    bvb64 = g.XB.v(1024, 1024, 0, 64)
    p.dma("sy", bvb64, V(bvw.ap.broadcast_to([64, D]), bvw.keys))
    for half in range(2):
        psb = g.ps[6]
        for c in range(KD):
            wv_ = g.XA.v((c % 4) * 512, 512)
            p.dma("sy", wv_, g.l1f_wv.v(slice(c * 128, (c + 1) * 128),
                                        slice(half * 512, (half + 1) * 512),
                                        key=("w", c, half)))
            p.mm(psb.v(0, 512, 0, 64),
                 V(Snew.ap[:, c * 64:(c + 1) * 64], Snew.keys), wv_,
                 start=(c == 0), stop=(c == KD - 1))
        p.tt("ve", V(vsm.ap[:, half * 512:(half + 1) * 512], vsm.keys),
             psb.v(0, 512, 0, 64),
             V(bvb64.ap[:, half * 512:(half + 1) * 512], bvb64.keys), ALU.add)
    yv = g.XB.v(2048, 512)
    for c in range(KD):
        psb = g.ps[7]
        p.mm(psb.v(0, 64), V(vsm.ap[:, c * 128:(c + 1) * 128], vsm.keys),
             attT, start=True, stop=True)
        p.tt("ve", V(yv.ap[:, c * 64:(c + 1) * 64], yv.keys), psb.v(0, 64),
             V(Snew.ap[:, c * 64:(c + 1) * 64], Snew.keys), ALU.add)
    ps_s = g.ps[0]; ps_q = g.ps[1]
    sqt = g.SM.v(864, 64)
    for c in range(KD):
        yc = V(yv.ap[:, c * 64:(c + 1) * 64], yv.keys)
        p.mm(ps_s.v(0, 64, 0, 1), g.sb_ones, yc, start=(c == 0), stop=(c == KD - 1))
        p.tt("ve", sqt, yc, yc, ALU.mult)
        p.mm(ps_q.v(0, 64, 0, 1), g.sb_ones, sqt, start=(c == 0), stop=(c == KD - 1))
    mrow = g.SM.v(928, 64, 0, 1)
    vrow = g.SM.v(864, 64, 0, 1)
    p.ts("ve", mrow, ps_s.v(0, 64, 0, 1), 1.0 / D, None, ALU.mult)
    p.ts("ve", vrow, ps_q.v(0, 64, 0, 1), 1.0 / D, None, ALU.mult)
    m2row = g.SM.v(736, 64, 0, 1)
    p.tt("ve", m2row, mrow, mrow, ALU.mult)
    p.tt("ve", vrow, vrow, m2row, ALU.subtract)
    p.act(vrow, vrow, AF.Sqrt, bias=ccv(g, "eps_ln", 1))
    p.recip(vrow, vrow)
    mr2 = g.SM.v(744, 64, 0, 1)
    p.tt("ve", mr2, mrow, vrow, ALU.mult)
    rb = g.XB.v(1024, 64)
    mb = g.XB.v(1088, 64)
    bounce_rows(g, vrow, 5, 1024, 64, rb)
    bounce_rows(g, mr2, 5, 1152, 64, mb)
    lng_sb = g.SM.v(736, 8); lnb_sb = g.SM.v(752, 8)
    p.dma("sy", lng_sb, g.l1f_lng.whole())
    p.dma("sy", lnb_sb, g.l1f_lnb.whole())
    rbf = V(rb.ap[0:128, :], rb.keys)
    mbf = V(mb.ap[0:128, :], mb.keys)
    for c in range(KD):
        yc = V(yv.ap[:, c * 64:(c + 1) * 64], yv.keys)
        p.tt("ve", yc, yc, rbf, ALU.mult)
        p.tt("ve", yc, yc, mbf, ALU.subtract)
        p.ts("ve", yc, yc, V(lng_sb.ap[:, c:c + 1], lng_sb.keys),
             V(lnb_sb.ap[:, c:c + 1], lnb_sb.keys), ALU.mult, ALU.add)
    if DEV_DUMP:
        p.dma("se", g.sc7.v(slice(0, 64), slice(0, 1024), key=("d", 0)), ss)
        p.dma("se", g.sc7.v(slice(128, 256), slice(0, 512), key=("d", 1)), siT)
        p.dma("se", g.sc7.v(slice(256, 384), slice(0, 1024), key=("d", 2)),
              V(gi.ap[:, 0:1024], gi.keys))
        p.dma("se", g.sc7.v(slice(384, 512), slice(0, 1024), key=("d", 3)),
              V(gh.ap[:, 0:1024], gh.keys))
        p.dma("se", g.sc7.v(slice(512, 640), slice(0, 512), key=("d", 4)), Snew)
        p.dma("se", g.sc7.v(slice(640, 704), slice(0, 1024), key=("d", 5)), vsm)
        p.dma("se", g.sc7.v(slice(704, 768), slice(0, 64), key=("d", 6)), att)
        p.dma("se", g.sc7.v(slice(768, 896), slice(0, 512), key=("d", 7)), yv)
    svm = g.XB.v(2560, 1024, 0, 64)
    bvpw = g.l1f_bvprow.whole()
    bvpb = g.XB.v(3584, 1024, 0, 64)
    p.dma("sy", bvpb, V(bvpw.ap.broadcast_to([64, D]), bvpw.keys))
    for half in range(2):
        psb = g.ps[2]
        for c in range(KD):
            wv_ = g.XA.v((c % 4) * 512, 512)
            p.dma("sy", wv_, g.l1f_wvp.v(slice(c * 128, (c + 1) * 128),
                                         slice(half * 512, (half + 1) * 512),
                                         key=("w", c, half)))
            p.mm(psb.v(0, 512, 0, 64),
                 V(yv.ap[:, c * 64:(c + 1) * 64], yv.keys), wv_,
                 start=(c == 0), stop=(c == KD - 1))
        p.tt("ve", V(svm.ap[:, half * 512:(half + 1) * 512], svm.keys),
             psb.v(0, 512, 0, 64),
             V(bvpb.ap[:, half * 512:(half + 1) * 512], bvpb.keys), ALU.add)
    if DEV_DUMP:
        p.dma("se", g.sc7.v(slice(896, 960), slice(0, 1024), key=("d", 8)), svm)
        p.dma("se", g.sc7.v(slice(960, 1024), slice(0, T), key=("d", 9)),
              V(g.YA.v(0, T).ap[0:64, :], g.YA.v(0, T).keys))
    g.sb_l1fbo = g.SM.v(736, 8)
    p.dma("sy", g.sb_l1fbo, g.l1f_bo.whole())
    wo_sb = []
    for c in range(KD):
        vdst = g.WA.v(c * D, D)
        p.dma("sy", vdst, g.l1f_wo.v(slice(c * 128, (c + 1) * 128),
                                     slice(0, D), key=("w", c)))
        wo_sb.append(vdst)
    for s in range(NSLAB):
        of = []
        for c in range(KD):
            psb = g.ps[3 + (c % 2)]
            p.mm(psb.v(),
                 V(svm.ap[:, c * 128:(c + 1) * 128], svm.keys),
                 V(g.YA.v(s * SLAB, SLAB).ap[0:64, :], g.YA.v(s * SLAB, SLAB).keys),
                 start=True, stop=True)
            oc = g.XA.v(c * SLAB, SLAB)
            p.copy("se", oc, psb.v())
            of.append(oc)
        for j in range(KD):
            psb = g.ps[5 + (j % 2)]
            for c in range(KD):
                p.mm(psb.v(), V(wo_sb[c].ap[:, j * 128:(j + 1) * 128],
                                wo_sb[c].keys),
                     of[c], start=(c == 0), stop=(c == KD - 1))
            p.act(g.XB.v(4096 + j * SLAB, SLAB), psb.v(), AF.Silu,
                  bias=V(g.sb_l1fbo.ap[:, j:j + 1], g.sb_l1fbo.keys))
        p.dma("se", g.gf.slab3(0, KD, s), g.XB.v3(4096, KD, SLAB))



def ev_copy(g):
    def f(j, ps, dst):
        g.p.copy("se", dst, ps)
    return f


def emit_addpass(g, a, b, outd, eng_q="se"):
    p = g.p
    for s in range(NSLAB):
        p.dma("sy", g.XA.v3(0, KD, SLAB), a.slab3(0, KD, s))
        p.dma("ge", g.XA.v3(4096, KD, SLAB), b.slab3(0, KD, s))
        for c in range(KD):
            p.tt("ve" if c % 2 else "ge", g.XA.v(c * SLAB, SLAB),
                 g.XA.v(c * SLAB, SLAB), g.XA.v(4096 + c * SLAB, SLAB), ALU.add)
        p.dma(eng_q, outd.slab3(0, KD, s), g.XA.v3(0, KD, SLAB))


def emit_l2_attn(g):
    p = g.p
    ln1g = g.SM.v(256, 8); ln1b = g.SM.v(264, 8)
    ln2g = g.SM.v(272, 8); ln2b = g.SM.v(280, 8)
    inb = g.SM.v(288, 16); dwb = g.SM.v(304, 8)
    dtb = g.SM.v(312, 8); bb_ = g.SM.v(320, 8); cb = g.SM.v(328, 8)
    alog = g.SM.v(336, 8); dskip = g.SM.v(344, 8); outb = g.SM.v(352, 8)
    f1b = g.SM.v(360, 16); f2b = g.SM.v(376, 8)
    dw0 = g.SM.v(384, 8); dw1 = g.SM.v(392, 8); dw2 = g.SM.v(400, 8)
    for sb, dr in ((ln1g, g.l2a_ln1g), (ln1b, g.l2a_ln1b), (ln2g, g.l2a_ln2g),
                   (ln2b, g.l2a_ln2b), (inb, g.l2a_inb), (dwb, g.l2a_dwb),
                   (dtb, g.l2a_dtb), (bb_, g.l2a_bb), (cb, g.l2a_cb),
                   (alog, g.l2a_alog), (dskip, g.l2a_dskip), (outb, g.l2a_outb),
                   (f1b, g.l2a_f1b), (f2b, g.l2a_f2b)):
        p.dma("sy", sb, dr.whole())
    for k3, sb in enumerate((dw0, dw1, dw2)):
        p.dma("sy", sb, g.l2a_dw.v(slice(k3, k3 + 1), key=k3))
    acol = g.SM.v(408, 8)
    p.act(acol, alog, AF.Exp)
    p.ts("ve", acol, acol, -1.0, None, ALU.mult)

    emit_ln(g, g.xnf, g.sc0, gamma=ln1g, beta=ln1b)
    emit_proj(g, g.sc0, g.l2a_inW, 0, KD, 16, g.hidf,
              ev_act(g, AF.Identity, inb))
    for c in range(KD):
        xp = g.XA.v((c % 2) * T, T)
        p.dma("sy", xp, g.hidf.v(slice(c * 128, (c + 1) * 128), slice(0, T),
                                 key=[(c, s) for s in range(NSLAB)]))
        mxc = g.XB.v((c % 2) * T, T)
        eng = "ve" if c % 2 else "ge"
        p.ts(eng, mxc, xp, V(dw1.ap[:, c:c + 1], dw1.keys),
             V(dwb.ap[:, c:c + 1], dwb.keys), ALU.mult, ALU.add)
        p.stt("ve", V(mxc.ap[:, 1:T], mxc.keys), V(xp.ap[:, 0:T - 1], xp.keys),
              V(dw0.ap[:, c:c + 1], dw0.keys), V(mxc.ap[:, 1:T], mxc.keys),
              ALU.mult, ALU.add)
        p.stt("ve", V(mxc.ap[:, 0:T - 1], mxc.keys), V(xp.ap[:, 1:T], xp.keys),
              V(dw2.ap[:, c:c + 1], dw2.keys), V(mxc.ap[:, 0:T - 1], mxc.keys),
              ALU.mult, ALU.add)
        p.dma("se", g.sc1.v(slice(c * 128, (c + 1) * 128), slice(0, T),
                            key=[(c, s) for s in range(NSLAB)]), mxc)
    emit_proj(g, g.sc1, g.l2a_dtW, 0, KD, KD, g.sc2, ev_softplus(g, dtb))
    emit_proj(g, g.sc1, g.l2a_bW, 0, KD, KD, g.sc3, ev_act(g, AF.Tanh, bb_))
    emit_proj(g, g.sc1, g.l2a_cW, 0, KD, KD, g.sc4, ev_act(g, AF.Tanh, cb))
    for c in range(KD):
        ks = [(c, s) for s in range(NSLAB)]
        dt = g.XA.v(0, T); bt = g.XA.v(4096, T)
        ct = g.XB.v(0, T); mxc = g.XB.v(4096, T); gtc = g.YA.v(0, T)
        p.dma("sy", dt, g.sc2.v(slice(c * 128, (c + 1) * 128), slice(0, T), key=ks))
        p.dma("se", bt, g.sc3.v(slice(c * 128, (c + 1) * 128), slice(0, T), key=ks))
        p.dma("ge", ct, g.sc4.v(slice(c * 128, (c + 1) * 128), slice(0, T), key=ks))
        p.dma("sy", mxc, g.sc1.v(slice(c * 128, (c + 1) * 128), slice(0, T), key=ks))
        p.dma("se", gtc, g.hidf.v(slice((8 + c) * 128, (9 + c) * 128), slice(0, T),
                                  key=[(8 + c, s) for s in range(NSLAB)]))
        decay = g.WB.v(0, T)
        p.act(decay, dt, AF.Exp, scale=V(acol.ap[:, c:c + 1], acol.keys))
        p.tt("ve", bt, bt, mxc, ALU.mult)
        p.tt("ve", bt, dt, bt, ALU.mult)
        st_ = g.WB.v(4096, T)
        p.scan(st_, decay, bt, 0.0)
        p.tt("ve", ct, ct, st_, ALU.mult)
        p.stt("ve", ct, mxc, V(dskip.ap[:, c:c + 1], dskip.keys), ct,
              ALU.mult, ALU.add)
        p.act(gtc, gtc, AF.Sigmoid)
        p.tt("ve", ct, ct, gtc, ALU.mult)
        p.dma("ge", g.sc5.v(slice(c * 128, (c + 1) * 128), slice(0, T), key=ks), ct)
    emit_proj(g, g.sc5, g.l2a_outW, 0, KD, KD, g.sc7,
              ev_act(g, AF.Identity, outb))
    emit_addpass(g, g.sc7, g.xnf, g.sc6)
    emit_ln(g, g.sc6, g.sc0, gamma=ln2g, beta=ln2b)
    emit_proj(g, g.sc0, g.l2a_f1W, 0, KD, 16, g.hidf,
              ev_act(g, AF.Gelu, f1b))
    emit_proj(g, g.hidf, g.l2a_f2W, 0, 16, KD, g.sc7,
              ev_act(g, AF.Identity, f2b))
    emit_addpass(g, g.sc7, g.sc6, g.sf)


def emit_l2_ffn(g):
    p = g.p
    rs1 = g.SM.v(416, 1, 0, 1)
    p.dma("sy", rs1, g.l2f_rs.whole())
    rsb = g.SM.v(424, 1)
    bounce_rows(g, rs1, 5, 2052, 1, rsb)
    svsb = []
    sosb = []
    for c in range(KD):
        a1 = g.WB.v(c * 64, 64)
        p.dma("sy", a1, g.l2f_svT.v(slice(c * 128, (c + 1) * 128), slice(0, 64),
                                    key=("wv", c)))
        svsb.append(a1)
        a2 = g.WB.v(512 + c * 64, 64)
        p.dma("sy", a2, g.l2f_soT.v(slice(c * 128, (c + 1) * 128), slice(0, 64),
                                    key=("wo", c)))
        sosb.append(a2)
    for tt_ in range(NTT):
        p.dma("sy", g.XA.v3(0, KD, 128),
              g.xnf.slab3(0, KD, tt_, ns=128, key_s=tt_ // 4))
        psv = g.ps[0]; pso = g.ps[1]
        for c in range(KD):
            xc = g.XA.v(c * 128, 128)
            p.mm(psv.v(0, 64), xc, svsb[c], start=(c == 0), stop=(c == KD - 1))
            p.mm(pso.v(0, 64), xc, sosb[c], start=(c == 0), stop=(c == KD - 1))
        for pgate, outd in ((psv, g.gv_t), (pso, g.go_t)):
            pr = g.XB.v(4096, 64)
            p.act(pr, pgate.v(0, 64), AF.Sigmoid)
            gt = g.XB.v(4160, 64)
            for hh in range(8):
                prh = V(pr.ap[:, hh * 8:(hh + 1) * 8], pr.keys)
                mx8 = g.SM.v(520, 8)
                p.max8(mx8, prh)
                ssum = g.SM.v(528, 1)
                p.tt("ve", ssum, V(mx8.ap[:, 0:1], mx8.keys),
                     V(mx8.ap[:, 1:2], mx8.keys), ALU.add)
                p.ts("ve", ssum, ssum, 1e-9, None, ALU.max)
                rr = g.SM.v(536, 1)
                p.recip(rr, ssum)
                p.ts("ve", rr, rr, rsb, None, ALU.mult)
                gh = V(gt.ap[:, hh * 8:(hh + 1) * 8], gt.keys)
                p.ts("ve", gh, prh, V(mx8.ap[:, 1:2], mx8.keys), None, ALU.is_ge)
                p.tt("ve", gh, gh, prh, ALU.mult)
                p.ts("ve", gh, gh, rr, None, ALU.mult)
            p.dma("se", outd.v(slice(tt_ * 128, (tt_ + 1) * 128), slice(0, 64),
                               key=(tt_,)), gt)
    emit_proj(g, g.xnf, g.l2f_qT, 0, KD, KD, g.sc0, ev_copy(g))
    emit_proj(g, g.xnf, g.l2f_kT, 0, KD, KD, g.sc1, ev_copy(g))
    for hh in range(8):
        wch = []
        for c in range(KD):
            vdst = g.WA.v(c * 1024, 1024)
            p.dma("sy", vdst, g.l2f_vwT.v(slice(c * 128, (c + 1) * 128),
                                          slice(hh * 1024, (hh + 1) * 1024),
                                          key=("w", c, hh)))
            wch.append(vdst)
        for tt_ in range(NTT):
            p.dma("sy", g.XA.v3(0, KD, 128),
                  g.xnf.slab3(0, KD, tt_, ns=128, key_s=tt_ // 4))
            gvt = g.XB.v(4096, 64)
            p.dma("se", gvt, g.gv_t.v(slice(tt_ * 128, (tt_ + 1) * 128),
                                      slice(0, 64), key=(tt_,)))
            vacc = g.XB.v(4224, 128)
            for eh in range(2):
                psb = g.ps[2 + eh]
                for c in range(KD):
                    p.mm(psb.v(), g.XA.v(c * 128, 128),
                         V(wch[c].ap[:, eh * 512:(eh + 1) * 512], wch[c].keys),
                         start=(c == 0), stop=(c == KD - 1))
                for e4 in range(4):
                    e = eh * 4 + e4
                    gcol = V(gvt.ap[:, hh * 8 + e:hh * 8 + e + 1], gvt.keys)
                    if e == 0:
                        p.ts("ve", vacc, psb.v(e4 * 128, 128), gcol, None, ALU.mult)
                    else:
                        p.stt("ve", vacc, psb.v(e4 * 128, 128), gcol, vacc,
                              ALU.mult, ALU.add)
            p.dma("ge", g.vtok.v(slice(tt_ * 128, (tt_ + 1) * 128),
                                 slice(hh * 128, (hh + 1) * 128),
                                 key=(tt_, hh)), vacc)
    for hh in range(8):
        qh = g.XA.v(0, T)
        kh = g.XA.v(4096, T)
        vh = g.XB.v(0, T)
        p.dma("sy", qh, g.sc0.v(slice(hh * 128, (hh + 1) * 128), slice(0, T),
                                key=[(hh, s) for s in range(NSLAB)]))
        p.dma("sy", kh, g.sc1.v(slice(hh * 128, (hh + 1) * 128), slice(0, T),
                                key=[(hh, s) for s in range(NSLAB)]))
        vsrc = g.vtok.t.ap().rearrange("(a p) d -> p a d", p=128)
        vsrc = vsrc[:, :, hh * 128:(hh + 1) * 128]
        p.dma("sy", vh, V(vsrc, [("vtok", t2, hh) for t2 in range(NTT)]))
        owch = []
        for e in range(8):
            vdst = g.WB.v(e * 512, 512)
            p.dma("sy", vdst, g.l2f_ow.v(
                slice(hh * 1024 + e * 128, hh * 1024 + (e + 1) * 128),
                slice(0, 512), key=("ow", hh, e, 0)))
            owch.append(vdst)
        owch2 = []
        for e in range(8):
            vdst = g.WB.v(4096 + e * 512, 512)
            p.dma("sy", vdst, g.l2f_ow.v(
                slice(hh * 1024 + e * 128, hh * 1024 + (e + 1) * 128),
                slice(512, 1024), key=("ow", hh, e, 1)))
            owch2.append(vdst)
        for qt_ in range(NTT):
            acc = g.YA.v(0, 128)
            nm = [g.SM.v(512, 1), g.SM.v(520, 1)]
            lsum = g.SM.v(528, 1)
            alpha = g.SM.v(536, 1)
            rs_ = g.SM.v(544, 1)
            p.memset("ve", acc, 0.0)
            p.memset("ve", nm[0], 3.0e38)
            p.memset("ve", lsum, 0.0)
            for ks in range(NSLAB):
                pss = g.ps[0]
                p.mm(pss.v(), V(qh.ap[:, qt_ * 128:(qt_ + 1) * 128], qh.keys),
                     V(kh.ap[:, ks * SLAB:(ks + 1) * SLAB], kh.keys),
                     start=True, stop=True)
                nmo, nmn = nm[ks % 2], nm[(ks + 1) % 2]
                nmt = g.SM.v(552, 1)
                p.red(nmt, pss.v(), ALU.max, negate=True)
                p.tt("ve", nmn, nmo, nmt, ALU.min)
                p.act(alpha, nmo, AF.Exp, bias=nmn, scale=-1.0)
                p_sb = g.XB.v(4096, SLAB)
                p.act(p_sb, pss.v(), AF.Exp, bias=nmn, accum=rs_)
                p.stt("ve", lsum, lsum, alpha, rs_, ALU.mult, ALU.add)
                pst = g.ps[1]
                for i4 in range(4):
                    p.transp(pst.v(i4 * 128, 128),
                             V(p_sb.ap[:, i4 * 128:(i4 + 1) * 128], p_sb.keys),
                             g.sb_ident)
                pT = g.XB.v(4608, SLAB)
                p.copy("se", pT, pst.v())
                psv2 = g.ps[2]
                for i4 in range(4):
                    kk = ks * 4 + i4
                    p.mm(psv2.v(0, 128),
                         V(pT.ap[:, i4 * 128:(i4 + 1) * 128], pT.keys),
                         V(vh.ap[:, kk * 128:(kk + 1) * 128], vh.keys),
                         start=(i4 == 0), stop=(i4 == 3))
                p.stt("ve", acc, acc, alpha, psv2.v(0, 128), ALU.mult, ALU.add)
            linv = g.SM.v(560, 1)
            p.recip(linv, lsum)
            p.ts("ve", acc, acc, linv, None, ALU.mult)
            got = g.XB.v(5120, 64)
            p.dma("se", got, g.go_t.v(slice(qt_ * 128, (qt_ + 1) * 128),
                                      slice(0, 64), key=(qt_,)))
            gtile = g.XB.v(5248, 1024)
            wgtT8 = g.XB.v(6272, 1024)
            for e in range(8):
                wgt = g.YA.v(128, 128)
                p.ts("ve", wgt, acc,
                     V(got.ap[:, hh * 8 + e:hh * 8 + e + 1], got.keys),
                     None, ALU.mult)
                pst2 = g.ps[3]
                p.transp(pst2.v(0, 128), wgt, g.sb_ident)
                p.copy("se", V(wgtT8.ap[:, e * 128:(e + 1) * 128], wgtT8.keys),
                       pst2.v(0, 128))
            for e in range(8):
                wgtT = V(wgtT8.ap[:, e * 128:(e + 1) * 128], wgtT8.keys)
                for dc in range(8):
                    ow = owch[e] if dc < 4 else owch2[e]
                    sl = dc % 4
                    p.mm(g.ps[dc].v(0, 128),
                         V(ow.ap[:, sl * 128:(sl + 1) * 128], ow.keys),
                         wgtT, start=(e == 0), stop=(e == 7))
            for dc in range(8):
                p.copy("se", V(gtile.ap[:, dc * 128:(dc + 1) * 128], gtile.keys),
                       g.ps[dc].v(0, 128))
            dst = g.sc2.slab3(0, KD, qt_, ns=128, key_s=qt_ // 4)
            if hh == 0:
                p.dma("ge", dst, g.XB.v3(5248, KD, 128))
            else:
                p.dma("ge", dst, g.XB.v3(5248, KD, 128), accum_op=ALU.add)
    for s in range(NSLAB):
        p.dma("sy", g.XA.v3(0, KD, SLAB), g.sc2.slab3(0, KD, s))
        for c in range(KD):
            p.act(g.XA.v(c * SLAB, SLAB), g.XA.v(c * SLAB, SLAB), AF.Silu)
        p.dma("se", g.gf.slab3(0, KD, s), g.XA.v3(0, KD, SLAB))


def emit_l3_attn(g):
    p = g.p
    emit_proj(g, g.xnf, g.l3a_wqT, 0, KD, KD, g.sc0, ev_copy(g))
    wkd = g.XB.v(0, 512)
    wvd = g.XB.v(512, 512)
    for c in range(KD):
        p.dma("sy", V(wkd.ap[:, c * 64:(c + 1) * 64], wkd.keys),
              g.l3a_wkdT.v(slice(c * 128, (c + 1) * 128), slice(0, 64),
                           key=("w", c)))
        p.dma("sy", V(wvd.ap[:, c * 64:(c + 1) * 64], wvd.keys),
              g.l3a_wvdT.v(slice(c * 128, (c + 1) * 128), slice(0, 64),
                           key=("w", c)))
    for s in range(NSLAB):
        p.dma("sy", g.XA.v3(0, KD, SLAB), g.xnf.slab3(0, KD, s))
        psb = g.ps[0]
        for c in range(KD):
            p.mm(psb.v(0, SLAB, 0, 64),
                 V(wkd.ap[:, c * 64:(c + 1) * 64], wkd.keys),
                 g.XA.v(c * SLAB, SLAB), start=(c == 0), stop=(c == KD - 1))
        lkc = g.XB.v(1024, SLAB, 0, 64)
        p.copy("se", lkc, psb.v(0, SLAB, 0, 64))
        p.dma("se", g.sc1.v(slice(0, 64), slice(s * SLAB, (s + 1) * SLAB),
                            key=("lk", s)), lkc)
    for tt_ in range(NTT):
        p.dma("sy", g.XA.v3(0, KD, 128),
              g.xnf.slab3(0, KD, tt_, ns=128, key_s=tt_ // 4))
        psb = g.ps[1]
        for c in range(KD):
            p.mm(psb.v(0, 64), g.XA.v(c * 128, 128),
                 V(wvd.ap[:, c * 64:(c + 1) * 64], wvd.keys),
                 start=(c == 0), stop=(c == KD - 1))
        lvc = g.XB.v(1536, 64)
        p.copy("se", lvc, psb.v(0, 64))
        p.dma("se", g.lv_t.v(slice(tt_ * 128, (tt_ + 1) * 128), slice(0, 64),
                             key=(tt_,)), lvc)
    lk_sb = g.XB.v(2048, T, 0, 64)
    p.dma("sy", lk_sb, g.sc1.v(slice(0, 64), slice(0, T),
                               key=[("lk", s) for s in range(NSLAB)]))
    lv_sb = g.XB.v(0, T // 2)
    lvsrc = g.lv_t.t.ap().rearrange("(a p) l -> p a l", p=128)
    p.dma("sy", lv_sb, V(lvsrc, [("lv_t", t2) for t2 in range(NTT)]))
    mks = g.WB.v(0, 2048)
    dsrc = g.dmask.t.ap().transpose([1, 0, 2])
    p.dma("sy", V(mks.ap.rearrange("p (m c) -> p m c", c=512), mks.keys),
          V(dsrc, [("dmask",)]))
    for hh in range(16):
        qh = g.XA.v(0, T, 0, 64)
        p.dma("sy", qh, g.sc0.v(slice(hh * 64, (hh + 1) * 64), slice(0, T),
                                key=[(hh // 2, s) for s in range(NSLAB)]))
        q2l = g.SM.v(256, 64, 0, 64)
        p.dma("sy", q2l, g.l3a_q2l.v(slice(hh * 64, (hh + 1) * 64),
                                     slice(0, 64), key=("w", hh)))
        vup = g.SM.v(320, 64, 0, 64)
        p.dma("sy", vup, g.l3a_vup.v(slice(hh * 64, (hh + 1) * 64),
                                     slice(0, 64), key=("w", hh)))
        qlat = g.YA.v(0, T, 0, 64)
        for s in range(NSLAB):
            psb = g.ps[0]
            p.mm(psb.v(0, SLAB, 0, 64), q2l,
                 V(qh.ap[:, s * SLAB:(s + 1) * SLAB], qh.keys),
                 start=True, stop=True)
            p.copy("se", V(qlat.ap[:, s * SLAB:(s + 1) * SLAB], qlat.keys),
                   psb.v(0, SLAB, 0, 64))
        for qt_ in range(NTT):
            acc = g.XA.v(4096, 64)
            nm = [g.SM.v(512, 1), g.SM.v(520, 1)]
            lsum = g.SM.v(528, 1)
            alpha = g.SM.v(536, 1)
            rs_ = g.SM.v(544, 1)
            p.memset("ve", acc, 0.0)
            p.memset("ve", nm[0], 3.0e38)
            p.memset("ve", lsum, 0.0)
            nks = qt_ // 4 + 1
            for ks in range(nks):
                pss = g.ps[1]
                p.mm(pss.v(),
                     V(qlat.ap[:, qt_ * 128:(qt_ + 1) * 128], qlat.keys),
                     V(lk_sb.ap[:, ks * SLAB:(ks + 1) * SLAB], lk_sb.keys),
                     start=True, stop=True)
                if ks == qt_ // 4:
                    p.tt("ve", pss.v(), pss.v(),
                         V(mks.ap[:, (qt_ % 4) * SLAB:((qt_ % 4) + 1) * SLAB],
                           mks.keys), ALU.add)
                nmo, nmn = nm[ks % 2], nm[(ks + 1) % 2]
                nmt = g.SM.v(552, 1)
                p.red(nmt, pss.v(), ALU.max, negate=True)
                p.tt("ve", nmn, nmo, nmt, ALU.min)
                p.act(alpha, nmo, AF.Exp, bias=nmn, scale=-1.0)
                p_sb = g.XA.v(4608, SLAB)
                p.act(p_sb, pss.v(), AF.Exp, bias=nmn, accum=rs_)
                p.stt("ve", lsum, lsum, alpha, rs_, ALU.mult, ALU.add)
                pst = g.ps[2]
                for i4 in range(4):
                    p.transp(pst.v(i4 * 128, 128),
                             V(p_sb.ap[:, i4 * 128:(i4 + 1) * 128], p_sb.keys),
                             g.sb_ident)
                pT = g.XA.v(5120, SLAB)
                p.copy("se", pT, pst.v())
                psv = g.ps[3]
                for i4 in range(4):
                    kk = ks * 4 + i4
                    p.mm(psv.v(0, 64),
                         V(pT.ap[:, i4 * 128:(i4 + 1) * 128], pT.keys),
                         V(lv_sb.ap[:, kk * 64:(kk + 1) * 64], lv_sb.keys),
                         start=(i4 == 0), stop=(i4 == 3))
                p.stt("ve", acc, acc, alpha, psv.v(0, 64), ALU.mult, ALU.add)
            linv = g.SM.v(560, 1)
            p.recip(linv, lsum)
            p.ts("ve", acc, acc, linv, None, ALU.mult)
            pst2 = g.ps[4]
            p.transp(pst2.v(0, 128, 0, 64), acc, g.sb_ident)
            ctxT = g.XA.v(5632, 128, 0, 64)
            p.copy("se", ctxT, pst2.v(0, 128, 0, 64))
            psh = g.ps[5]
            p.mm(psh.v(0, 128, 0, 64), vup, ctxT, start=True, stop=True)
            hp_ = g.XA.v(5760, 128, 0, 64)
            p.copy("se", hp_, psh.v(0, 128, 0, 64))
            p.dma("se", g.sc2.v(slice(hh * 64, (hh + 1) * 64),
                                slice(qt_ * 128, (qt_ + 1) * 128),
                                key=(hh // 2, qt_ // 4)), hp_)
    emit_proj(g, g.sc2, g.l3a_woT, 0, KD, KD, g.sf, ev_copy(g))


def emit_l3_ffn(g):
    p = g.p
    DH3 = 3 * D
    for jg in range(3):
        for w_i, wdr in ((0, g.l3f_wgT), (1, g.l3f_wuT)):
            for c in range(KD):
                iw = g.BI.v((c % 2) * 1024, 1024)
                p.dma("sy", iw, wdr.v(slice(c * 128, (c + 1) * 128),
                                      slice(jg * D, (jg + 1) * D),
                                      key=("w", c, jg)))
                p.act(g.WA.v(w_i * 8192 + c * 1024, 1024), iw, AF.Copy,
                      bias=0.0, scale=float(1.0 / 127.0))
        for s in range(NSLAB):
            p.dma("sy", g.XA.v3(0, KD, SLAB), g.xnf.slab3(0, KD, s))
            xs = [g.XA.v(c * SLAB, SLAB) for c in range(KD)]
            for j in range(KD):
                psg = g.ps[j % 2]
                psu = g.ps[2 + j % 2]
                for c in range(KD):
                    p.mm(psg.v(), V(g.WA.v(c * 1024, 1024).ap[:, j * 128:(j + 1) * 128],
                                    g.WA.v(c * 1024, 1024).keys),
                         xs[c], start=(c == 0), stop=(c == KD - 1))
                for c in range(KD):
                    p.mm(psu.v(), V(g.WA.v(8192 + c * 1024, 1024).ap[:, j * 128:(j + 1) * 128],
                                    g.WA.v(8192 + c * 1024, 1024).keys),
                         xs[c], start=(c == 0), stop=(c == KD - 1))
                hcell = g.XB.v(j * SLAB, SLAB)
                p.act(hcell, psg.v(), AF.Silu)
                p.tt("ve", hcell, hcell, psu.v(), ALU.mult)
            p.dma("se", g.hidf.slab3(jg * KD, KD, s), g.XB.v3(0, KD, SLAB))
    for s in range(NSLAB):
        p.dma("sy", g.WA.v3(0, 3 * KD, SLAB), g.hidf.slab3(0, 3 * KD, s))
        hs = [g.WA.v(k * SLAB, SLAB) for k in range(3 * KD)]
        for k in range(3 * KD):
            iw = g.BI.v((k % 2) * 1024, 1024)
            p.dma("sy", iw, g.l3f_wdT.v(slice(k * 128, (k + 1) * 128),
                                        slice(0, D), key=("w", k, s)))
            wk = g.XA.v((k % 2) * D, D)
            p.act(wk, iw, AF.Copy, bias=0.0, scale=float(1.0 / 127.0))
            for j in range(KD):
                p.mm(g.ps[j].v(), V(wk.ap[:, j * 128:(j + 1) * 128], wk.keys),
                     hs[k], start=(k == 0), stop=(k == 3 * KD - 1))
        for j in range(KD):
            ot = g.XB.v(4096 + j * SLAB, SLAB)
            p.act(ot, g.ps[j].v(), AF.Silu)
        p.dma("se", g.gf.slab3(0, KD, s), g.XB.v3(4096, KD, SLAB))


def emit_combine(g, xin, xout):
    p = g.p
    for s in range(NSLAB):
        p.dma("sy", g.XA.v3(0, KD, SLAB), xin.slab3(0, KD, s))
        p.dma("sy", g.XA.v3(4096, KD, SLAB), g.sf.slab3(0, KD, s))
        p.dma("ge", g.XB.v3(0, KD, SLAB), g.gf.slab3(0, KD, s))
        for c in range(KD):
            xc = g.XA.v(c * SLAB, SLAB)
            sc_ = g.XA.v(4096 + c * SLAB, SLAB)
            gc_ = g.XB.v(c * SLAB, SLAB)
            eng = "ve" if c % 2 else "ge"
            p.tt(eng, sc_, sc_, gc_, ALU.mult)
            p.tt(eng, sc_, sc_, xc, ALU.add)
        p.dma("se", xout.slab3(0, KD, s), g.XA.v3(4096, KD, SLAB))


def emit_all(g):
    p = g.p
    g.sb_bq = g.SM.v(256, 8)
    g.sb_bk = g.SM.v(264, 8)
    g.sb_bo = g.SM.v(272, 8)
    g.sb_bin = g.SM.v(280, 32)
    g.sb_bout = g.SM.v(312, 8)
    p.dma("sy", g.sb_bq, g.l0_bq.whole())
    p.dma("sy", g.sb_bk, g.l0_bk.whole())
    p.dma("sy", g.sb_bo, g.l0_bo.whole())
    p.dma("sy", g.sb_bin, g.l0_bin.whole())
    p.dma("sy", g.sb_bout, g.l0_bout.whole())

    xcur = g.x_f
    emitters = [(emit_l0_attn, emit_l0_ffn), (emit_l1_attn, emit_l1_ffn),
                (emit_l2_attn, emit_l2_ffn), (emit_l3_attn, emit_l3_ffn)]
    for blk in range(N_BLOCKS):
        emit_ln(g, xcur)
        emitters[blk][0](g)
        emitters[blk][1](g)
        last = (blk == N_BLOCKS - 1)
        tgt = g.out_f if (last and DEV_TAP is None) else g.xf[blk % 2]
        emit_combine(g, xcur, tgt)
        xcur = tgt

    if DEV_TAP == "alpha":
        for tt_ in range(NTT):
            av = g.XA.v(0, 64)
            p.dma("sy", av, g.alpha_t.v(slice(tt_ * 128, (tt_ + 1) * 128),
                                        slice(0, 64), key=(tt_,)))
            pst = g.ps[0]
            p.transp(pst.v(0, 128, 0, 64), av, g.sb_ident)
            ot = g.XB.v(0, 128, 0, 64)
            p.copy("se", ot, pst.v(0, 128, 0, 64))
            p.dma("se", g.out_f.v(slice(0, 64),
                                  slice(tt_ * 128, (tt_ + 1) * 128),
                                  key=("a", tt_)), ot)
        return
    if DEV_TAP is not None:
        srcd = (g.xf[int(DEV_TAP[2])] if DEV_TAP.startswith("xf")
                else getattr(g, DEV_TAP))
        for s in range(NSLAB):
            ks = ("qt", s) if DEV_TAP == "attnf" else None
            p.dma("sy", g.XA.v3(0, KD, SLAB), srcd.slab3(0, KD, s, key_s=ks))
            p.dma("se", g.out_f.slab3(0, KD, s), g.XA.v3(0, KD, SLAB))


def kernel(**inputs):
    h, cores = host_prepare(inputs)
    nc = build_program()
    in_maps = []
    for c in range(NCORES):
        m = dict(h)
        m.update(cores[c])
        in_maps.append(m)
    res = run_bass_kernel_spmd(nc, in_maps, core_ids=list(range(NCORES)))
    out = np.stack([np.ascontiguousarray(np.asarray(r["out_f"]).T)
                    for r in res.results])
    return out.astype(np.float32)


# version 03
# speedup vs baseline: 1.0144x; 1.0144x over previous
import numpy as np
import concourse.bass as bass
import concourse.mybir as mybir
from concourse.bass_utils import run_bass_kernel_spmd

F32 = mybir.dt.float32
BF16 = mybir.dt.bfloat16
I32 = mybir.dt.int32
AF = mybir.ActivationFunctionType
ALU = mybir.AluOpType
AX = mybir.AxisListType

D = 1024
T = 4096
KD = 8
NSLAB = 8
SLAB = 512
NTT = 32
NCORES = 8

N_BLOCKS = 4
LAST_PROG = None
DEV_DUMP = False
DEV_TAP = None

BIGNEG = -3.0e38
RC1 = 98304.0
RC2 = RC1 + 384.0


class V:
    __slots__ = ("ap", "keys")

    def __init__(self, ap, keys):
        self.ap = ap
        self.keys = frozenset(keys)

    def __getitem__(self, sl):
        return V(self.ap.__getitem__(sl), self.keys)


class Arena:
    def __init__(self, nc, name, ncols, dtype=F32, gran=512):
        self.name = name
        self.gran = gran
        self.ncols = ncols
        self.t = nc.alloc_sbuf_tensor(name, [128, ncols], dtype)

    def v(self, col0, ncols, r0=0, r1=128):
        assert col0 + ncols <= self.ncols, (self.name, col0, ncols)
        ap = self.t.ap()[r0:r1, col0:col0 + ncols]
        g0 = col0 // self.gran
        g1 = (col0 + ncols - 1) // self.gran
        return V(ap, ((self.name, g) for g in range(g0, g1 + 1)))

    def v3(self, col0, nch, ns):
        vv = self.v(col0, nch * ns)
        return V(vv.ap.rearrange("p (c t) -> p c t", t=ns), vv.keys)


class Psum:
    def __init__(self, nc, name):
        self.name = name
        self.t = nc.alloc_psum_tensor(name, [128, 512], F32)

    def v(self, col0=0, ncols=512, r0=0, r1=128):
        return V(self.t.ap()[r0:r1, col0:col0 + ncols], [(self.name,)])


class Dram:
    def __init__(self, nc, name, shape, dtype=F32, kind="Internal"):
        self.name = name
        self.shape = shape
        self.t = nc.dram_tensor(name, list(shape), dtype, kind=kind)

    def v(self, *slices, key=None):
        ap = self.t.ap()
        if slices:
            ap = ap.__getitem__(tuple(slices))
        if key is None:
            keys = [(self.name,)]
        elif isinstance(key, list):
            keys = [(self.name,) + (k if isinstance(k, tuple) else (k,)) for k in key]
        else:
            keys = [(self.name,) + (key if isinstance(key, tuple) else (key,))]
        return V(ap, keys)

    def cs(self, c, s, nc_=128, ns=SLAB):
        return self.v(slice(c * nc_, (c + 1) * nc_),
                      slice(s * ns, (s + 1) * ns), key=(c, s))

    def slab3(self, c0, nch, s, ns=SLAB, key_s=None):
        ap = self.t.ap().rearrange("(c p) t -> p c t", p=128)
        ap = ap[:, c0:c0 + nch, s * ns:(s + 1) * ns]
        ks = s if key_s is None else key_s
        return V(ap, [(self.name, c, ks) for c in range(c0, c0 + nch)])

    def whole(self):
        return self.v()


class Prog:
    CE = ("te", "ve", "se", "ge", "sy")

    def __init__(self, nc):
        self.nc = nc
        self.ops = []
        self.state = {}

    def op(self, eng, fn, reads=(), writes=(), kind="c"):
        rkeys = [r.keys for r in reads]
        wkeys = [w.keys for w in writes]
        deps = set()
        for kset in rkeys:
            for k in kset:
                st = self.state.get(k)
                if st and st[0] is not None:
                    deps.add(st[0])
        for kset in wkeys:
            for k in kset:
                st = self.state.get(k)
                if st:
                    if st[0] is not None:
                        deps.add(st[0])
                    deps.update(st[1])
        oid = len(self.ops)
        deps.discard(oid)
        self.ops.append(dict(eng=eng, kind=kind, fn=fn, deps=deps))
        for kset in rkeys:
            for k in kset:
                self.state.setdefault(k, [None, set()])[1].add(oid)
        for kset in wkeys:
            for k in kset:
                self.state[k] = [oid, set()]
        return oid

    def dma(self, queue, dst, src, **kw):
        nc = self.nc
        eng = {"sy": nc.sync, "se": nc.scalar, "ge": nc.gpsimd}[queue]
        return self.op(queue, lambda: eng.dma_start(out=dst.ap, in_=src.ap, **kw),
                       reads=[src], writes=[dst], kind="d")

    def mm(self, out, lhsT, rhs, start, stop):
        nc = self.nc
        return self.op("te", lambda: nc.tensor.matmul(
            out.ap, lhsT.ap, rhs.ap, start=start, stop=stop),
            reads=[lhsT, rhs], writes=[out])

    def transp(self, out, in_, ident):
        nc = self.nc
        return self.op("te", lambda: nc.tensor.transpose(out.ap, in_.ap, ident.ap),
                       reads=[in_, ident], writes=[out])

    def tt(self, eng, out, a, b, alu):
        nc = self.nc
        e = {"ve": nc.vector, "ge": nc.gpsimd}[eng]
        return self.op(eng, lambda: e.tensor_tensor(out.ap, a.ap, b.ap, alu),
                       reads=[a, b], writes=[out])

    def ts(self, eng, out, a, s1, s2, op1, op2=None):
        nc = self.nc
        e = {"ve": nc.vector, "ge": nc.gpsimd}[eng]
        reads = [a] + [x for x in (s1, s2) if isinstance(x, V)]
        s1a = s1.ap if isinstance(s1, V) else s1
        s2a = s2.ap if isinstance(s2, V) else s2
        if op2 is None:
            return self.op(eng, lambda: e.tensor_scalar(out.ap, a.ap, s1a, None, op1),
                           reads=reads, writes=[out])
        return self.op(eng, lambda: e.tensor_scalar(out.ap, a.ap, s1a, s2a, op1, op2),
                       reads=reads, writes=[out])

    def stt(self, eng, out, in0, scalar, in1, op0, op1):
        nc = self.nc
        e = {"ve": nc.vector, "ge": nc.gpsimd}[eng]
        sa = scalar.ap if isinstance(scalar, V) else scalar
        reads = [in0, in1] + ([scalar] if isinstance(scalar, V) else [])
        return self.op(eng, lambda: e.scalar_tensor_tensor(
            out.ap, in0.ap, sa, in1.ap, op0, op1), reads=reads, writes=[out])

    def act(self, out, in_, func, bias=0.0, scale=1.0, accum=None):
        nc = self.nc
        reads = [in_] + [x for x in (bias, scale) if isinstance(x, V)]
        ba = bias.ap if isinstance(bias, V) else bias
        sa = scale.ap if isinstance(scale, V) else scale
        writes = [out] + ([accum] if accum is not None else [])
        if accum is None:
            return self.op("se", lambda: nc.scalar.activation(
                out.ap, in_.ap, func, bias=ba, scale=sa), reads=reads, writes=writes)
        return self.op("se", lambda: nc.scalar.activation(
            out.ap, in_.ap, func, bias=ba, scale=sa, accum_out=accum.ap),
            reads=reads, writes=writes)

    def copy(self, eng, out, in_):
        nc = self.nc
        if eng == "se":
            return self.op("se", lambda: nc.scalar.copy(out.ap, in_.ap),
                           reads=[in_], writes=[out])
        e = {"ve": nc.vector, "ge": nc.gpsimd}[eng]
        return self.op(eng, lambda: e.tensor_copy(out.ap, in_.ap),
                       reads=[in_], writes=[out])

    def red(self, out, in_, alu, negate=False, absval=False):
        nc = self.nc
        return self.op("ve", lambda: nc.vector.tensor_reduce(
            out.ap, in_.ap, AX.X, alu,
            apply_absolute_value=absval, negate=negate),
            reads=[in_], writes=[out])

    def memset(self, eng, out, val):
        nc = self.nc
        e = {"ve": nc.vector, "ge": nc.gpsimd}[eng]
        return self.op(eng, lambda: e.memset(out.ap, val), reads=[], writes=[out])

    def recip(self, out, in_):
        nc = self.nc
        return self.op("ve", lambda: nc.vector.reciprocal(out.ap, in_.ap),
                       reads=[in_], writes=[out])

    def scan(self, out, data0, data1, initial):
        nc = self.nc
        init = initial.ap if isinstance(initial, V) else initial
        reads = [data0, data1] + ([initial] if isinstance(initial, V) else [])
        return self.op("ve", lambda: nc.vector.tensor_tensor_scan(
            out.ap, data0.ap, data1.ap, init, ALU.mult, ALU.add),
            reads=reads, writes=[out])

    def max8(self, out, in_):
        nc = self.nc
        return self.op("ve", lambda: nc.vector.max(out=out.ap, in_=in_.ap),
                       reads=[in_], writes=[out])

    def match_replace(self, out, to_replace, values, imm):
        nc = self.nc
        return self.op("ve", lambda: nc.vector.match_replace(
            out=out.ap, in_to_replace=to_replace.ap, in_values=values.ap,
            imm_value=imm), reads=[to_replace, values], writes=[out])

    def collective_max(self, ins, outs):
        nc = self.nc
        return self.op("ge", lambda: nc.gpsimd.collective_compute(
            "AllReduce", ALU.max, replica_groups=[list(range(NCORES))],
            ins=[ins.ap.opt()], outs=[outs.ap.opt()]),
            reads=[ins], writes=[outs], kind="cc")

    def finalize(self, block):
        nc = self.nc
        streams = {e: [] for e in self.CE}
        for oid, o in enumerate(self.ops):
            streams[o["eng"]].append(oid)
        for o in self.ops:
            best = {}
            keep = set()
            for d in o["deps"]:
                do = self.ops[d]
                if do["eng"] == o["eng"] or do["kind"] == "cc":
                    keep.add(d)
                    continue
                k = (do["eng"], do["kind"])
                if k not in best or d > best[k]:
                    best[k] = d
            keep.update(best.values())
            o["deps"] = keep
        need_sig = set()
        for o in self.ops:
            need_sig.update(o["deps"])
        sems = {}
        for e in self.CE:
            sems[e + "_c"] = nc.alloc_semaphore(name="s_" + e)
            sems[e + "_d"] = nc.alloc_semaphore(name="d_" + e)
        sigval = {}
        dcount = {e: 0 for e in self.CE}
        ccount = {e: 0 for e in self.CE}
        for e in self.CE:
            for oid in streams[e]:
                o = self.ops[oid]
                if o["kind"] == "d":
                    dcount[e] += 1
                    sigval[oid] = (e + "_d", 16 * dcount[e])
                elif oid in need_sig:
                    ccount[e] += 1
                    sigval[oid] = (e + "_c", ccount[e])

        def mk_body(e):
            def body(eng):
                know = {}
                for oid in streams[e]:
                    o = self.ops[oid]
                    for dd in sorted(o["deps"]):
                        sname, val = sigval[dd]
                        if know.get(sname, 0) < val:
                            eng.wait_ge(sems[sname], val)
                            know[sname] = val
                    ins = o["fn"]()
                    if o["kind"] == "d":
                        sname, val = sigval[oid]
                        ins.then_inc(sems[sname], 16)
                        if oid in need_sig:
                            eng.wait_ge(sems[sname], val)
                            know[sname] = val
                    elif oid in need_sig:
                        if e == "te":
                            eng.drain()
                            eng.nop().then_inc(sems[sigval[oid][0]], 1)
                        else:
                            ins.then_inc(sems[sigval[oid][0]], 1)
                if dcount[e]:
                    eng.wait_ge(sems[e + "_d"], 16 * dcount[e])
            return body

        block.tensor(mk_body("te"))
        block.vector(mk_body("ve"))
        block.scalar(mk_body("se"))
        block.gpsimd(mk_body("ge"))
        block.sync(mk_body("sy"))


CONSTS = {
    "eps_ln": 1e-5,
    "one": 1.0,
}
CIDX = {k: i for i, k in enumerate(CONSTS)}


def _bias_cols(b):
    b = np.asarray(b, np.float32).reshape(-1)
    n = b.size // 128
    return np.ascontiguousarray(b.reshape(n, 128).T)


def _rope_tables():
    half = 32
    theta = 1.0 / (10000.0 ** (np.arange(half, dtype=np.float64) / half))
    ang = np.arange(T, dtype=np.float64)[:, None] * theta[None, :]
    sin = np.sin(ang).astype(np.float32).T
    cos = np.cos(ang).astype(np.float32).T
    C = np.tile(cos, (4, 1))
    S = np.tile(np.concatenate([-sin, sin], axis=0), (2, 1))
    return np.ascontiguousarray(C), np.ascontiguousarray(S)


def _diag_masks():
    out = []
    for off in range(0, 512, 128):
        q = off + np.arange(128)[:, None]
        k = np.arange(512)[None, :]
        out.append(np.where(k <= q, np.float32(0.0), np.float32(BIGNEG)))
    return np.ascontiguousarray(np.stack(out).astype(np.float32))


def host_prepare(inputs):
    x = np.asarray(inputs["L0_attn_x"], np.float32)
    h = {}
    perm = np.zeros(D, np.int64)
    for hh in range(16):
        b0 = hh * 64
        perm[b0:b0 + 32] = b0 + np.arange(0, 64, 2)
        perm[b0 + 32:b0 + 64] = b0 + np.arange(1, 64, 2)
    h["l0_wqT"] = np.ascontiguousarray(np.asarray(inputs["L0_attn_W_q"], np.float32).T[:, perm])
    h["l0_wkT"] = np.ascontiguousarray(np.asarray(inputs["L0_attn_W_k"], np.float32).T[:, perm])
    h["l0_wvT"] = np.ascontiguousarray(np.asarray(inputs["L0_attn_W_v"], np.float32).T)
    h["l0_woT"] = np.ascontiguousarray(np.asarray(inputs["L0_attn_W_o"], np.float32).T)
    h["l0_bq"] = _bias_cols(np.asarray(inputs["L0_attn_b_q"], np.float32)[perm])
    h["l0_bk"] = _bias_cols(np.asarray(inputs["L0_attn_b_k"], np.float32)[perm])
    h["l0_bv_row"] = np.asarray(inputs["L0_attn_b_v"], np.float32).reshape(1, D).copy()
    h["l0_bo"] = _bias_cols(inputs["L0_attn_b_o"])
    C, S = _rope_tables()
    h["ropeCq"] = np.ascontiguousarray(C * np.float32(0.125))
    h["ropeSq"] = np.ascontiguousarray(S * np.float32(0.125))
    h["ropeCk"] = C
    h["ropeSk"] = S
    h["dmask"] = _diag_masks()
    h["l0_win"] = np.ascontiguousarray(np.asarray(inputs["L0_ffn_W_in"], np.float32))
    h["l0_bin"] = _bias_cols(inputs["L0_ffn_b_in"])
    h["l0_wout"] = np.ascontiguousarray(np.asarray(inputs["L0_ffn_W_out"], np.float32))
    h["l0_bout"] = _bias_cols(inputs["L0_ffn_b_out"])
    h["ident"] = np.eye(128, dtype=np.float32)
    h["ones"] = np.ones((128, 1), np.float32)
    cv = np.zeros((128, 8), np.float32)
    for k, i in CIDX.items():
        cv[:, i] = CONSTS[k]
    h["cvec"] = np.ascontiguousarray(cv)
    h["zeros64"] = np.zeros((1, 64), np.float32)
    for d_ in ("fwd", "bwd"):
        wz = np.asarray(inputs[f"L1_attn_{d_}_Wz"], np.float32)
        wh = np.asarray(inputs[f"L1_attn_{d_}_Wh"], np.float32)
        h[f"l1a_wz_{d_}"] = np.ascontiguousarray(
            np.concatenate([wz[i].T for i in range(4)], axis=0))
        h[f"l1a_wh_{d_}"] = np.ascontiguousarray(
            np.concatenate([wh[i].T for i in range(4)], axis=0))
        bz = np.asarray(inputs[f"L1_attn_{d_}_bz"], np.float32)
        bh = np.asarray(inputs[f"L1_attn_{d_}_bh"], np.float32)
        h[f"l1a_bz_{d_}"] = np.ascontiguousarray(
            np.concatenate([_bias_cols(bz[i]) for i in range(4)], axis=1))
        h[f"l1a_bh_{d_}"] = np.ascontiguousarray(
            np.concatenate([_bias_cols(bh[i]) for i in range(4)], axis=1))
    h["l1a_fusionT"] = np.ascontiguousarray(
        np.asarray(inputs["L1_attn_fusion_W"], np.float32).T)
    h["l1a_fb"] = _bias_cols(inputs["L1_attn_fusion_b"])
    h["l1a_g"] = _bias_cols(inputs["L1_attn_ln_gamma"])
    h["l1a_b"] = _bias_cols(inputs["L1_attn_ln_beta"])

    w1 = np.asarray(inputs["L1_ffn_w1"], np.float32)
    h["l1f_w1a"] = np.ascontiguousarray(w1[:D])
    h["l1f_w1b"] = np.ascontiguousarray(w1[D:])
    h["l1f_b1"] = _bias_cols(inputs["L1_ffn_b1"])
    h["l1f_w2"] = np.ascontiguousarray(np.asarray(inputs["L1_ffn_w2"], np.float32))
    h["l1f_b2row"] = np.asarray(inputs["L1_ffn_b2"], np.float32).reshape(1, 64).copy()
    h["l1f_tau"] = np.asarray(inputs["L1_ffn_tau"], np.float32).reshape(1, 1).copy()
    h["l1f_wihT"] = np.ascontiguousarray(np.asarray(inputs["L1_ffn_Wih"], np.float32).T)
    h["l1f_whhT"] = np.ascontiguousarray(np.asarray(inputs["L1_ffn_Whh"], np.float32).T)
    h["l1f_bih"] = _bias_cols(inputs["L1_ffn_bih"])
    h["l1f_bhh"] = _bias_cols(inputs["L1_ffn_bhh"])
    h["l1f_wq"] = np.ascontiguousarray(np.asarray(inputs["L1_ffn_wq"], np.float32))
    h["l1f_bq"] = _bias_cols(inputs["L1_ffn_bq"])
    h["l1f_wk"] = np.ascontiguousarray(np.asarray(inputs["L1_ffn_wk"], np.float32))
    h["l1f_bk"] = _bias_cols(inputs["L1_ffn_bk"])
    h["l1f_wv"] = np.ascontiguousarray(np.asarray(inputs["L1_ffn_wv"], np.float32))
    h["l1f_bvrow"] = np.asarray(inputs["L1_ffn_bv"], np.float32).reshape(1, D).copy()
    h["l1f_lng"] = _bias_cols(inputs["L1_ffn_ln_g"])
    h["l1f_lnb"] = _bias_cols(inputs["L1_ffn_ln_b"])
    h["l1f_wvp"] = np.ascontiguousarray(np.asarray(inputs["L1_ffn_wvp"], np.float32))
    h["l1f_bvprow"] = np.asarray(inputs["L1_ffn_bvp"], np.float32).reshape(1, D).copy()
    h["l1f_wo"] = np.ascontiguousarray(np.asarray(inputs["L1_ffn_wo"], np.float32))
    h["l1f_bo"] = _bias_cols(inputs["L1_ffn_bo"])
    h["l1f_slot"] = np.ascontiguousarray(np.asarray(inputs["L1_ffn_slot_init"], np.float32))

    h["l2a_ln1g"] = _bias_cols(inputs["L2_attn_ln1_g"])
    h["l2a_ln1b"] = _bias_cols(inputs["L2_attn_ln1_b"])
    h["l2a_ln2g"] = _bias_cols(inputs["L2_attn_ln2_g"])
    h["l2a_ln2b"] = _bias_cols(inputs["L2_attn_ln2_b"])
    h["l2a_inW"] = np.ascontiguousarray(np.asarray(inputs["L2_attn_in_W"], np.float32))
    h["l2a_inb"] = _bias_cols(inputs["L2_attn_in_b"])
    dw = np.asarray(inputs["L2_attn_dw_W"], np.float32)
    h["l2a_dw"] = np.ascontiguousarray(
        np.stack([_bias_cols(dw[:, 0, k]) for k in range(3)]))
    h["l2a_dwb"] = _bias_cols(inputs["L2_attn_dw_b"])
    h["l2a_dtW"] = np.ascontiguousarray(np.asarray(inputs["L2_attn_dt_W"], np.float32))
    h["l2a_dtb"] = _bias_cols(inputs["L2_attn_dt_b"])
    h["l2a_bW"] = np.ascontiguousarray(np.asarray(inputs["L2_attn_b_W"], np.float32))
    h["l2a_bb"] = _bias_cols(inputs["L2_attn_b_b"])
    h["l2a_cW"] = np.ascontiguousarray(np.asarray(inputs["L2_attn_c_W"], np.float32))
    h["l2a_cb"] = _bias_cols(inputs["L2_attn_c_b"])
    h["l2a_alog"] = _bias_cols(inputs["L2_attn_a_log"])
    h["l2a_dskip"] = _bias_cols(inputs["L2_attn_d_skip"])
    h["l2a_outW"] = np.ascontiguousarray(np.asarray(inputs["L2_attn_out_W"], np.float32))
    h["l2a_outb"] = _bias_cols(inputs["L2_attn_out_b"])
    h["l2a_f1W"] = np.ascontiguousarray(np.asarray(inputs["L2_attn_f1_W"], np.float32))
    h["l2a_f1b"] = _bias_cols(inputs["L2_attn_f1_b"])
    h["l2a_f2W"] = np.ascontiguousarray(np.asarray(inputs["L2_attn_f2_W"], np.float32))
    h["l2a_f2b"] = _bias_cols(inputs["L2_attn_f2_b"])

    SC2 = np.float32(1.0 / np.sqrt(128.0))
    h["l2f_qT"] = np.ascontiguousarray(
        np.asarray(inputs["L2_ffn_q_w"], np.float32).T * SC2)
    h["l2f_kT"] = np.ascontiguousarray(np.asarray(inputs["L2_ffn_k_w"], np.float32).T)
    vw = np.asarray(inputs["L2_ffn_v_w"], np.float32)
    h["l2f_vwT"] = np.ascontiguousarray(vw.transpose(1, 0, 2).reshape(D, 64 * 128))
    h["l2f_ow"] = np.ascontiguousarray(
        np.asarray(inputs["L2_ffn_o_w"], np.float32).reshape(64 * 128, D))
    h["l2f_svT"] = np.ascontiguousarray(np.asarray(inputs["L2_ffn_sel_v_w"], np.float32).T)
    h["l2f_soT"] = np.ascontiguousarray(np.asarray(inputs["L2_ffn_sel_o_w"], np.float32).T)
    h["l2f_rs"] = np.asarray(inputs["L2_ffn_route_scale"], np.float32).reshape(1, 1).copy()

    h["l3a_wqT"] = np.ascontiguousarray(np.asarray(inputs["L3_attn_Wq"], np.float32).T)
    h["l3a_wkdT"] = np.ascontiguousarray(np.asarray(inputs["L3_attn_Wk_down"], np.float32).T)
    h["l3a_wvdT"] = np.ascontiguousarray(np.asarray(inputs["L3_attn_Wv_down"], np.float32).T)
    h["l3a_q2l"] = np.ascontiguousarray(
        (np.asarray(inputs["L3_attn_q_to_latent"], np.float32) *
         np.float32(0.125)).reshape(16 * 64, 64))
    h["l3a_vup"] = np.ascontiguousarray(
        np.asarray(inputs["L3_attn_v_up"], np.float32).reshape(16 * 64, 64))
    h["l3a_woT"] = np.ascontiguousarray(np.asarray(inputs["L3_attn_Wout"], np.float32).T)

    h["l3f_wgT"] = np.ascontiguousarray(np.asarray(inputs["L3_ffn_w_gate"], np.int32).T)
    h["l3f_wuT"] = np.ascontiguousarray(np.asarray(inputs["L3_ffn_w_up"], np.int32).T)
    h["l3f_wdT"] = np.ascontiguousarray(np.asarray(inputs["L3_ffn_w_down"], np.int32).T)

    cores = [{"x_f": np.ascontiguousarray(x[c].T)} for c in range(NCORES)]
    return h, cores


class Ctx:
    pass


def build_program():
    nc = bass.Bass("TRN2", target_bir_lowering=False)
    g = Ctx()
    g.nc = nc

    def inp(name, shape, dtype=F32):
        return Dram(nc, name, shape, dtype, kind="ExternalInput")

    g.x_f = inp("x_f", [D, T])
    g.l0_wqT = inp("l0_wqT", [D, D]); g.l0_wkT = inp("l0_wkT", [D, D])
    g.l0_wvT = inp("l0_wvT", [D, D]); g.l0_woT = inp("l0_woT", [D, D])
    g.l0_bq = inp("l0_bq", [128, 8]); g.l0_bk = inp("l0_bk", [128, 8])
    g.l0_bv_row = inp("l0_bv_row", [1, D]); g.l0_bo = inp("l0_bo", [128, 8])
    g.ropeCq = inp("ropeCq", [128, T]); g.ropeSq = inp("ropeSq", [128, T])
    g.ropeCk = inp("ropeCk", [128, T]); g.ropeSk = inp("ropeSk", [128, T])
    g.dmask = inp("dmask", [4, 128, 512])
    g.l0_win = inp("l0_win", [D, 4 * D]); g.l0_bin = inp("l0_bin", [128, 32])
    g.l0_wout = inp("l0_wout", [4 * D, D]); g.l0_bout = inp("l0_bout", [128, 8])
    g.ident = inp("ident", [128, 128])
    g.ones = inp("ones", [128, 1])
    g.cvec = inp("cvec", [128, 8])
    g.zeros64 = inp("zeros64", [1, 64])
    if N_BLOCKS >= 2:
        for d_ in ("fwd", "bwd"):
            setattr(g, f"l1a_wz_{d_}", inp(f"l1a_wz_{d_}", [4 * D, D]))
            setattr(g, f"l1a_wh_{d_}", inp(f"l1a_wh_{d_}", [4 * D, D]))
            setattr(g, f"l1a_bz_{d_}", inp(f"l1a_bz_{d_}", [128, 32]))
            setattr(g, f"l1a_bh_{d_}", inp(f"l1a_bh_{d_}", [128, 32]))
        g.l1a_fusionT = inp("l1a_fusionT", [2 * D, D])
        g.l1a_fb = inp("l1a_fb", [128, 8])
        g.l1a_g = inp("l1a_g", [128, 8]); g.l1a_b = inp("l1a_b", [128, 8])
        g.l1f_w1a = inp("l1f_w1a", [D, 512]); g.l1f_w1b = inp("l1f_w1b", [D, 512])
        g.l1f_b1 = inp("l1f_b1", [128, 4]); g.l1f_w2 = inp("l1f_w2", [512, 64])
        g.l1f_b2row = inp("l1f_b2row", [1, 64]); g.l1f_tau = inp("l1f_tau", [1, 1])
        g.l1f_wihT = inp("l1f_wihT", [D, 3 * D]); g.l1f_whhT = inp("l1f_whhT", [D, 3 * D])
        g.l1f_bih = inp("l1f_bih", [128, 24]); g.l1f_bhh = inp("l1f_bhh", [128, 24])
        g.l1f_wq = inp("l1f_wq", [D, 256]); g.l1f_bq = inp("l1f_bq", [128, 2])
        g.l1f_wk = inp("l1f_wk", [D, 256]); g.l1f_bk = inp("l1f_bk", [128, 2])
        g.l1f_wv = inp("l1f_wv", [D, D]); g.l1f_bvrow = inp("l1f_bvrow", [1, D])
        g.l1f_lng = inp("l1f_lng", [128, 8]); g.l1f_lnb = inp("l1f_lnb", [128, 8])
        g.l1f_wvp = inp("l1f_wvp", [D, D]); g.l1f_bvprow = inp("l1f_bvprow", [1, D])
        g.l1f_wo = inp("l1f_wo", [D, D]); g.l1f_bo = inp("l1f_bo", [128, 8])
        g.l1f_slot = inp("l1f_slot", [64, D])
    if N_BLOCKS >= 3:
        for nm, shp in [("l2a_ln1g", [128, 8]), ("l2a_ln1b", [128, 8]),
                        ("l2a_ln2g", [128, 8]), ("l2a_ln2b", [128, 8]),
                        ("l2a_inW", [D, 2 * D]), ("l2a_inb", [128, 16]),
                        ("l2a_dw", [3, 128, 8]), ("l2a_dwb", [128, 8]),
                        ("l2a_dtW", [D, D]), ("l2a_dtb", [128, 8]),
                        ("l2a_bW", [D, D]), ("l2a_bb", [128, 8]),
                        ("l2a_cW", [D, D]), ("l2a_cb", [128, 8]),
                        ("l2a_alog", [128, 8]), ("l2a_dskip", [128, 8]),
                        ("l2a_outW", [D, D]), ("l2a_outb", [128, 8]),
                        ("l2a_f1W", [D, 2 * D]), ("l2a_f1b", [128, 16]),
                        ("l2a_f2W", [2 * D, D]), ("l2a_f2b", [128, 8]),
                        ("l2f_qT", [D, D]), ("l2f_kT", [D, D]),
                        ("l2f_vwT", [D, 64 * 128]), ("l2f_ow", [64 * 128, D]),
                        ("l2f_svT", [D, 64]), ("l2f_soT", [D, 64]),
                        ("l2f_rs", [1, 1])]:
            setattr(g, nm, inp(nm, shp))
    if N_BLOCKS >= 4:
        for nm, shp, dt in [("l3a_wqT", [D, D], F32), ("l3a_wkdT", [D, 64], F32),
                            ("l3a_wvdT", [D, 64], F32), ("l3a_q2l", [16 * 64, 64], F32),
                            ("l3a_vup", [16 * 64, 64], F32), ("l3a_woT", [D, D], F32),
                            ("l3f_wgT", [D, 3 * D], I32), ("l3f_wuT", [D, 3 * D], I32),
                            ("l3f_wdT", [3 * D, D], I32)]:
            setattr(g, nm, inp(nm, shp, dt))

    g.out_f = Dram(nc, "out_f", [D, T], kind="ExternalOutput")

    g.xf = [Dram(nc, f"xf{i}", [D, T]) for i in range(2)]
    g.xnf = Dram(nc, "xnf", [D, T])
    g.sf = Dram(nc, "sf", [D, T])
    g.gf = Dram(nc, "gf", [D, T])
    g.qrope = Dram(nc, "qrope", [D, T])
    g.krope = Dram(nc, "krope", [D, T])
    g.vtok = Dram(nc, "vtok", [T, D])
    g.attnf = Dram(nc, "attnf", [D, T])
    g.xsimf = Dram(nc, "xsimf", [D, T])
    g.hidf = Dram(nc, "hidf", [4 * D, T])
    g.w_in_sim = Dram(nc, "w_in_sim", [D, 4 * D])
    g.w_out_sim = Dram(nc, "w_out_sim", [4 * D, D])
    g.rows = Dram(nc, "rows", [6, 4 * D])
    g.cc_in = Dram(nc, "cc_in", [1, 64])
    g.cc_out = Dram(nc, "cc_out", [1, 64])
    for i in range(8):
        setattr(g, f"sc{i}", Dram(nc, f"sc{i}", [D, T]))
    g.alpha_t = Dram(nc, "alpha_t", [T, 64])
    g.gv_t = Dram(nc, "gv_t", [T, 64])
    g.go_t = Dram(nc, "go_t", [T, 64])
    g.lv_t = Dram(nc, "lv_t", [T, 64])

    with nc.Block() as block:
        p = Prog(nc)
        g.p = p
        global LAST_PROG
        LAST_PROG = p
        g.WA = Arena(nc, "WA", 16384)
        g.WB = Arena(nc, "WB", 8192)
        g.XA = Arena(nc, "XA", 8192)
        g.XB = Arena(nc, "XB", 8192)
        g.YA = Arena(nc, "YA", 4096)
        g.SM = Arena(nc, "SM", 1024, gran=8)
        g.BB = Arena(nc, "BB", 4096, BF16)
        g.BI = Arena(nc, "BI", 2048, I32)
        g.ps = [Psum(nc, f"ps{i}") for i in range(8)]

        g.sb_ident = g.SM.v(0, 128)
        g.sb_ones = g.SM.v(128, 1)
        g.sb_cvec = g.SM.v(136, 8)
        p.dma("sy", g.sb_ident, g.ident.whole())
        p.dma("sy", g.sb_ones, g.ones.whole())
        p.dma("sy", g.sb_cvec, g.cvec.whole())

        emit_all(g)
        p.finalize(block)
    return nc


def dve_abs(g, eng, out, in_, tmp):
    p = g.p
    p.ts(eng, tmp, in_, -1.0, None, ALU.mult)
    p.tt(eng, out, tmp, in_, ALU.max)


def ident64(g):
    return V(g.sb_ident.ap[0:64, 0:64], g.sb_ident.keys)


def ccv(g, name, rows=128):
    v = g.SM.v(136 + CIDX[name], 1)
    return V(v.ap[0:rows, :], v.keys)


def round_even(g, eng, buf, tmp_f, tmp_f2, tmp_b):
    p = g.p
    C1 = 98304.0
    p.ts(eng, tmp_f, buf, C1, None, ALU.add)
    p.copy(eng, tmp_b, tmp_f)
    p.copy(eng, tmp_f2, tmp_b)
    p.ts(eng, tmp_f, tmp_b, -(C1 + 768.0), None, ALU.add)
    p.tt(eng, buf, buf, tmp_f, ALU.subtract)
    p.copy(eng, tmp_b, buf)
    p.tt(eng, tmp_f2, tmp_f2, tmp_b, ALU.add)
    p.ts(eng, tmp_f, tmp_b, -192.0, None, ALU.add)
    p.tt(eng, buf, buf, tmp_f, ALU.subtract)
    p.copy(eng, tmp_b, buf)
    p.tt(eng, tmp_f2, tmp_f2, tmp_b, ALU.add)
    p.ts(eng, buf, tmp_f2, -(C1 + 960.0), None, ALU.add)


def quantize_tile(g, eng, buf, scl, inv, lo, hi, tmp_f, tmp_f2, tmp_b):
    p = g.p
    p.ts(eng, buf, buf, scl, None, ALU.mult)
    round_even(g, eng, buf, tmp_f, tmp_f2, tmp_b)
    p.ts(eng, buf, buf, lo, hi, ALU.max, ALU.min)
    p.ts(eng, buf, buf, inv, None, ALU.mult)


def fold_partitions(g, buf, tmp, alu=ALU.max, width=128):
    p = g.p
    while width > 1:
        half = width // 2
        p.dma("ge", tmp[0:half, :], buf[half:width, :])
        p.tt("ve", buf[0:half, :], buf[0:half, :], tmp[0:half, :], alu)
        width = half
    return buf[0:1, :]


def bounce_rows(g, rowV, row_idx, col0, n, out128):
    p = g.p
    dst = g.rows.v(slice(row_idx, row_idx + 1), slice(col0, col0 + n),
                   key=(row_idx, col0))
    p.dma("se", dst, rowV)
    p.dma("sy", out128, V(dst.ap.broadcast_to([128, n]), dst.keys))


def emit_ln(g, xin, xout=None, gamma=None, beta=None, add=None):
    p = g.p
    if xout is None:
        xout = g.xnf
    for s in range(NSLAB):
        p.dma("sy" if s % 2 else "se", g.XA.v3(0, KD, SLAB), xin.slab3(0, KD, s))
        xs = [g.XA.v(c * SLAB, SLAB) for c in range(KD)]
        if add is not None:
            p.dma("ge", g.XB.v3(4096, KD, SLAB), add.slab3(0, KD, s))
            for c in range(KD):
                p.tt("ve" if c % 2 else "ge", xs[c], xs[c],
                     g.XB.v(4096 + c * SLAB, SLAB), ALU.add)
        ps_s = g.ps[0].v(0, SLAB, 0, 1)
        ps_q = g.ps[1].v(0, SLAB, 0, 1)
        sq = g.XB.v(0, SLAB)
        for c in range(KD):
            p.mm(ps_s, g.sb_ones, xs[c], start=(c == 0), stop=(c == KD - 1))
        for c in range(KD):
            p.tt("ve", sq, xs[c], xs[c], ALU.mult)
            p.mm(ps_q, g.sb_ones, sq, start=(c == 0), stop=(c == KD - 1))
        mean = g.XB.v(512, SLAB, 0, 1)
        msq = g.XB.v(1024, SLAB, 0, 1)
        var = g.XB.v(1536, SLAB, 0, 1)
        rstd = g.XB.v(2048, SLAB, 0, 1)
        mr = g.XB.v(2560, SLAB, 0, 1)
        p.ts("ve", mean, ps_s, 1.0 / D, None, ALU.mult)
        p.tt("ve", msq, mean, mean, ALU.mult)
        p.stt("ve", var, ps_q, 1.0 / D, msq, ALU.mult, ALU.subtract)
        sdt = g.XB.v(3584, SLAB, 0, 1)
        p.act(sdt, var, AF.Sqrt, bias=ccv(g, "eps_ln", 1))
        p.recip(rstd, sdt)
        p.tt("ve", mr, mean, rstd, ALU.mult)
        rstd_b = g.XB.v(3072, SLAB)
        mr_b = g.XB.v(3584, SLAB)
        bounce_rows(g, rstd, 0, s * SLAB, SLAB, rstd_b)
        bounce_rows(g, mr, 1, s * SLAB, SLAB, mr_b)
        for c in range(KD):
            t = g.XB.v(4096 + c * SLAB, SLAB)
            eng = "ve" if c % 2 else "ge"
            p.tt(eng, t, xs[c], rstd_b, ALU.mult)
            p.tt(eng, t, t, mr_b, ALU.subtract)
            if gamma is not None:
                p.ts(eng, t, t, V(gamma.ap[:, c:c + 1], gamma.keys),
                     V(beta.ap[:, c:c + 1], beta.keys), ALU.mult, ALU.add)
        p.dma("se", xout.slab3(0, KD, s), g.XB.v3(4096, KD, SLAB))


def emit_l0_attn(g):
    p = g.p

    def load_w(wdram, tag):
        out = []
        for c in range(KD):
            vdst = g.WA.v(c * D, D)
            p.dma("sy", vdst, wdram.v(slice(c * 128, c * 128 + 128),
                                      slice(0, D), key=(tag, c)))
            out.append(vdst)
        return out

    for which, wdram, bias_sb, Cd, Sd, outd in (
        ("q", g.l0_wqT, g.sb_bq, g.ropeCq, g.ropeSq, g.qrope),
        ("k", g.l0_wkT, g.sb_bk, g.ropeCk, g.ropeSk, g.krope),
    ):
        wch = load_w(wdram, which)
        for s in range(NSLAB):
            p.dma("sy", g.XA.v3(0, KD, SLAB), g.xnf.slab3(0, KD, s))
            xs = [g.XA.v(c * SLAB, SLAB) for c in range(KD)]
            ct = g.WB.v(4096, SLAB)
            st_ = g.WB.v(4608, SLAB)
            p.dma("sy", ct, Cd.v(slice(0, 128),
                                 slice(s * SLAB, (s + 1) * SLAB), key=s))
            p.dma("sy", st_, Sd.v(slice(0, 128),
                                  slice(s * SLAB, (s + 1) * SLAB), key=s))
            for j in range(KD):
                psb = g.ps[j % 4]
                for c in range(KD):
                    p.mm(psb.v(), V(wch[c].ap[:, j * 128:(j + 1) * 128],
                                    wch[c].keys),
                         xs[c], start=(c == 0), stop=(c == KD - 1))
                qt = g.XB.v(j * SLAB, SLAB)
                p.act(qt, psb.v(), AF.Identity,
                      bias=V(bias_sb.ap[:, j:j + 1], bias_sb.keys))
                sw = g.XB.v(4096 + (j % 4) * SLAB, SLAB)
                for b4 in range(4):
                    p.dma("ge", sw[b4 * 32:(b4 + 1) * 32, :],
                          qt[(b4 ^ 1) * 32:((b4 ^ 1) + 1) * 32, :])
                ro = g.YA.v(j * SLAB, SLAB)
                p.tt("ve", ro, qt, ct, ALU.mult)
                p.tt("ve", sw, sw, st_, ALU.mult)
                p.tt("ve", ro, ro, sw, ALU.add)
            p.dma("se", outd.slab3(0, KD, s), g.YA.v3(0, KD, SLAB))

    wv = load_w(g.l0_wvT, "v")
    bvb = g.WB.v(2048, D)
    bvsrc = g.l0_bv_row.whole()
    p.dma("sy", bvb, V(bvsrc.ap.broadcast_to([128, D]), bvsrc.keys))
    for tt_ in range(NTT):
        p.dma("sy", g.XA.v3(0, KD, 128),
              g.xnf.slab3(0, KD, tt_, ns=128, key_s=tt_ // 4))
        xs = [g.XA.v(c * 128, 128) for c in range(KD)]
        for js in range(2):
            psb = g.ps[4 + js]
            for c in range(KD):
                p.mm(psb.v(), xs[c],
                     V(wv[c].ap[:, js * SLAB:(js + 1) * SLAB], wv[c].keys),
                     start=(c == 0), stop=(c == KD - 1))
            vt = g.YA.v(2048 + js * SLAB, SLAB)
            p.tt("ve", vt, psb.v(),
                 V(bvb.ap[:, js * SLAB:(js + 1) * SLAB], bvb.keys), ALU.add)
        p.dma("se", g.vtok.v(slice(tt_ * 128, (tt_ + 1) * 128), slice(0, D),
                             key=(tt_,)), g.YA.v(2048, 2 * SLAB))

    mks = g.WB.v(0, 2048)
    dsrc = g.dmask.t.ap().transpose([1, 0, 2])
    p.dma("sy", V(mks.ap.rearrange("p (m c) -> p m c", c=512), mks.keys),
          V(dsrc, [("dmask",)]))
    for hp in range(8):
        qh2 = g.XA.v(0, T)
        kh2 = g.XA.v(4096, T)
        vh2 = g.XB.v(0, T)
        p.dma("sy", qh2, g.qrope.v(slice(hp * 128, (hp + 1) * 128), slice(0, T),
                                   key=[(hp, s) for s in range(NSLAB)]))
        p.dma("sy", kh2, g.krope.v(slice(hp * 128, (hp + 1) * 128), slice(0, T),
                                   key=[(hp, s) for s in range(NSLAB)]))
        vsrc = g.vtok.t.ap().rearrange("(a p) d -> p a d", p=128)
        vsrc = vsrc[:, :, hp * 128:(hp + 1) * 128]
        p.dma("sy", vh2, V(vsrc, [("vtok", t2) for t2 in range(NTT)]))
        for qt_ in range(NTT):
            hold2 = g.YA.v(0, 128)
            for h01 in range(2):
                acc = g.YA.v(512, 64)
                nm = [g.SM.v(512, 1), g.SM.v(520, 1)]
                lsum = g.SM.v(528, 1)
                alpha = g.SM.v(536, 1)
                rs = g.SM.v(544, 1)
                p.memset("ve", acc, 0.0)
                p.memset("ve", nm[0], 3.0e38)
                p.memset("ve", lsum, 0.0)
                nks = qt_ // 4 + 1
                for ks in range(nks):
                    pss = g.ps[0]
                    p.mm(pss.v(),
                         V(qh2.ap[h01 * 64:(h01 + 1) * 64,
                                  qt_ * 128:(qt_ + 1) * 128], qh2.keys),
                         V(kh2.ap[h01 * 64:(h01 + 1) * 64,
                                  ks * SLAB:(ks + 1) * SLAB], kh2.keys),
                         start=True, stop=True)
                    if ks == qt_ // 4:
                        p.tt("ve", pss.v(), pss.v(),
                             V(mks.ap[:, (qt_ % 4) * SLAB:
                                      ((qt_ % 4) + 1) * SLAB], mks.keys),
                             ALU.add)
                    nmo, nmn = nm[ks % 2], nm[(ks + 1) % 2]
                    nmt = g.SM.v(552, 1)
                    p.red(nmt, pss.v(), ALU.max, negate=True)
                    p.tt("ve", nmn, nmo, nmt, ALU.min)
                    p.act(alpha, nmo, AF.Exp, bias=nmn, scale=-1.0)
                    p_sb = g.XB.v(4096, SLAB)
                    p.act(p_sb, pss.v(), AF.Exp, bias=nmn, accum=rs)
                    p.stt("ve", lsum, lsum, alpha, rs, ALU.mult, ALU.add)
                    pst = g.ps[1]
                    for i4 in range(4):
                        p.transp(pst.v(i4 * 128, 128),
                                 V(p_sb.ap[:, i4 * 128:(i4 + 1) * 128],
                                   p_sb.keys), g.sb_ident)
                    pT = g.XB.v(4608, SLAB)
                    p.copy("se", pT, pst.v())
                    psv = g.ps[2]
                    for i4 in range(4):
                        kk = ks * 4 + i4
                        p.mm(psv.v(0, 64),
                             V(pT.ap[:, i4 * 128:(i4 + 1) * 128], pT.keys),
                             V(vh2.ap[:, kk * 128 + h01 * 64:
                                      kk * 128 + h01 * 64 + 64], vh2.keys),
                             start=(i4 == 0), stop=(i4 == 3))
                    p.stt("ve", acc, acc, alpha, psv.v(0, 64),
                          ALU.mult, ALU.add)
                linv = g.SM.v(560, 1)
                p.recip(linv, lsum)
                p.ts("ve", V(hold2.ap[:, h01 * 64:(h01 + 1) * 64], hold2.keys),
                     acc, linv, None, ALU.mult)
            ps3 = g.ps[3]
            p.transp(ps3.v(0, 128), hold2, g.sb_ident)
            outt = g.YA.v(256, 128)
            p.copy("se", outt, ps3.v(0, 128))
            p.dma("se", g.attnf.v(slice(hp * 128, (hp + 1) * 128),
                                  slice(qt_ * 128, (qt_ + 1) * 128),
                                  key=(hp, ("qt", qt_ // 4))), outt)

    wo = load_w(g.l0_woT, "o")
    for s in range(NSLAB):
        p.dma("sy", g.XA.v3(0, KD, SLAB),
              g.attnf.slab3(0, KD, s, key_s=("qt", s)))
        xs = [g.XA.v(c * SLAB, SLAB) for c in range(KD)]
        for j in range(KD):
            psb = g.ps[j % 4]
            for c in range(KD):
                p.mm(psb.v(), V(wo[c].ap[:, j * 128:(j + 1) * 128], wo[c].keys),
                     xs[c], start=(c == 0), stop=(c == KD - 1))
            ot = g.XB.v(j * SLAB, SLAB)
            p.act(ot, psb.v(), AF.Identity,
                  bias=V(g.sb_bo.ap[:, j:j + 1], g.sb_bo.keys))
        p.dma("se", g.sf.slab3(0, KD, s), g.XB.v3(0, KD, SLAB))


def emit_l0_ffn(g):
    p = g.p
    DFF = 4 * D

    def global_amax_allreduce(amx128, idx):
        col = g.SM.v(576 + idx * 64, 1)
        p.red(col, amx128, ALU.max)
        tmp = g.XB.v(3584, 1)
        one_ = fold_partitions(g, col, tmp)
        p.ts("ve", one_, one_, 1e-8, None, ALU.max)
        p.dma("ge", g.cc_in.whole(), g.zeros64.whole())
        ccw = g.cc_in.whole()
        p.dma("ge", V(ccw.ap[0:1, 0:1], ccw.keys), one_)
        p.collective_max(g.cc_in.whole(), g.cc_out.whole())
        cco = g.cc_out.whole()
        gmax = g.SM.v(584 + idx * 64, 1, 0, 1)
        p.dma("ge", gmax, V(cco.ap[0:1, 0:1], cco.keys))
        scl1 = g.SM.v(592 + idx * 64, 1, 0, 1)
        inv1 = g.SM.v(600 + idx * 64, 1, 0, 1)
        p.recip(scl1, gmax)
        p.ts("ve", scl1, scl1, 32767.0, None, ALU.mult)
        p.recip(inv1, scl1)
        sclb = g.SM.v(608 + idx * 64, 1)
        invb = g.SM.v(616 + idx * 64, 1)
        bounce_rows(g, scl1, 2, idx * 8, 1, sclb)
        bounce_rows(g, inv1, 2, idx * 8 + 4, 1, invb)
        return sclb, invb

    amx = g.XB.v(3072, SLAB)
    for s in range(NSLAB):
        p.dma("sy", g.XA.v3(0, KD, SLAB), g.xnf.slab3(0, KD, s))
        for c in range(KD):
            xc = g.XA.v(c * SLAB, SLAB)
            ab = g.XB.v(2560, SLAB)
            dve_abs(g, "ve", ab, xc, g.XB.v(2048, SLAB))
            if s == 0 and c == 0:
                p.copy("ve", amx, ab)
            else:
                p.tt("ve", amx, amx, ab, ALU.max)
    sclb, invb = global_amax_allreduce(amx, 0)
    for s in range(NSLAB):
        p.dma("sy", g.XA.v3(0, KD, SLAB), g.xnf.slab3(0, KD, s))
        for c in range(KD):
            xc = g.XA.v(c * SLAB, SLAB)
            eng = "ve" if c % 2 else "ge"
            tmp_f = g.XB.v((c % 4) * SLAB, SLAB)
            tmp_f2 = g.XB.v(2048 + (c % 4) * SLAB, SLAB)
            tmp_b = g.BB.v((c % 4) * SLAB, SLAB)
            quantize_tile(g, eng, xc, sclb, invb, -32768.0, 32767.0,
                          tmp_f, tmp_f2, tmp_b)
        p.dma("se", g.xsimf.slab3(0, KD, s), g.XA.v3(0, KD, SLAB))

    wamx = g.XB.v(0, DFF)
    for c in range(KD):
        wc = g.XA.v((c % 2) * DFF, DFF)
        p.dma("sy", wc, g.l0_win.v(slice(c * 128, c * 128 + 128),
                                   slice(0, DFF), key=("w", c)))
        ab = g.XB.v(DFF, DFF)
        dve_abs(g, "ve", ab, wc, g.YA.v(0, DFF))
        if c == 0:
            p.copy("ve", wamx, ab)
        else:
            p.tt("ve", wamx, wamx, ab, ALU.max)
    wtmp = g.YA.v(0, DFF)
    arow = fold_partitions(g, wamx, wtmp)
    p.ts("ve", arow, arow, 1e-8, None, ALU.max)
    srow = g.XB.v(DFF, DFF, 0, 1)
    p.recip(srow, arow)
    p.ts("ve", srow, srow, 127.0, None, ALU.mult)
    irow = g.YA.v(0, DFF, 0, 1)
    p.recip(irow, srow)
    dstS = g.rows.v(slice(3, 4), slice(0, DFF), key=(3, 0))
    p.dma("se", dstS, srow)
    dstI = g.rows.v(slice(4, 5), slice(0, DFF), key=(4, 0))
    p.dma("se", dstI, irow)
    sclwb = g.XB.v(0, DFF)
    invwb = g.XB.v(DFF, DFF)
    p.dma("sy", sclwb, V(dstS.ap.broadcast_to([128, DFF]), dstS.keys))
    p.dma("sy", invwb, V(dstI.ap.broadcast_to([128, DFF]), dstI.keys))
    for c in range(KD):
        wc = g.XA.v((c % 2) * DFF, DFF)
        p.dma("sy", wc, g.l0_win.v(slice(c * 128, c * 128 + 128),
                                   slice(0, DFF), key=("w", c)))
        eng = "ve" if c % 2 else "ge"
        p.tt(eng, wc, wc, sclwb, ALU.mult)
        tmp_f = g.YA.v(0, DFF)
        tmp_f2 = g.WB.v(4096, DFF)
        tmp_b = g.BB.v(0, DFF)
        round_even(g, eng, wc, tmp_f, tmp_f2, tmp_b)
        p.ts(eng, wc, wc, -128.0, 127.0, ALU.max, ALU.min)
        p.tt(eng, wc, wc, invwb, ALU.mult)
        p.dma("se", g.w_in_sim.v(slice(c * 128, c * 128 + 128),
                                 slice(0, DFF), key=("w", c)), wc)

    amx2 = g.WB.v(0, SLAB)
    first = [True]
    for fg in range(4):
        wg = []
        for c in range(KD):
            wgc = g.WA.v(c * D, D)
            p.dma("sy", wgc, g.w_in_sim.v(slice(c * 128, c * 128 + 128),
                                          slice(fg * D, (fg + 1) * D),
                                          key=("w", c)))
            wg.append(wgc)
        for s in range(NSLAB):
            p.dma("sy", g.XA.v3(0, KD, SLAB), g.xsimf.slab3(0, KD, s))
            xs = [g.XA.v(c * SLAB, SLAB) for c in range(KD)]
            for jf in range(KD):
                ffc = fg * 8 + jf
                psb = g.ps[jf]
                for c in range(KD):
                    p.mm(psb.v(), V(wg[c].ap[:, jf * 128:(jf + 1) * 128],
                                    wg[c].keys),
                         xs[c], start=(c == 0), stop=(c == KD - 1))
                hc = g.XB.v(jf * SLAB, SLAB)
                p.act(hc, psb.v(), AF.Gelu,
                      bias=V(g.sb_bin.ap[:, ffc:ffc + 1], g.sb_bin.keys))
                ab2 = g.WB.v(512, SLAB)
                dve_abs(g, "ve", ab2, hc, g.WB.v(1024, SLAB))
                if first[0]:
                    p.copy("ve", amx2, ab2)
                    first[0] = False
                else:
                    p.tt("ve", amx2, amx2, ab2, ALU.max)
            p.dma("se", g.hidf.slab3(fg * KD, KD, s), g.XB.v3(0, KD, SLAB))
    scl2, inv2 = global_amax_allreduce(amx2, 1)

    for c in range(4 * KD):
        wc = g.XA.v((c % 4) * D, D)
        p.dma("sy", wc, g.l0_wout.v(slice(c * 128, c * 128 + 128),
                                    slice(0, D), key=("w", c)))
        am = g.SM.v(704, 1)
        p.red(am, wc, ALU.max, absval=True)
        p.ts("ve", am, am, 1e-8, None, ALU.max)
        sc = g.SM.v(712, 1)
        iv = g.SM.v(720, 1)
        p.recip(sc, am)
        p.ts("ve", sc, sc, 127.0, None, ALU.mult)
        p.recip(iv, sc)
        eng = "ve" if c % 2 else "ge"
        p.ts(eng, wc, wc, sc, None, ALU.mult)
        tmp_f = g.XB.v((c % 4) * D, D)
        tmp_f2 = g.XB.v(4096 + (c % 4) * D, D)
        tmp_b = g.BB.v((c % 2) * D, D)
        round_even(g, eng, wc, tmp_f, tmp_f2, tmp_b)
        p.ts(eng, wc, wc, -128.0, 127.0, ALU.max, ALU.min)
        p.ts(eng, wc, wc, iv, None, ALU.mult)
        p.dma("se", g.w_out_sim.v(slice(c * 128, c * 128 + 128),
                                  slice(0, D), key=("w", c)), wc)

    for s in range(NSLAB):
        p.dma("sy", g.WA.v3(0, 4 * KD, SLAB), g.hidf.slab3(0, 4 * KD, s))
        hs = []
        for k in range(4 * KD):
            hc = g.WA.v(k * SLAB, SLAB)
            eng = "ve" if k % 2 else "ge"
            tmp_f = g.XB.v((k % 8) * SLAB, SLAB)
            tmp_f2 = g.XB.v(4096 + (k % 8) * SLAB, SLAB)
            tmp_b = g.BB.v((k % 8) * SLAB, SLAB)
            quantize_tile(g, eng, hc, scl2, inv2, -32768.0, 32767.0,
                          tmp_f, tmp_f2, tmp_b)
            hs.append(hc)
        for k in range(4 * KD):
            wk = g.XA.v((k % 2) * D, D)
            p.dma("sy", wk, g.w_out_sim.v(slice(k * 128, k * 128 + 128),
                                          slice(0, D), key=("w", k)))
            for j in range(KD):
                p.mm(g.ps[j].v(), V(wk.ap[:, j * 128:(j + 1) * 128], wk.keys),
                     hs[k], start=(k == 0), stop=(k == 4 * KD - 1))
        for j in range(KD):
            ot = g.XB.v(4096 + j * SLAB, SLAB)
            p.act(ot, g.ps[j].v(), AF.Silu,
                  bias=V(g.sb_bout.ap[:, j:j + 1], g.sb_bout.keys))
        p.dma("se", g.gf.slab3(0, KD, s), g.XB.v3(4096, KD, SLAB))



def emit_proj(g, xin, wd, wrow0, nkc, njc, outd, evac, in_keys=None):
    p = g.p
    wch = []
    ncols = njc * 128
    for c in range(nkc):
        vdst = g.WA.v(c * ncols, ncols)
        p.dma("sy", vdst, wd.v(slice(wrow0 + c * 128, wrow0 + (c + 1) * 128),
                               slice(0, ncols), key=("w", wrow0, c)))
        wch.append(vdst)
    for s in range(NSLAB):
        half = (s % 2) * (nkc * SLAB) if nkc <= 8 else 0
        p.dma("sy" if s % 2 else "se", g.XA.v3(half, nkc, SLAB),
              xin.slab3(0, nkc, s))
        xs = [g.XA.v(half + c * SLAB, SLAB) for c in range(nkc)]
        for j in range(njc):
            psb = g.ps[j % 4]
            for c in range(nkc):
                p.mm(psb.v(), V(wch[c].ap[:, j * 128:(j + 1) * 128], wch[c].keys),
                     xs[c], start=(c == 0), stop=(c == nkc - 1))
            evac(j, psb.v(), g.XB.v(j * SLAB, SLAB))
        p.dma("se" if s % 2 else "ge", outd.slab3(0, njc, s), g.XB.v3(0, njc, SLAB))


def ev_act(g, func, bias_sb, scale=1.0):
    def f(j, ps, dst):
        g.p.act(dst, ps, func, bias=V(bias_sb.ap[:, j:j + 1], bias_sb.keys),
                scale=scale)
    return f


def ev_softplus(g, bias_sb):
    def f(j, ps, dst):
        p = g.p
        pre = g.YA.v(0, SLAB)
        ab = g.YA.v(512, SLAB)
        lg = g.YA.v(1024, SLAB)
        p.act(pre, ps, AF.Identity, bias=V(bias_sb.ap[:, j:j + 1], bias_sb.keys))
        dve_abs(g, "ve", ab, pre, g.YA.v(1536, SLAB))
        p.act(ab, ab, AF.Exp, scale=-1.0)
        p.act(lg, ab, AF.Ln, bias=ccv(g, "one"))
        p.ts("ve", dst, pre, 0.0, None, ALU.max)
        p.tt("ve", dst, dst, lg, ALU.add)
    return f


def emit_reverse(g, srcd, dstd):
    p = g.p
    for s in range(NSLAB):
        p.dma("sy", g.XA.v3(0, KD, SLAB), srcd.slab3(0, KD, s))
        for c in range(KD):
            xc = g.XA.v(c * SLAB, SLAB)
            t = g.XB.v(c * SLAB, SLAB)
            p.copy("ve" if c % 2 else "ge", t, V(xc.ap[:, ::-1], xc.keys))
        p.dma("se", dstd.slab3(0, KD, NSLAB - 1 - s), g.XB.v3(0, KD, SLAB))


def emit_l1_attn(g):
    p = g.p
    g.sb_l1g = g.SM.v(256, 8)
    g.sb_l1b = g.SM.v(264, 8)
    p.dma("sy", g.sb_l1g, g.l1a_g.whole())
    p.dma("sy", g.sb_l1b, g.l1a_b.whole())
    emit_reverse(g, g.xnf, g.sc0)
    hfin = {}
    for d_ in ("fwd", "bwd"):
        bz_sb = g.SM.v(272, 32)
        bh_sb = g.SM.v(304, 32)
        p.dma("sy", bz_sb, getattr(g, f"l1a_bz_{d_}").whole())
        p.dma("sy", bh_sb, getattr(g, f"l1a_bh_{d_}").whole())
        hcur = g.xnf if d_ == "fwd" else g.sc0
        ping = [g.sc1, g.sc2] if d_ == "fwd" else [g.sc3, g.sc4]
        wz = getattr(g, f"l1a_wz_{d_}")
        wh = getattr(g, f"l1a_wh_{d_}")
        for i in range(4):
            emit_proj(g, hcur, wz, i * D, KD, KD, g.sc5,
                      ev_act(g, AF.Sigmoid, V(bz_sb.ap[:, i * 8:(i + 1) * 8],
                                              bz_sb.keys)))
            emit_proj(g, hcur, wh, i * D, KD, KD, g.sc6,
                      ev_act(g, AF.Tanh, V(bh_sb.ap[:, i * 8:(i + 1) * 8],
                                           bh_sb.keys)))
            for c in range(KD):
                zc = g.XA.v(0, T)
                hc = g.XA.v(4096, T)
                p.dma("sy", zc, g.sc5.v(slice(c * 128, (c + 1) * 128), slice(0, T),
                                        key=[(c, s) for s in range(NSLAB)]))
                p.dma("se", hc, g.sc6.v(slice(c * 128, (c + 1) * 128), slice(0, T),
                                        key=[(c, s) for s in range(NSLAB)]))
                p.tt("ve", hc, zc, hc, ALU.mult)
                p.act(zc, zc, AF.Identity, bias=ccv(g, "one"), scale=-1.0)
                p.ts("ve", zc, zc, 1e-8, 1.0 - 1e-8, ALU.max, ALU.min)
                sc = g.XB.v(0, T)
                p.scan(sc, zc, hc, 0.0)
                p.dma("ge", g.sc7.v(slice(c * 128, (c + 1) * 128), slice(0, T),
                                    key=[(c, s) for s in range(NSLAB)]), sc)
            nxt = ping[i % 2]
            emit_ln(g, g.sc7, nxt, gamma=g.sb_l1g, beta=g.sb_l1b,
                    add=(hcur if i > 0 else None))
            hcur = nxt
        hfin[d_] = hcur
    emit_reverse(g, hfin["bwd"], g.sc0)
    g.sb_fb = g.SM.v(272, 8)
    p.dma("sy", g.sb_fb, g.l1a_fb.whole())
    wch = []
    for c in range(16):
        vdst = g.WA.v(c * D, D)
        p.dma("sy", vdst, g.l1a_fusionT.v(slice(c * 128, (c + 1) * 128),
                                          slice(0, D), key=("w", c)))
        wch.append(vdst)
    for s in range(NSLAB):
        p.dma("sy", g.XA.v3(0, KD, SLAB), hfin["fwd"].slab3(0, KD, s))
        p.dma("se", g.XA.v3(4096, KD, SLAB), g.sc0.slab3(0, KD, s))
        xs = [g.XA.v(c * SLAB, SLAB) for c in range(16)]
        for j in range(KD):
            psb = g.ps[j % 4]
            for c in range(16):
                p.mm(psb.v(), V(wch[c].ap[:, j * 128:(j + 1) * 128], wch[c].keys),
                     xs[c], start=(c == 0), stop=(c == 15))
            p.act(g.XB.v(j * SLAB, SLAB), psb.v(), AF.Identity,
                  bias=V(g.sb_fb.ap[:, j:j + 1], g.sb_fb.keys))
        p.dma("ge", g.sf.slab3(0, KD, s), g.XB.v3(0, KD, SLAB))


def emit_l1_ffn(g):
    p = g.p
    sl = g.XB.v(0, D, 0, 64)
    p.dma("sy", sl, g.l1f_slot.whole())
    tmp = g.XB.v(1024, D)
    msum = fold_partitions(g, g.XB.v(0, D), tmp, alu=ALU.add, width=64)
    p.ts("ve", msum, msum, 1.0 / 64.0, None, ALU.mult)
    dstm = g.rows.v(slice(5, 6), slice(0, D), key=(5, 0))
    p.dma("se", dstm, msum)
    mcol = g.SM.v(336, 8)
    p.dma("sy", mcol, V(dstm.ap.rearrange("o (c p) -> (o p) c", p=128), dstm.keys),
          allow_slow_non_contiguous=True)
    hb = g.SM.v(344, 4)
    b1sb = g.SM.v(352, 4)
    p.dma("sy", b1sb, g.l1f_b1.whole())
    for jc in range(4):
        psb = g.ps[4]
        for c in range(KD):
            wv_ = g.XB.v(2048 + c * 128, 128)
            p.dma("sy", wv_, g.l1f_w1b.v(slice(c * 128, (c + 1) * 128),
                                         slice(jc * 128, (jc + 1) * 128),
                                         key=("w", c, jc)))
            p.mm(psb.v(0, 1), wv_, V(mcol.ap[:, c:c + 1], mcol.keys),
                 start=(c == 0), stop=(c == KD - 1))
        p.act(V(hb.ap[:, jc:jc + 1], hb.keys), psb.v(0, 1), AF.Identity,
              bias=V(b1sb.ap[:, jc:jc + 1], b1sb.keys))
    w1a_sb = []
    for c in range(KD):
        vdst = g.XB.v(4096 + c * 512, 512)
        p.dma("sy", vdst, g.l1f_w1a.v(slice(c * 128, (c + 1) * 128),
                                      slice(0, 512), key=("w", c)))
        w1a_sb.append(vdst)
    for s in range(NSLAB):
        p.dma("sy", g.XA.v3(0, KD, SLAB), g.xnf.slab3(0, KD, s))
        xs = [g.XA.v(c * SLAB, SLAB) for c in range(KD)]
        for j in range(4):
            psb = g.ps[j % 2]
            for c in range(KD):
                p.mm(psb.v(), V(w1a_sb[c].ap[:, j * 128:(j + 1) * 128],
                                w1a_sb[c].keys),
                     xs[c], start=(c == 0), stop=(c == KD - 1))
            p.act(g.WA.v(j * T + s * SLAB, SLAB), psb.v(), AF.Gelu,
                  bias=V(hb.ap[:, j:j + 1], hb.keys))
    w2sb = g.SM.v(360, 256, 0, 128)
    w2sb = g.SM.v(360, 256)
    for c in range(4):
        p.dma("sy", V(w2sb.ap[:, c * 64:(c + 1) * 64], w2sb.keys),
              g.l1f_w2.v(slice(c * 128, (c + 1) * 128), slice(0, 64),
                         key=("w", c)))
    b2b = g.SM.v(616, 64)
    b2w = g.l1f_b2row.whole()
    p.dma("sy", b2b, V(b2w.ap.broadcast_to([128, 64]), b2w.keys))
    tausb = g.SM.v(680, 1, 0, 1)
    p.dma("sy", tausb, g.l1f_tau.whole())
    dve_abs(g, "ve", tausb, tausb, g.SM.v(760, 1, 0, 1))
    p.ts("ve", tausb, tausb, 0.1, None, ALU.add)
    rtau1 = g.SM.v(688, 1, 0, 1)
    p.recip(rtau1, tausb)
    rtaub = g.SM.v(696, 1)
    bounce_rows(g, rtau1, 5, 2048, 1, rtaub)
    ps_si = [g.ps[5], g.ps[6]]
    ps_dn = g.ps[7]
    for tt_ in range(NTT):
        p.dma("sy", g.XA.v3(0, KD, 128),
              g.xnf.slab3(0, KD, tt_, ns=128, key_s=tt_ // 4))
        pst = g.ps[0]
        xnt = g.XB.v(0, 1024)
        for c in range(KD):
            p.transp(pst.v(c % 4 * 128, 128), g.XA.v(c * 128, 128), g.sb_ident)
            p.copy("se", V(xnt.ap[:, c * 128:(c + 1) * 128], xnt.keys),
                   pst.v(c % 4 * 128, 128))
        ps_lg = g.ps[1]
        for c in range(4):
            hch = g.WA.v(c * T + tt_ * 128, 128)
            p.mm(ps_lg.v(0, 64), hch,
                 V(w2sb.ap[:, c * 64:(c + 1) * 64], w2sb.keys),
                 start=(c == 0), stop=(c == 3))
        lg = g.XB.v(1024, 64)
        p.tt("ve", lg, ps_lg.v(0, 64), b2b, ALU.add)
        p.ts("ve", lg, lg, rtaub, None, ALU.mult)
        mx8 = g.SM.v(704, 8)
        p.max8(mx8, lg)
        nm1 = g.SM.v(712, 1)
        p.ts("ve", nm1, V(mx8.ap[:, 0:1], mx8.keys), -1.0, None, ALU.mult)
        sel = g.XB.v(1088, 64)
        p.ts("ve", sel, lg, V(mx8.ap[:, 7:8], mx8.keys), None, ALU.is_ge)
        ex = g.XB.v(1152, 64)
        p.act(ex, lg, AF.Exp, bias=nm1)
        p.tt("ve", ex, ex, sel, ALU.mult)
        ssum = g.SM.v(720, 1)
        p.red(ssum, ex, ALU.add)
        rsum = g.SM.v(728, 1)
        p.recip(rsum, ssum)
        alph = g.XB.v(1216, 64)
        p.ts("ve", alph, ex, rsum, None, ALU.mult)
        p.dma("se", g.alpha_t.v(slice(tt_ * 128, (tt_ + 1) * 128), slice(0, 64),
                                key=(tt_,)), alph)
        ps_at = g.ps[2]
        p.transp(ps_at.v(0, 128, 0, 64), alph, g.sb_ident)
        p.copy("se", V(g.YA.v(tt_ * 128, 128).ap[0:64, :], g.YA.v(tt_ * 128, 128).keys),
               ps_at.v(0, 128, 0, 64))
        for half in range(2):
            p.mm(ps_si[half].v(0, 512, 0, 64), alph,
                 V(xnt.ap[:, half * 512:(half + 1) * 512], xnt.keys),
                 start=(tt_ == 0), stop=(tt_ == NTT - 1))
        p.mm(ps_dn.v(0, 1, 0, 64), alph, g.sb_ones,
             start=(tt_ == 0), stop=(tt_ == NTT - 1))
    ss = g.XB.v(0, 1024, 0, 64)
    p.copy("se", V(ss.ap[:, 0:512], ss.keys), ps_si[0].v(0, 512, 0, 64))
    p.copy("se", V(ss.ap[:, 512:1024], ss.keys), ps_si[1].v(0, 512, 0, 64))
    dcol = g.SM.v(736, 1, 0, 64)
    p.act(dcol, ps_dn.v(0, 1, 0, 64), AF.Identity, bias=1e-8 if False else 0.0)
    p.ts("ve", dcol, dcol, 1e-8, None, ALU.add)
    rd = g.SM.v(744, 1, 0, 64)
    p.recip(rd, dcol)
    p.ts("ve", ss, ss, rd, None, ALU.mult)
    siT = g.XB.v(1024, 512)
    slT = g.XB.v(1536, 512)
    slin = g.XB.v(2048, 1024, 0, 64)
    p.dma("sy", slin, g.l1f_slot.whole())
    pst = g.ps[0]
    for c in range(KD):
        p.transp(pst.v(0, 64), V(ss.ap[:, c * 128:(c + 1) * 128], ss.keys),
                 ident64(g))
        p.copy("se", V(siT.ap[:, c * 64:(c + 1) * 64], siT.keys),
               pst.v(0, 64))
        p.transp(pst.v(64, 64),
                 V(slin.ap[:, c * 128:(c + 1) * 128], slin.keys), ident64(g))
        p.copy("se", V(slT.ap[:, c * 64:(c + 1) * 64], slT.keys),
               pst.v(64, 64))
    bih_sb = g.SM.v(752, 24)
    bhh_sb = g.SM.v(776, 24)
    p.dma("sy", bih_sb, g.l1f_bih.whole())
    p.dma("sy", bhh_sb, g.l1f_bhh.whole())
    gi = g.XB.v(2048 + 1024, 24 * 64)
    gh = g.XB.v(2048 + 1024 + 24 * 64, 24 * 64)
    for src_sb, wdr, bias, dst in ((siT, g.l1f_wihT, bih_sb, gi),
                                   (slT, g.l1f_whhT, bhh_sb, gh)):
        for m in range(24):
            psb = g.ps[1 + (m % 2)]
            for c in range(KD):
                wv_ = g.XA.v((c % 4) * 128, 128)
                p.dma("sy", wv_, wdr.v(slice(c * 128, (c + 1) * 128),
                                       slice(m * 128, (m + 1) * 128),
                                       key=("w", c, m)))
                p.mm(psb.v(0, 64), wv_,
                     V(src_sb.ap[:, c * 64:(c + 1) * 64], src_sb.keys),
                     start=(c == 0), stop=(c == KD - 1))
            p.act(V(dst.ap[:, m * 64:(m + 1) * 64], dst.keys), psb.v(0, 64),
                  AF.Identity, bias=V(bias.ap[:, m:m + 1], bias.keys))
    Snew = g.YA.v(0, 512)
    Snew = g.XB.v(2048 + 1024 + 48 * 64, 512)
    for c in range(KD):
        ir = V(gi.ap[:, c * 64:(c + 1) * 64], gi.keys)
        iz = V(gi.ap[:, (8 + c) * 64:(9 + c) * 64], gi.keys)
        inn = V(gi.ap[:, (16 + c) * 64:(17 + c) * 64], gi.keys)
        hr = V(gh.ap[:, c * 64:(c + 1) * 64], gh.keys)
        hz = V(gh.ap[:, (8 + c) * 64:(9 + c) * 64], gh.keys)
        hn = V(gh.ap[:, (16 + c) * 64:(17 + c) * 64], gh.keys)
        r_ = g.SM.v(800, 64)
        z_ = g.SM.v(864, 64)
        n_ = g.SM.v(928, 64)
        p.tt("ve", r_, ir, hr, ALU.add)
        p.act(r_, r_, AF.Sigmoid)
        p.tt("ve", z_, iz, hz, ALU.add)
        p.act(z_, z_, AF.Sigmoid)
        p.tt("ve", n_, r_, hn, ALU.mult)
        p.tt("ve", n_, n_, inn, ALU.add)
        p.act(n_, n_, AF.Tanh)
        sc_ = V(slT.ap[:, c * 64:(c + 1) * 64], slT.keys)
        zs = V(Snew.ap[:, c * 64:(c + 1) * 64], Snew.keys)
        p.tt("ve", r_, z_, sc_, ALU.mult)
        p.act(z_, z_, AF.Identity, bias=ccv(g, "one"), scale=-1.0)
        p.tt("ve", z_, z_, n_, ALU.mult)
        p.tt("ve", zs, z_, r_, ALU.add)
    bq_sb = g.SM.v(992, 2); bk_sb = g.SM.v(1000, 2)
    p.dma("sy", bq_sb, g.l1f_bq.whole())
    p.dma("sy", bk_sb, g.l1f_bk.whole())
    qf = g.SM.v(800, 128)
    kf = g.SM.v(0, 0) if False else None
    kf = g.XB.v(2048 + 1024 + 48 * 64 + 512, 128)
    for wdr, bsb, dst in ((g.l1f_wq, bq_sb, qf), (g.l1f_wk, bk_sb, kf)):
        for m in range(2):
            psb = g.ps[3]
            for c in range(KD):
                wv_ = g.XA.v(4096 + (c % 4) * 128, 128)
                p.dma("sy", wv_, wdr.v(slice(c * 128, (c + 1) * 128),
                                       slice(m * 128, (m + 1) * 128),
                                       key=("w", c, m)))
                p.mm(psb.v(0, 64), wv_,
                     V(Snew.ap[:, c * 64:(c + 1) * 64], Snew.keys),
                     start=(c == 0), stop=(c == KD - 1))
            p.act(V(dst.ap[:, m * 64:(m + 1) * 64], dst.keys), psb.v(0, 64),
                  AF.Identity, bias=V(bsb.ap[:, m:m + 1], bsb.keys))
    ps_sc = g.ps[4]
    for m in range(2):
        p.mm(ps_sc.v(0, 64, 0, 64), V(qf.ap[:, m * 64:(m + 1) * 64], qf.keys),
             V(kf.ap[:, m * 64:(m + 1) * 64], kf.keys),
             start=(m == 0), stop=(m == 1))
    att = g.SM.v(864, 64, 0, 64)
    p.act(att, ps_sc.v(0, 64, 0, 64), AF.Identity, scale=0.0625)
    nmx = g.SM.v(928, 1, 0, 64)
    p.red(nmx, att, ALU.max, negate=True)
    rs_ = g.SM.v(936, 1, 0, 64)
    p.act(att, att, AF.Exp, bias=nmx, accum=rs_)
    rr = g.SM.v(944, 1, 0, 64)
    p.recip(rr, rs_)
    p.ts("ve", att, att, rr, None, ALU.mult)
    ps_at2 = g.ps[5]
    p.transp(ps_at2.v(0, 64, 0, 64), att, ident64(g))
    attT = g.SM.v(864, 64, 0, 64)
    p.copy("se", attT, ps_at2.v(0, 64, 0, 64))
    vsm = g.XB.v(0, 1024, 0, 64)
    bvw = g.l1f_bvrow.whole()
    bvb64 = g.XB.v(1024, 1024, 0, 64)
    p.dma("sy", bvb64, V(bvw.ap.broadcast_to([64, D]), bvw.keys))
    for half in range(2):
        psb = g.ps[6]
        for c in range(KD):
            wv_ = g.XA.v((c % 4) * 512, 512)
            p.dma("sy", wv_, g.l1f_wv.v(slice(c * 128, (c + 1) * 128),
                                        slice(half * 512, (half + 1) * 512),
                                        key=("w", c, half)))
            p.mm(psb.v(0, 512, 0, 64),
                 V(Snew.ap[:, c * 64:(c + 1) * 64], Snew.keys), wv_,
                 start=(c == 0), stop=(c == KD - 1))
        p.tt("ve", V(vsm.ap[:, half * 512:(half + 1) * 512], vsm.keys),
             psb.v(0, 512, 0, 64),
             V(bvb64.ap[:, half * 512:(half + 1) * 512], bvb64.keys), ALU.add)
    yv = g.XB.v(2048, 512)
    for c in range(KD):
        psb = g.ps[7]
        p.mm(psb.v(0, 64), V(vsm.ap[:, c * 128:(c + 1) * 128], vsm.keys),
             attT, start=True, stop=True)
        p.tt("ve", V(yv.ap[:, c * 64:(c + 1) * 64], yv.keys), psb.v(0, 64),
             V(Snew.ap[:, c * 64:(c + 1) * 64], Snew.keys), ALU.add)
    ps_s = g.ps[0]; ps_q = g.ps[1]
    sqt = g.SM.v(864, 64)
    for c in range(KD):
        yc = V(yv.ap[:, c * 64:(c + 1) * 64], yv.keys)
        p.mm(ps_s.v(0, 64, 0, 1), g.sb_ones, yc, start=(c == 0), stop=(c == KD - 1))
        p.tt("ve", sqt, yc, yc, ALU.mult)
        p.mm(ps_q.v(0, 64, 0, 1), g.sb_ones, sqt, start=(c == 0), stop=(c == KD - 1))
    mrow = g.SM.v(928, 64, 0, 1)
    vrow = g.SM.v(864, 64, 0, 1)
    p.ts("ve", mrow, ps_s.v(0, 64, 0, 1), 1.0 / D, None, ALU.mult)
    p.ts("ve", vrow, ps_q.v(0, 64, 0, 1), 1.0 / D, None, ALU.mult)
    m2row = g.SM.v(736, 64, 0, 1)
    p.tt("ve", m2row, mrow, mrow, ALU.mult)
    p.tt("ve", vrow, vrow, m2row, ALU.subtract)
    p.act(vrow, vrow, AF.Sqrt, bias=ccv(g, "eps_ln", 1))
    p.recip(vrow, vrow)
    mr2 = g.SM.v(744, 64, 0, 1)
    p.tt("ve", mr2, mrow, vrow, ALU.mult)
    rb = g.XB.v(1024, 64)
    mb = g.XB.v(1088, 64)
    bounce_rows(g, vrow, 5, 1024, 64, rb)
    bounce_rows(g, mr2, 5, 1152, 64, mb)
    lng_sb = g.SM.v(736, 8); lnb_sb = g.SM.v(752, 8)
    p.dma("sy", lng_sb, g.l1f_lng.whole())
    p.dma("sy", lnb_sb, g.l1f_lnb.whole())
    rbf = V(rb.ap[0:128, :], rb.keys)
    mbf = V(mb.ap[0:128, :], mb.keys)
    for c in range(KD):
        yc = V(yv.ap[:, c * 64:(c + 1) * 64], yv.keys)
        p.tt("ve", yc, yc, rbf, ALU.mult)
        p.tt("ve", yc, yc, mbf, ALU.subtract)
        p.ts("ve", yc, yc, V(lng_sb.ap[:, c:c + 1], lng_sb.keys),
             V(lnb_sb.ap[:, c:c + 1], lnb_sb.keys), ALU.mult, ALU.add)
    if DEV_DUMP:
        p.dma("se", g.sc7.v(slice(0, 64), slice(0, 1024), key=("d", 0)), ss)
        p.dma("se", g.sc7.v(slice(128, 256), slice(0, 512), key=("d", 1)), siT)
        p.dma("se", g.sc7.v(slice(256, 384), slice(0, 1024), key=("d", 2)),
              V(gi.ap[:, 0:1024], gi.keys))
        p.dma("se", g.sc7.v(slice(384, 512), slice(0, 1024), key=("d", 3)),
              V(gh.ap[:, 0:1024], gh.keys))
        p.dma("se", g.sc7.v(slice(512, 640), slice(0, 512), key=("d", 4)), Snew)
        p.dma("se", g.sc7.v(slice(640, 704), slice(0, 1024), key=("d", 5)), vsm)
        p.dma("se", g.sc7.v(slice(704, 768), slice(0, 64), key=("d", 6)), att)
        p.dma("se", g.sc7.v(slice(768, 896), slice(0, 512), key=("d", 7)), yv)
    svm = g.XB.v(2560, 1024, 0, 64)
    bvpw = g.l1f_bvprow.whole()
    bvpb = g.XB.v(3584, 1024, 0, 64)
    p.dma("sy", bvpb, V(bvpw.ap.broadcast_to([64, D]), bvpw.keys))
    for half in range(2):
        psb = g.ps[2]
        for c in range(KD):
            wv_ = g.XA.v((c % 4) * 512, 512)
            p.dma("sy", wv_, g.l1f_wvp.v(slice(c * 128, (c + 1) * 128),
                                         slice(half * 512, (half + 1) * 512),
                                         key=("w", c, half)))
            p.mm(psb.v(0, 512, 0, 64),
                 V(yv.ap[:, c * 64:(c + 1) * 64], yv.keys), wv_,
                 start=(c == 0), stop=(c == KD - 1))
        p.tt("ve", V(svm.ap[:, half * 512:(half + 1) * 512], svm.keys),
             psb.v(0, 512, 0, 64),
             V(bvpb.ap[:, half * 512:(half + 1) * 512], bvpb.keys), ALU.add)
    if DEV_DUMP:
        p.dma("se", g.sc7.v(slice(896, 960), slice(0, 1024), key=("d", 8)), svm)
        p.dma("se", g.sc7.v(slice(960, 1024), slice(0, T), key=("d", 9)),
              V(g.YA.v(0, T).ap[0:64, :], g.YA.v(0, T).keys))
    g.sb_l1fbo = g.SM.v(736, 8)
    p.dma("sy", g.sb_l1fbo, g.l1f_bo.whole())
    wo_sb = []
    for c in range(KD):
        vdst = g.WA.v(c * D, D)
        p.dma("sy", vdst, g.l1f_wo.v(slice(c * 128, (c + 1) * 128),
                                     slice(0, D), key=("w", c)))
        wo_sb.append(vdst)
    for s in range(NSLAB):
        of = []
        for c in range(KD):
            psb = g.ps[3 + (c % 2)]
            p.mm(psb.v(),
                 V(svm.ap[:, c * 128:(c + 1) * 128], svm.keys),
                 V(g.YA.v(s * SLAB, SLAB).ap[0:64, :], g.YA.v(s * SLAB, SLAB).keys),
                 start=True, stop=True)
            oc = g.XA.v(c * SLAB, SLAB)
            p.copy("se", oc, psb.v())
            of.append(oc)
        for j in range(KD):
            psb = g.ps[5 + (j % 2)]
            for c in range(KD):
                p.mm(psb.v(), V(wo_sb[c].ap[:, j * 128:(j + 1) * 128],
                                wo_sb[c].keys),
                     of[c], start=(c == 0), stop=(c == KD - 1))
            p.act(g.XB.v(4096 + j * SLAB, SLAB), psb.v(), AF.Silu,
                  bias=V(g.sb_l1fbo.ap[:, j:j + 1], g.sb_l1fbo.keys))
        p.dma("se", g.gf.slab3(0, KD, s), g.XB.v3(4096, KD, SLAB))



def ev_copy(g):
    def f(j, ps, dst):
        g.p.copy("se", dst, ps)
    return f


def emit_addpass(g, a, b, outd, eng_q="se"):
    p = g.p
    for s in range(NSLAB):
        p.dma("sy", g.XA.v3(0, KD, SLAB), a.slab3(0, KD, s))
        p.dma("ge", g.XA.v3(4096, KD, SLAB), b.slab3(0, KD, s))
        for c in range(KD):
            p.tt("ve" if c % 2 else "ge", g.XA.v(c * SLAB, SLAB),
                 g.XA.v(c * SLAB, SLAB), g.XA.v(4096 + c * SLAB, SLAB), ALU.add)
        p.dma(eng_q, outd.slab3(0, KD, s), g.XA.v3(0, KD, SLAB))


def emit_l2_attn(g):
    p = g.p
    ln1g = g.SM.v(256, 8); ln1b = g.SM.v(264, 8)
    ln2g = g.SM.v(272, 8); ln2b = g.SM.v(280, 8)
    inb = g.SM.v(288, 16); dwb = g.SM.v(304, 8)
    dtb = g.SM.v(312, 8); bb_ = g.SM.v(320, 8); cb = g.SM.v(328, 8)
    alog = g.SM.v(336, 8); dskip = g.SM.v(344, 8); outb = g.SM.v(352, 8)
    f1b = g.SM.v(360, 16); f2b = g.SM.v(376, 8)
    dw0 = g.SM.v(384, 8); dw1 = g.SM.v(392, 8); dw2 = g.SM.v(400, 8)
    for sb, dr in ((ln1g, g.l2a_ln1g), (ln1b, g.l2a_ln1b), (ln2g, g.l2a_ln2g),
                   (ln2b, g.l2a_ln2b), (inb, g.l2a_inb), (dwb, g.l2a_dwb),
                   (dtb, g.l2a_dtb), (bb_, g.l2a_bb), (cb, g.l2a_cb),
                   (alog, g.l2a_alog), (dskip, g.l2a_dskip), (outb, g.l2a_outb),
                   (f1b, g.l2a_f1b), (f2b, g.l2a_f2b)):
        p.dma("sy", sb, dr.whole())
    for k3, sb in enumerate((dw0, dw1, dw2)):
        p.dma("sy", sb, g.l2a_dw.v(slice(k3, k3 + 1), key=k3))
    acol = g.SM.v(408, 8)
    p.act(acol, alog, AF.Exp)
    p.ts("ve", acol, acol, -1.0, None, ALU.mult)

    emit_ln(g, g.xnf, g.sc0, gamma=ln1g, beta=ln1b)
    emit_proj(g, g.sc0, g.l2a_inW, 0, KD, 16, g.hidf,
              ev_act(g, AF.Identity, inb))
    for c in range(KD):
        xp = g.XA.v((c % 2) * T, T)
        p.dma("sy", xp, g.hidf.v(slice(c * 128, (c + 1) * 128), slice(0, T),
                                 key=[(c, s) for s in range(NSLAB)]))
        mxc = g.XB.v((c % 2) * T, T)
        eng = "ve" if c % 2 else "ge"
        p.ts(eng, mxc, xp, V(dw1.ap[:, c:c + 1], dw1.keys),
             V(dwb.ap[:, c:c + 1], dwb.keys), ALU.mult, ALU.add)
        p.stt("ve", V(mxc.ap[:, 1:T], mxc.keys), V(xp.ap[:, 0:T - 1], xp.keys),
              V(dw0.ap[:, c:c + 1], dw0.keys), V(mxc.ap[:, 1:T], mxc.keys),
              ALU.mult, ALU.add)
        p.stt("ve", V(mxc.ap[:, 0:T - 1], mxc.keys), V(xp.ap[:, 1:T], xp.keys),
              V(dw2.ap[:, c:c + 1], dw2.keys), V(mxc.ap[:, 0:T - 1], mxc.keys),
              ALU.mult, ALU.add)
        p.dma("se", g.sc1.v(slice(c * 128, (c + 1) * 128), slice(0, T),
                            key=[(c, s) for s in range(NSLAB)]), mxc)
    emit_proj(g, g.sc1, g.l2a_dtW, 0, KD, KD, g.sc2, ev_softplus(g, dtb))
    emit_proj(g, g.sc1, g.l2a_bW, 0, KD, KD, g.sc3, ev_act(g, AF.Tanh, bb_))
    emit_proj(g, g.sc1, g.l2a_cW, 0, KD, KD, g.sc4, ev_act(g, AF.Tanh, cb))
    for c in range(KD):
        ks = [(c, s) for s in range(NSLAB)]
        dt = g.XA.v(0, T); bt = g.XA.v(4096, T)
        ct = g.XB.v(0, T); mxc = g.XB.v(4096, T); gtc = g.YA.v(0, T)
        p.dma("sy", dt, g.sc2.v(slice(c * 128, (c + 1) * 128), slice(0, T), key=ks))
        p.dma("se", bt, g.sc3.v(slice(c * 128, (c + 1) * 128), slice(0, T), key=ks))
        p.dma("ge", ct, g.sc4.v(slice(c * 128, (c + 1) * 128), slice(0, T), key=ks))
        p.dma("sy", mxc, g.sc1.v(slice(c * 128, (c + 1) * 128), slice(0, T), key=ks))
        p.dma("se", gtc, g.hidf.v(slice((8 + c) * 128, (9 + c) * 128), slice(0, T),
                                  key=[(8 + c, s) for s in range(NSLAB)]))
        decay = g.WB.v(0, T)
        p.act(decay, dt, AF.Exp, scale=V(acol.ap[:, c:c + 1], acol.keys))
        p.tt("ve", bt, bt, mxc, ALU.mult)
        p.tt("ve", bt, dt, bt, ALU.mult)
        st_ = g.WB.v(4096, T)
        p.scan(st_, decay, bt, 0.0)
        p.tt("ve", ct, ct, st_, ALU.mult)
        p.stt("ve", ct, mxc, V(dskip.ap[:, c:c + 1], dskip.keys), ct,
              ALU.mult, ALU.add)
        p.act(gtc, gtc, AF.Sigmoid)
        p.tt("ve", ct, ct, gtc, ALU.mult)
        p.dma("ge", g.sc5.v(slice(c * 128, (c + 1) * 128), slice(0, T), key=ks), ct)
    emit_proj(g, g.sc5, g.l2a_outW, 0, KD, KD, g.sc7,
              ev_act(g, AF.Identity, outb))
    emit_addpass(g, g.sc7, g.xnf, g.sc6)
    emit_ln(g, g.sc6, g.sc0, gamma=ln2g, beta=ln2b)
    emit_proj(g, g.sc0, g.l2a_f1W, 0, KD, 16, g.hidf,
              ev_act(g, AF.Gelu, f1b))
    emit_proj(g, g.hidf, g.l2a_f2W, 0, 16, KD, g.sc7,
              ev_act(g, AF.Identity, f2b))
    emit_addpass(g, g.sc7, g.sc6, g.sf)


def emit_l2_ffn(g):
    p = g.p
    rs1 = g.SM.v(416, 1, 0, 1)
    p.dma("sy", rs1, g.l2f_rs.whole())
    rsb = g.SM.v(424, 1)
    bounce_rows(g, rs1, 5, 2052, 1, rsb)
    svsb = []
    sosb = []
    for c in range(KD):
        a1 = g.WB.v(c * 64, 64)
        p.dma("sy", a1, g.l2f_svT.v(slice(c * 128, (c + 1) * 128), slice(0, 64),
                                    key=("wv", c)))
        svsb.append(a1)
        a2 = g.WB.v(512 + c * 64, 64)
        p.dma("sy", a2, g.l2f_soT.v(slice(c * 128, (c + 1) * 128), slice(0, 64),
                                    key=("wo", c)))
        sosb.append(a2)
    for tt_ in range(NTT):
        p.dma("sy", g.XA.v3(0, KD, 128),
              g.xnf.slab3(0, KD, tt_, ns=128, key_s=tt_ // 4))
        psv = g.ps[0]; pso = g.ps[1]
        for c in range(KD):
            xc = g.XA.v(c * 128, 128)
            p.mm(psv.v(0, 64), xc, svsb[c], start=(c == 0), stop=(c == KD - 1))
            p.mm(pso.v(0, 64), xc, sosb[c], start=(c == 0), stop=(c == KD - 1))
        for pgate, outd in ((psv, g.gv_t), (pso, g.go_t)):
            pr = g.XB.v(4096, 64)
            p.act(pr, pgate.v(0, 64), AF.Sigmoid)
            gt = g.XB.v(4160, 64)
            for hh in range(8):
                prh = V(pr.ap[:, hh * 8:(hh + 1) * 8], pr.keys)
                mx8 = g.SM.v(520, 8)
                p.max8(mx8, prh)
                ssum = g.SM.v(528, 1)
                p.tt("ve", ssum, V(mx8.ap[:, 0:1], mx8.keys),
                     V(mx8.ap[:, 1:2], mx8.keys), ALU.add)
                p.ts("ve", ssum, ssum, 1e-9, None, ALU.max)
                rr = g.SM.v(536, 1)
                p.recip(rr, ssum)
                p.ts("ve", rr, rr, rsb, None, ALU.mult)
                gh = V(gt.ap[:, hh * 8:(hh + 1) * 8], gt.keys)
                p.ts("ve", gh, prh, V(mx8.ap[:, 1:2], mx8.keys), None, ALU.is_ge)
                p.tt("ve", gh, gh, prh, ALU.mult)
                p.ts("ve", gh, gh, rr, None, ALU.mult)
            p.dma("se", outd.v(slice(tt_ * 128, (tt_ + 1) * 128), slice(0, 64),
                               key=(tt_,)), gt)
    emit_proj(g, g.xnf, g.l2f_qT, 0, KD, KD, g.sc0, ev_copy(g))
    emit_proj(g, g.xnf, g.l2f_kT, 0, KD, KD, g.sc1, ev_copy(g))
    for hh in range(8):
        wch = []
        for c in range(KD):
            vdst = g.WA.v(c * 1024, 1024)
            p.dma("sy", vdst, g.l2f_vwT.v(slice(c * 128, (c + 1) * 128),
                                          slice(hh * 1024, (hh + 1) * 1024),
                                          key=("w", c, hh)))
            wch.append(vdst)
        for tt_ in range(NTT):
            p.dma("sy", g.XA.v3(0, KD, 128),
                  g.xnf.slab3(0, KD, tt_, ns=128, key_s=tt_ // 4))
            gvt = g.XB.v(4096, 64)
            p.dma("se", gvt, g.gv_t.v(slice(tt_ * 128, (tt_ + 1) * 128),
                                      slice(0, 64), key=(tt_,)))
            vacc = g.XB.v(4224, 128)
            for eh in range(2):
                psb = g.ps[2 + eh]
                for c in range(KD):
                    p.mm(psb.v(), g.XA.v(c * 128, 128),
                         V(wch[c].ap[:, eh * 512:(eh + 1) * 512], wch[c].keys),
                         start=(c == 0), stop=(c == KD - 1))
                for e4 in range(4):
                    e = eh * 4 + e4
                    gcol = V(gvt.ap[:, hh * 8 + e:hh * 8 + e + 1], gvt.keys)
                    if e == 0:
                        p.ts("ve", vacc, psb.v(e4 * 128, 128), gcol, None, ALU.mult)
                    else:
                        p.stt("ve", vacc, psb.v(e4 * 128, 128), gcol, vacc,
                              ALU.mult, ALU.add)
            p.dma("ge", g.vtok.v(slice(tt_ * 128, (tt_ + 1) * 128),
                                 slice(hh * 128, (hh + 1) * 128),
                                 key=(tt_, hh)), vacc)
    for hh in range(8):
        qh = g.XA.v(0, T)
        kh = g.XA.v(4096, T)
        vh = g.XB.v(0, T)
        p.dma("sy", qh, g.sc0.v(slice(hh * 128, (hh + 1) * 128), slice(0, T),
                                key=[(hh, s) for s in range(NSLAB)]))
        p.dma("sy", kh, g.sc1.v(slice(hh * 128, (hh + 1) * 128), slice(0, T),
                                key=[(hh, s) for s in range(NSLAB)]))
        vsrc = g.vtok.t.ap().rearrange("(a p) d -> p a d", p=128)
        vsrc = vsrc[:, :, hh * 128:(hh + 1) * 128]
        p.dma("sy", vh, V(vsrc, [("vtok", t2, hh) for t2 in range(NTT)]))
        owch = []
        for e in range(8):
            vdst = g.WB.v(e * 512, 512)
            p.dma("sy", vdst, g.l2f_ow.v(
                slice(hh * 1024 + e * 128, hh * 1024 + (e + 1) * 128),
                slice(0, 512), key=("ow", hh, e, 0)))
            owch.append(vdst)
        owch2 = []
        for e in range(8):
            vdst = g.WB.v(4096 + e * 512, 512)
            p.dma("sy", vdst, g.l2f_ow.v(
                slice(hh * 1024 + e * 128, hh * 1024 + (e + 1) * 128),
                slice(512, 1024), key=("ow", hh, e, 1)))
            owch2.append(vdst)
        for qt_ in range(NTT):
            acc = g.YA.v(0, 128)
            nm = [g.SM.v(512, 1), g.SM.v(520, 1)]
            lsum = g.SM.v(528, 1)
            alpha = g.SM.v(536, 1)
            rs_ = g.SM.v(544, 1)
            p.memset("ve", acc, 0.0)
            p.memset("ve", nm[0], 3.0e38)
            p.memset("ve", lsum, 0.0)
            for ks in range(NSLAB):
                pss = g.ps[0]
                p.mm(pss.v(), V(qh.ap[:, qt_ * 128:(qt_ + 1) * 128], qh.keys),
                     V(kh.ap[:, ks * SLAB:(ks + 1) * SLAB], kh.keys),
                     start=True, stop=True)
                nmo, nmn = nm[ks % 2], nm[(ks + 1) % 2]
                nmt = g.SM.v(552, 1)
                p.red(nmt, pss.v(), ALU.max, negate=True)
                p.tt("ve", nmn, nmo, nmt, ALU.min)
                p.act(alpha, nmo, AF.Exp, bias=nmn, scale=-1.0)
                p_sb = g.XB.v(4096, SLAB)
                p.act(p_sb, pss.v(), AF.Exp, bias=nmn, accum=rs_)
                p.stt("ve", lsum, lsum, alpha, rs_, ALU.mult, ALU.add)
                pst = g.ps[1]
                for i4 in range(4):
                    p.transp(pst.v(i4 * 128, 128),
                             V(p_sb.ap[:, i4 * 128:(i4 + 1) * 128], p_sb.keys),
                             g.sb_ident)
                pT = g.XB.v(4608, SLAB)
                p.copy("se", pT, pst.v())
                psv2 = g.ps[2]
                for i4 in range(4):
                    kk = ks * 4 + i4
                    p.mm(psv2.v(0, 128),
                         V(pT.ap[:, i4 * 128:(i4 + 1) * 128], pT.keys),
                         V(vh.ap[:, kk * 128:(kk + 1) * 128], vh.keys),
                         start=(i4 == 0), stop=(i4 == 3))
                p.stt("ve", acc, acc, alpha, psv2.v(0, 128), ALU.mult, ALU.add)
            linv = g.SM.v(560, 1)
            p.recip(linv, lsum)
            p.ts("ve", acc, acc, linv, None, ALU.mult)
            got = g.XB.v(5120, 64)
            p.dma("se", got, g.go_t.v(slice(qt_ * 128, (qt_ + 1) * 128),
                                      slice(0, 64), key=(qt_,)))
            gtile = g.XB.v(5248, 1024)
            wgtT8 = g.XB.v(6272, 1024)
            for e in range(8):
                wgt = g.YA.v(128, 128)
                p.ts("ve", wgt, acc,
                     V(got.ap[:, hh * 8 + e:hh * 8 + e + 1], got.keys),
                     None, ALU.mult)
                pst2 = g.ps[3]
                p.transp(pst2.v(0, 128), wgt, g.sb_ident)
                p.copy("se", V(wgtT8.ap[:, e * 128:(e + 1) * 128], wgtT8.keys),
                       pst2.v(0, 128))
            for e in range(8):
                wgtT = V(wgtT8.ap[:, e * 128:(e + 1) * 128], wgtT8.keys)
                for dc in range(8):
                    ow = owch[e] if dc < 4 else owch2[e]
                    sl = dc % 4
                    p.mm(g.ps[dc].v(0, 128),
                         V(ow.ap[:, sl * 128:(sl + 1) * 128], ow.keys),
                         wgtT, start=(e == 0), stop=(e == 7))
            for dc in range(8):
                p.copy("se", V(gtile.ap[:, dc * 128:(dc + 1) * 128], gtile.keys),
                       g.ps[dc].v(0, 128))
            dst = g.sc2.slab3(0, KD, qt_, ns=128, key_s=qt_ // 4)
            if hh == 0:
                p.dma("ge", dst, g.XB.v3(5248, KD, 128))
            else:
                p.dma("ge", dst, g.XB.v3(5248, KD, 128), accum_op=ALU.add)
    for s in range(NSLAB):
        p.dma("sy", g.XA.v3(0, KD, SLAB), g.sc2.slab3(0, KD, s))
        for c in range(KD):
            p.act(g.XA.v(c * SLAB, SLAB), g.XA.v(c * SLAB, SLAB), AF.Silu)
        p.dma("se", g.gf.slab3(0, KD, s), g.XA.v3(0, KD, SLAB))


def emit_l3_attn(g):
    p = g.p
    emit_proj(g, g.xnf, g.l3a_wqT, 0, KD, KD, g.sc0, ev_copy(g))
    wkd = g.XB.v(0, 512)
    wvd = g.XB.v(512, 512)
    for c in range(KD):
        p.dma("sy", V(wkd.ap[:, c * 64:(c + 1) * 64], wkd.keys),
              g.l3a_wkdT.v(slice(c * 128, (c + 1) * 128), slice(0, 64),
                           key=("w", c)))
        p.dma("sy", V(wvd.ap[:, c * 64:(c + 1) * 64], wvd.keys),
              g.l3a_wvdT.v(slice(c * 128, (c + 1) * 128), slice(0, 64),
                           key=("w", c)))
    for s in range(NSLAB):
        p.dma("sy", g.XA.v3(0, KD, SLAB), g.xnf.slab3(0, KD, s))
        psb = g.ps[0]
        for c in range(KD):
            p.mm(psb.v(0, SLAB, 0, 64),
                 V(wkd.ap[:, c * 64:(c + 1) * 64], wkd.keys),
                 g.XA.v(c * SLAB, SLAB), start=(c == 0), stop=(c == KD - 1))
        lkc = g.XB.v(1024, SLAB, 0, 64)
        p.copy("se", lkc, psb.v(0, SLAB, 0, 64))
        p.dma("se", g.sc1.v(slice(0, 64), slice(s * SLAB, (s + 1) * SLAB),
                            key=("lk", s)), lkc)
    for tt_ in range(NTT):
        p.dma("sy", g.XA.v3(0, KD, 128),
              g.xnf.slab3(0, KD, tt_, ns=128, key_s=tt_ // 4))
        psb = g.ps[1]
        for c in range(KD):
            p.mm(psb.v(0, 64), g.XA.v(c * 128, 128),
                 V(wvd.ap[:, c * 64:(c + 1) * 64], wvd.keys),
                 start=(c == 0), stop=(c == KD - 1))
        lvc = g.XB.v(1536, 64)
        p.copy("se", lvc, psb.v(0, 64))
        p.dma("se", g.lv_t.v(slice(tt_ * 128, (tt_ + 1) * 128), slice(0, 64),
                             key=(tt_,)), lvc)
    lk_sb = g.XB.v(2048, T, 0, 64)
    p.dma("sy", lk_sb, g.sc1.v(slice(0, 64), slice(0, T),
                               key=[("lk", s) for s in range(NSLAB)]))
    lv_sb = g.XB.v(0, T // 2)
    lvsrc = g.lv_t.t.ap().rearrange("(a p) l -> p a l", p=128)
    p.dma("sy", lv_sb, V(lvsrc, [("lv_t", t2) for t2 in range(NTT)]))
    mks = g.WB.v(0, 2048)
    dsrc = g.dmask.t.ap().transpose([1, 0, 2])
    p.dma("sy", V(mks.ap.rearrange("p (m c) -> p m c", c=512), mks.keys),
          V(dsrc, [("dmask",)]))
    for hh in range(16):
        qh = g.XA.v(0, T, 0, 64)
        p.dma("sy", qh, g.sc0.v(slice(hh * 64, (hh + 1) * 64), slice(0, T),
                                key=[(hh // 2, s) for s in range(NSLAB)]))
        q2l = g.SM.v(256, 64, 0, 64)
        p.dma("sy", q2l, g.l3a_q2l.v(slice(hh * 64, (hh + 1) * 64),
                                     slice(0, 64), key=("w", hh)))
        vup = g.SM.v(320, 64, 0, 64)
        p.dma("sy", vup, g.l3a_vup.v(slice(hh * 64, (hh + 1) * 64),
                                     slice(0, 64), key=("w", hh)))
        qlat = g.YA.v(0, T, 0, 64)
        for s in range(NSLAB):
            psb = g.ps[0]
            p.mm(psb.v(0, SLAB, 0, 64), q2l,
                 V(qh.ap[:, s * SLAB:(s + 1) * SLAB], qh.keys),
                 start=True, stop=True)
            p.copy("se", V(qlat.ap[:, s * SLAB:(s + 1) * SLAB], qlat.keys),
                   psb.v(0, SLAB, 0, 64))
        for qt_ in range(NTT):
            acc = g.XA.v(4096, 64)
            nm = [g.SM.v(512, 1), g.SM.v(520, 1)]
            lsum = g.SM.v(528, 1)
            alpha = g.SM.v(536, 1)
            rs_ = g.SM.v(544, 1)
            p.memset("ve", acc, 0.0)
            p.memset("ve", nm[0], 3.0e38)
            p.memset("ve", lsum, 0.0)
            nks = qt_ // 4 + 1
            for ks in range(nks):
                pss = g.ps[1]
                p.mm(pss.v(),
                     V(qlat.ap[:, qt_ * 128:(qt_ + 1) * 128], qlat.keys),
                     V(lk_sb.ap[:, ks * SLAB:(ks + 1) * SLAB], lk_sb.keys),
                     start=True, stop=True)
                if ks == qt_ // 4:
                    p.tt("ve", pss.v(), pss.v(),
                         V(mks.ap[:, (qt_ % 4) * SLAB:((qt_ % 4) + 1) * SLAB],
                           mks.keys), ALU.add)
                nmo, nmn = nm[ks % 2], nm[(ks + 1) % 2]
                nmt = g.SM.v(552, 1)
                p.red(nmt, pss.v(), ALU.max, negate=True)
                p.tt("ve", nmn, nmo, nmt, ALU.min)
                p.act(alpha, nmo, AF.Exp, bias=nmn, scale=-1.0)
                p_sb = g.XA.v(4608, SLAB)
                p.act(p_sb, pss.v(), AF.Exp, bias=nmn, accum=rs_)
                p.stt("ve", lsum, lsum, alpha, rs_, ALU.mult, ALU.add)
                pst = g.ps[2]
                for i4 in range(4):
                    p.transp(pst.v(i4 * 128, 128),
                             V(p_sb.ap[:, i4 * 128:(i4 + 1) * 128], p_sb.keys),
                             g.sb_ident)
                pT = g.XA.v(5120, SLAB)
                p.copy("se", pT, pst.v())
                psv = g.ps[3]
                for i4 in range(4):
                    kk = ks * 4 + i4
                    p.mm(psv.v(0, 64),
                         V(pT.ap[:, i4 * 128:(i4 + 1) * 128], pT.keys),
                         V(lv_sb.ap[:, kk * 64:(kk + 1) * 64], lv_sb.keys),
                         start=(i4 == 0), stop=(i4 == 3))
                p.stt("ve", acc, acc, alpha, psv.v(0, 64), ALU.mult, ALU.add)
            linv = g.SM.v(560, 1)
            p.recip(linv, lsum)
            p.ts("ve", acc, acc, linv, None, ALU.mult)
            pst2 = g.ps[4]
            p.transp(pst2.v(0, 128, 0, 64), acc, g.sb_ident)
            ctxT = g.XA.v(5632, 128, 0, 64)
            p.copy("se", ctxT, pst2.v(0, 128, 0, 64))
            psh = g.ps[5]
            p.mm(psh.v(0, 128, 0, 64), vup, ctxT, start=True, stop=True)
            hp_ = g.XA.v(5760, 128, 0, 64)
            p.copy("se", hp_, psh.v(0, 128, 0, 64))
            p.dma("se", g.sc2.v(slice(hh * 64, (hh + 1) * 64),
                                slice(qt_ * 128, (qt_ + 1) * 128),
                                key=(hh // 2, qt_ // 4)), hp_)
    emit_proj(g, g.sc2, g.l3a_woT, 0, KD, KD, g.sf, ev_copy(g))


def emit_l3_ffn(g):
    p = g.p
    DH3 = 3 * D
    for jg in range(3):
        for w_i, wdr in ((0, g.l3f_wgT), (1, g.l3f_wuT)):
            for c in range(KD):
                iw = g.BI.v((c % 2) * 1024, 1024)
                p.dma("sy", iw, wdr.v(slice(c * 128, (c + 1) * 128),
                                      slice(jg * D, (jg + 1) * D),
                                      key=("w", c, jg)))
                p.act(g.WA.v(w_i * 8192 + c * 1024, 1024), iw, AF.Copy,
                      bias=0.0, scale=float(1.0 / 127.0))
        for s in range(NSLAB):
            p.dma("sy", g.XA.v3(0, KD, SLAB), g.xnf.slab3(0, KD, s))
            xs = [g.XA.v(c * SLAB, SLAB) for c in range(KD)]
            for j in range(KD):
                psg = g.ps[j % 2]
                psu = g.ps[2 + j % 2]
                for c in range(KD):
                    p.mm(psg.v(), V(g.WA.v(c * 1024, 1024).ap[:, j * 128:(j + 1) * 128],
                                    g.WA.v(c * 1024, 1024).keys),
                         xs[c], start=(c == 0), stop=(c == KD - 1))
                for c in range(KD):
                    p.mm(psu.v(), V(g.WA.v(8192 + c * 1024, 1024).ap[:, j * 128:(j + 1) * 128],
                                    g.WA.v(8192 + c * 1024, 1024).keys),
                         xs[c], start=(c == 0), stop=(c == KD - 1))
                hcell = g.XB.v(j * SLAB, SLAB)
                p.act(hcell, psg.v(), AF.Silu)
                p.tt("ve", hcell, hcell, psu.v(), ALU.mult)
            p.dma("se", g.hidf.slab3(jg * KD, KD, s), g.XB.v3(0, KD, SLAB))
    for s in range(NSLAB):
        p.dma("sy", g.WA.v3(0, 3 * KD, SLAB), g.hidf.slab3(0, 3 * KD, s))
        hs = [g.WA.v(k * SLAB, SLAB) for k in range(3 * KD)]
        for k in range(3 * KD):
            iw = g.BI.v((k % 2) * 1024, 1024)
            p.dma("sy", iw, g.l3f_wdT.v(slice(k * 128, (k + 1) * 128),
                                        slice(0, D), key=("w", k, s)))
            wk = g.XA.v((k % 2) * D, D)
            p.act(wk, iw, AF.Copy, bias=0.0, scale=float(1.0 / 127.0))
            for j in range(KD):
                p.mm(g.ps[j].v(), V(wk.ap[:, j * 128:(j + 1) * 128], wk.keys),
                     hs[k], start=(k == 0), stop=(k == 3 * KD - 1))
        for j in range(KD):
            ot = g.XB.v(4096 + j * SLAB, SLAB)
            p.act(ot, g.ps[j].v(), AF.Silu)
        p.dma("se", g.gf.slab3(0, KD, s), g.XB.v3(4096, KD, SLAB))


def emit_combine(g, xin, xout):
    p = g.p
    for s in range(NSLAB):
        p.dma("sy", g.XA.v3(0, KD, SLAB), xin.slab3(0, KD, s))
        p.dma("sy", g.XA.v3(4096, KD, SLAB), g.sf.slab3(0, KD, s))
        p.dma("ge", g.XB.v3(0, KD, SLAB), g.gf.slab3(0, KD, s))
        for c in range(KD):
            xc = g.XA.v(c * SLAB, SLAB)
            sc_ = g.XA.v(4096 + c * SLAB, SLAB)
            gc_ = g.XB.v(c * SLAB, SLAB)
            eng = "ve" if c % 2 else "ge"
            p.tt(eng, sc_, sc_, gc_, ALU.mult)
            p.tt(eng, sc_, sc_, xc, ALU.add)
        p.dma("se", xout.slab3(0, KD, s), g.XA.v3(4096, KD, SLAB))


def emit_all(g):
    p = g.p
    g.sb_bq = g.SM.v(256, 8)
    g.sb_bk = g.SM.v(264, 8)
    g.sb_bo = g.SM.v(272, 8)
    g.sb_bin = g.SM.v(280, 32)
    g.sb_bout = g.SM.v(312, 8)
    p.dma("sy", g.sb_bq, g.l0_bq.whole())
    p.dma("sy", g.sb_bk, g.l0_bk.whole())
    p.dma("sy", g.sb_bo, g.l0_bo.whole())
    p.dma("sy", g.sb_bin, g.l0_bin.whole())
    p.dma("sy", g.sb_bout, g.l0_bout.whole())

    xcur = g.x_f
    emitters = [(emit_l0_attn, emit_l0_ffn), (emit_l1_attn, emit_l1_ffn),
                (emit_l2_attn, emit_l2_ffn), (emit_l3_attn, emit_l3_ffn)]
    for blk in range(N_BLOCKS):
        emit_ln(g, xcur)
        emitters[blk][0](g)
        emitters[blk][1](g)
        last = (blk == N_BLOCKS - 1)
        tgt = g.out_f if (last and DEV_TAP is None) else g.xf[blk % 2]
        emit_combine(g, xcur, tgt)
        xcur = tgt

    if DEV_TAP == "alpha":
        for tt_ in range(NTT):
            av = g.XA.v(0, 64)
            p.dma("sy", av, g.alpha_t.v(slice(tt_ * 128, (tt_ + 1) * 128),
                                        slice(0, 64), key=(tt_,)))
            pst = g.ps[0]
            p.transp(pst.v(0, 128, 0, 64), av, g.sb_ident)
            ot = g.XB.v(0, 128, 0, 64)
            p.copy("se", ot, pst.v(0, 128, 0, 64))
            p.dma("se", g.out_f.v(slice(0, 64),
                                  slice(tt_ * 128, (tt_ + 1) * 128),
                                  key=("a", tt_)), ot)
        return
    if DEV_TAP is not None:
        srcd = (g.xf[int(DEV_TAP[2])] if DEV_TAP.startswith("xf")
                else getattr(g, DEV_TAP))
        for s in range(NSLAB):
            ks = ("qt", s) if DEV_TAP == "attnf" else None
            p.dma("sy", g.XA.v3(0, KD, SLAB), srcd.slab3(0, KD, s, key_s=ks))
            p.dma("se", g.out_f.slab3(0, KD, s), g.XA.v3(0, KD, SLAB))


def kernel(**inputs):
    h, cores = host_prepare(inputs)
    nc = build_program()
    in_maps = []
    for c in range(NCORES):
        m = dict(h)
        m.update(cores[c])
        in_maps.append(m)
    res = run_bass_kernel_spmd(nc, in_maps, core_ids=list(range(NCORES)))
    out = np.stack([np.ascontiguousarray(np.asarray(r["out_f"]).T)
                    for r in res.results])
    return out.astype(np.float32)


# version 04 (final)
# speedup vs baseline: 1.0259x; 1.0114x over previous
import numpy as np
import concourse.bass as bass
import concourse.mybir as mybir
from concourse.bass_utils import run_bass_kernel_spmd

F32 = mybir.dt.float32
BF16 = mybir.dt.bfloat16
I32 = mybir.dt.int32
AF = mybir.ActivationFunctionType
ALU = mybir.AluOpType
AX = mybir.AxisListType

D = 1024
T = 4096
KD = 8
NSLAB = 8
SLAB = 512
NTT = 32
NCORES = 8

N_BLOCKS = 4
LAST_PROG = None
DEV_DUMP = False
DEV_TAP = None

BIGNEG = -3.0e38
RC1 = 98304.0
RC2 = RC1 + 384.0


class V:
    __slots__ = ("ap", "keys")

    def __init__(self, ap, keys):
        self.ap = ap
        self.keys = frozenset(keys)

    def __getitem__(self, sl):
        return V(self.ap.__getitem__(sl), self.keys)


class Arena:
    def __init__(self, nc, name, ncols, dtype=F32, gran=512):
        self.name = name
        self.gran = gran
        self.ncols = ncols
        self.t = nc.alloc_sbuf_tensor(name, [128, ncols], dtype)

    def v(self, col0, ncols, r0=0, r1=128):
        assert col0 + ncols <= self.ncols, (self.name, col0, ncols)
        ap = self.t.ap()[r0:r1, col0:col0 + ncols]
        g0 = col0 // self.gran
        g1 = (col0 + ncols - 1) // self.gran
        return V(ap, ((self.name, g) for g in range(g0, g1 + 1)))

    def v3(self, col0, nch, ns):
        vv = self.v(col0, nch * ns)
        return V(vv.ap.rearrange("p (c t) -> p c t", t=ns), vv.keys)


class Psum:
    def __init__(self, nc, name):
        self.name = name
        self.t = nc.alloc_psum_tensor(name, [128, 512], F32)

    def v(self, col0=0, ncols=512, r0=0, r1=128):
        return V(self.t.ap()[r0:r1, col0:col0 + ncols], [(self.name,)])


class Dram:
    def __init__(self, nc, name, shape, dtype=F32, kind="Internal"):
        self.name = name
        self.shape = shape
        self.t = nc.dram_tensor(name, list(shape), dtype, kind=kind)

    def v(self, *slices, key=None):
        ap = self.t.ap()
        if slices:
            ap = ap.__getitem__(tuple(slices))
        if key is None:
            keys = [(self.name,)]
        elif isinstance(key, list):
            keys = [(self.name,) + (k if isinstance(k, tuple) else (k,)) for k in key]
        else:
            keys = [(self.name,) + (key if isinstance(key, tuple) else (key,))]
        return V(ap, keys)

    def cs(self, c, s, nc_=128, ns=SLAB):
        return self.v(slice(c * nc_, (c + 1) * nc_),
                      slice(s * ns, (s + 1) * ns), key=(c, s))

    def slab3(self, c0, nch, s, ns=SLAB, key_s=None):
        ap = self.t.ap().rearrange("(c p) t -> p c t", p=128)
        ap = ap[:, c0:c0 + nch, s * ns:(s + 1) * ns]
        ks = s if key_s is None else key_s
        return V(ap, [(self.name, c, ks) for c in range(c0, c0 + nch)])

    def whole(self):
        return self.v()


class Prog:
    CE = ("te", "ve", "se", "ge", "sy")

    def __init__(self, nc):
        self.nc = nc
        self.ops = []
        self.state = {}

    def op(self, eng, fn, reads=(), writes=(), kind="c"):
        rkeys = [r.keys for r in reads]
        wkeys = [w.keys for w in writes]
        deps = set()
        for kset in rkeys:
            for k in kset:
                st = self.state.get(k)
                if st and st[0] is not None:
                    deps.add(st[0])
        for kset in wkeys:
            for k in kset:
                st = self.state.get(k)
                if st:
                    if st[0] is not None:
                        deps.add(st[0])
                    deps.update(st[1])
        oid = len(self.ops)
        deps.discard(oid)
        self.ops.append(dict(eng=eng, kind=kind, fn=fn, deps=deps))
        for kset in rkeys:
            for k in kset:
                self.state.setdefault(k, [None, set()])[1].add(oid)
        for kset in wkeys:
            for k in kset:
                self.state[k] = [oid, set()]
        return oid

    def dma(self, queue, dst, src, **kw):
        nc = self.nc
        eng = {"sy": nc.sync, "se": nc.scalar, "ge": nc.gpsimd}[queue]
        return self.op(queue, lambda: eng.dma_start(out=dst.ap, in_=src.ap, **kw),
                       reads=[src], writes=[dst], kind="d")

    def mm(self, out, lhsT, rhs, start, stop):
        nc = self.nc
        return self.op("te", lambda: nc.tensor.matmul(
            out.ap, lhsT.ap, rhs.ap, start=start, stop=stop),
            reads=[lhsT, rhs], writes=[out])

    def transp(self, out, in_, ident):
        nc = self.nc
        return self.op("te", lambda: nc.tensor.transpose(out.ap, in_.ap, ident.ap),
                       reads=[in_, ident], writes=[out])

    def tt(self, eng, out, a, b, alu):
        nc = self.nc
        e = {"ve": nc.vector, "ge": nc.gpsimd}[eng]
        return self.op(eng, lambda: e.tensor_tensor(out.ap, a.ap, b.ap, alu),
                       reads=[a, b], writes=[out])

    def ts(self, eng, out, a, s1, s2, op1, op2=None):
        nc = self.nc
        e = {"ve": nc.vector, "ge": nc.gpsimd}[eng]
        reads = [a] + [x for x in (s1, s2) if isinstance(x, V)]
        s1a = s1.ap if isinstance(s1, V) else s1
        s2a = s2.ap if isinstance(s2, V) else s2
        if op2 is None:
            return self.op(eng, lambda: e.tensor_scalar(out.ap, a.ap, s1a, None, op1),
                           reads=reads, writes=[out])
        return self.op(eng, lambda: e.tensor_scalar(out.ap, a.ap, s1a, s2a, op1, op2),
                       reads=reads, writes=[out])

    def stt(self, eng, out, in0, scalar, in1, op0, op1):
        nc = self.nc
        e = {"ve": nc.vector, "ge": nc.gpsimd}[eng]
        sa = scalar.ap if isinstance(scalar, V) else scalar
        reads = [in0, in1] + ([scalar] if isinstance(scalar, V) else [])
        return self.op(eng, lambda: e.scalar_tensor_tensor(
            out.ap, in0.ap, sa, in1.ap, op0, op1), reads=reads, writes=[out])

    def act(self, out, in_, func, bias=0.0, scale=1.0, accum=None):
        nc = self.nc
        reads = [in_] + [x for x in (bias, scale) if isinstance(x, V)]
        ba = bias.ap if isinstance(bias, V) else bias
        sa = scale.ap if isinstance(scale, V) else scale
        writes = [out] + ([accum] if accum is not None else [])
        if accum is None:
            return self.op("se", lambda: nc.scalar.activation(
                out.ap, in_.ap, func, bias=ba, scale=sa), reads=reads, writes=writes)
        return self.op("se", lambda: nc.scalar.activation(
            out.ap, in_.ap, func, bias=ba, scale=sa, accum_out=accum.ap),
            reads=reads, writes=writes)

    def copy(self, eng, out, in_):
        nc = self.nc
        if eng == "se":
            return self.op("se", lambda: nc.scalar.copy(out.ap, in_.ap),
                           reads=[in_], writes=[out])
        e = {"ve": nc.vector, "ge": nc.gpsimd}[eng]
        return self.op(eng, lambda: e.tensor_copy(out.ap, in_.ap),
                       reads=[in_], writes=[out])

    def red(self, out, in_, alu, negate=False, absval=False):
        nc = self.nc
        return self.op("ve", lambda: nc.vector.tensor_reduce(
            out.ap, in_.ap, AX.X, alu,
            apply_absolute_value=absval, negate=negate),
            reads=[in_], writes=[out])

    def memset(self, eng, out, val):
        nc = self.nc
        e = {"ve": nc.vector, "ge": nc.gpsimd}[eng]
        return self.op(eng, lambda: e.memset(out.ap, val), reads=[], writes=[out])

    def recip(self, out, in_):
        nc = self.nc
        return self.op("ve", lambda: nc.vector.reciprocal(out.ap, in_.ap),
                       reads=[in_], writes=[out])

    def scan(self, out, data0, data1, initial):
        nc = self.nc
        init = initial.ap if isinstance(initial, V) else initial
        reads = [data0, data1] + ([initial] if isinstance(initial, V) else [])
        return self.op("ve", lambda: nc.vector.tensor_tensor_scan(
            out.ap, data0.ap, data1.ap, init, ALU.mult, ALU.add),
            reads=reads, writes=[out])

    def max8(self, out, in_):
        nc = self.nc
        return self.op("ve", lambda: nc.vector.max(out=out.ap, in_=in_.ap),
                       reads=[in_], writes=[out])

    def match_replace(self, out, to_replace, values, imm):
        nc = self.nc
        return self.op("ve", lambda: nc.vector.match_replace(
            out=out.ap, in_to_replace=to_replace.ap, in_values=values.ap,
            imm_value=imm), reads=[to_replace, values], writes=[out])

    def collective_max(self, ins, outs):
        nc = self.nc
        return self.op("ge", lambda: nc.gpsimd.collective_compute(
            "AllReduce", ALU.max, replica_groups=[list(range(NCORES))],
            ins=[ins.ap.opt()], outs=[outs.ap.opt()]),
            reads=[ins], writes=[outs], kind="cc")

    def finalize(self, block):
        nc = self.nc
        streams = {e: [] for e in self.CE}
        for oid, o in enumerate(self.ops):
            streams[o["eng"]].append(oid)
        for o in self.ops:
            best = {}
            keep = set()
            for d in o["deps"]:
                do = self.ops[d]
                if do["eng"] == o["eng"] or do["kind"] == "cc":
                    keep.add(d)
                    continue
                k = (do["eng"], do["kind"])
                if k not in best or d > best[k]:
                    best[k] = d
            keep.update(best.values())
            o["deps"] = keep
        need_sig = set()
        for o in self.ops:
            need_sig.update(o["deps"])
        sems = {}
        for e in self.CE:
            sems[e + "_c"] = nc.alloc_semaphore(name="s_" + e)
            sems[e + "_d"] = nc.alloc_semaphore(name="d_" + e)
        sigval = {}
        dcount = {e: 0 for e in self.CE}
        ccount = {e: 0 for e in self.CE}
        for e in self.CE:
            for oid in streams[e]:
                o = self.ops[oid]
                if o["kind"] == "d":
                    dcount[e] += 1
                    sigval[oid] = (e + "_d", 16 * dcount[e])
                elif oid in need_sig:
                    ccount[e] += 1
                    sigval[oid] = (e + "_c", ccount[e])

        def mk_body(e):
            def body(eng):
                know = {}
                pend = [0]

                def flush():
                    if pend[0]:
                        eng.drain()
                        eng.nop().then_inc(sems["te_c"], pend[0])
                        pend[0] = 0

                for oid in streams[e]:
                    o = self.ops[oid]
                    for dd in sorted(o["deps"]):
                        sname, val = sigval[dd]
                        if know.get(sname, 0) < val:
                            flush()
                            eng.wait_ge(sems[sname], val)
                            know[sname] = val
                    ins = o["fn"]()
                    if o["kind"] == "d":
                        sname, val = sigval[oid]
                        ins.then_inc(sems[sname], 16)
                        if oid in need_sig:
                            eng.wait_ge(sems[sname], val)
                            know[sname] = val
                    elif oid in need_sig:
                        if e == "te":
                            pend[0] += 1
                            if pend[0] >= 16:
                                flush()
                        else:
                            ins.then_inc(sems[sigval[oid][0]], 1)
                flush()
                if dcount[e]:
                    eng.wait_ge(sems[e + "_d"], 16 * dcount[e])
            return body

        block.tensor(mk_body("te"))
        block.vector(mk_body("ve"))
        block.scalar(mk_body("se"))
        block.gpsimd(mk_body("ge"))
        block.sync(mk_body("sy"))


CONSTS = {
    "eps_ln": 1e-5,
    "one": 1.0,
}
CIDX = {k: i for i, k in enumerate(CONSTS)}


def _bias_cols(b):
    b = np.asarray(b, np.float32).reshape(-1)
    n = b.size // 128
    return np.ascontiguousarray(b.reshape(n, 128).T)


def _rope_tables():
    half = 32
    theta = 1.0 / (10000.0 ** (np.arange(half, dtype=np.float64) / half))
    ang = np.arange(T, dtype=np.float64)[:, None] * theta[None, :]
    sin = np.sin(ang).astype(np.float32).T
    cos = np.cos(ang).astype(np.float32).T
    C = np.tile(cos, (4, 1))
    S = np.tile(np.concatenate([-sin, sin], axis=0), (2, 1))
    return np.ascontiguousarray(C), np.ascontiguousarray(S)


def _diag_masks():
    out = []
    for off in range(0, 512, 128):
        q = off + np.arange(128)[:, None]
        k = np.arange(512)[None, :]
        out.append(np.where(k <= q, np.float32(0.0), np.float32(BIGNEG)))
    return np.ascontiguousarray(np.stack(out).astype(np.float32))


def host_prepare(inputs):
    x = np.asarray(inputs["L0_attn_x"], np.float32)
    h = {}
    perm = np.zeros(D, np.int64)
    for hh in range(16):
        b0 = hh * 64
        perm[b0:b0 + 32] = b0 + np.arange(0, 64, 2)
        perm[b0 + 32:b0 + 64] = b0 + np.arange(1, 64, 2)
    h["l0_wqT"] = np.ascontiguousarray(np.asarray(inputs["L0_attn_W_q"], np.float32).T[:, perm])
    h["l0_wkT"] = np.ascontiguousarray(np.asarray(inputs["L0_attn_W_k"], np.float32).T[:, perm])
    h["l0_wvT"] = np.ascontiguousarray(np.asarray(inputs["L0_attn_W_v"], np.float32).T)
    h["l0_woT"] = np.ascontiguousarray(np.asarray(inputs["L0_attn_W_o"], np.float32).T)
    h["l0_bq"] = _bias_cols(np.asarray(inputs["L0_attn_b_q"], np.float32)[perm])
    h["l0_bk"] = _bias_cols(np.asarray(inputs["L0_attn_b_k"], np.float32)[perm])
    h["l0_bv_row"] = np.asarray(inputs["L0_attn_b_v"], np.float32).reshape(1, D).copy()
    h["l0_bo"] = _bias_cols(inputs["L0_attn_b_o"])
    C, S = _rope_tables()
    h["ropeCq"] = np.ascontiguousarray(C * np.float32(0.125))
    h["ropeSq"] = np.ascontiguousarray(S * np.float32(0.125))
    h["ropeCk"] = C
    h["ropeSk"] = S
    h["dmask"] = _diag_masks()
    h["l0_win"] = np.ascontiguousarray(np.asarray(inputs["L0_ffn_W_in"], np.float32))
    h["l0_bin"] = _bias_cols(inputs["L0_ffn_b_in"])
    h["l0_wout"] = np.ascontiguousarray(np.asarray(inputs["L0_ffn_W_out"], np.float32))
    h["l0_bout"] = _bias_cols(inputs["L0_ffn_b_out"])
    h["ident"] = np.eye(128, dtype=np.float32)
    h["ones"] = np.ones((128, 1), np.float32)
    cv = np.zeros((128, 8), np.float32)
    for k, i in CIDX.items():
        cv[:, i] = CONSTS[k]
    h["cvec"] = np.ascontiguousarray(cv)
    h["zeros64"] = np.zeros((1, 64), np.float32)
    for d_ in ("fwd", "bwd"):
        wz = np.asarray(inputs[f"L1_attn_{d_}_Wz"], np.float32)
        wh = np.asarray(inputs[f"L1_attn_{d_}_Wh"], np.float32)
        h[f"l1a_wz_{d_}"] = np.ascontiguousarray(
            np.concatenate([wz[i].T for i in range(4)], axis=0))
        h[f"l1a_wh_{d_}"] = np.ascontiguousarray(
            np.concatenate([wh[i].T for i in range(4)], axis=0))
        bz = np.asarray(inputs[f"L1_attn_{d_}_bz"], np.float32)
        bh = np.asarray(inputs[f"L1_attn_{d_}_bh"], np.float32)
        h[f"l1a_bz_{d_}"] = np.ascontiguousarray(
            np.concatenate([_bias_cols(bz[i]) for i in range(4)], axis=1))
        h[f"l1a_bh_{d_}"] = np.ascontiguousarray(
            np.concatenate([_bias_cols(bh[i]) for i in range(4)], axis=1))
    h["l1a_fusionT"] = np.ascontiguousarray(
        np.asarray(inputs["L1_attn_fusion_W"], np.float32).T)
    h["l1a_fb"] = _bias_cols(inputs["L1_attn_fusion_b"])
    h["l1a_g"] = _bias_cols(inputs["L1_attn_ln_gamma"])
    h["l1a_b"] = _bias_cols(inputs["L1_attn_ln_beta"])

    w1 = np.asarray(inputs["L1_ffn_w1"], np.float32)
    h["l1f_w1a"] = np.ascontiguousarray(w1[:D])
    h["l1f_w1b"] = np.ascontiguousarray(w1[D:])
    h["l1f_b1"] = _bias_cols(inputs["L1_ffn_b1"])
    h["l1f_w2"] = np.ascontiguousarray(np.asarray(inputs["L1_ffn_w2"], np.float32))
    h["l1f_b2row"] = np.asarray(inputs["L1_ffn_b2"], np.float32).reshape(1, 64).copy()
    h["l1f_tau"] = np.asarray(inputs["L1_ffn_tau"], np.float32).reshape(1, 1).copy()
    h["l1f_wihT"] = np.ascontiguousarray(np.asarray(inputs["L1_ffn_Wih"], np.float32).T)
    h["l1f_whhT"] = np.ascontiguousarray(np.asarray(inputs["L1_ffn_Whh"], np.float32).T)
    h["l1f_bih"] = _bias_cols(inputs["L1_ffn_bih"])
    h["l1f_bhh"] = _bias_cols(inputs["L1_ffn_bhh"])
    h["l1f_wq"] = np.ascontiguousarray(np.asarray(inputs["L1_ffn_wq"], np.float32))
    h["l1f_bq"] = _bias_cols(inputs["L1_ffn_bq"])
    h["l1f_wk"] = np.ascontiguousarray(np.asarray(inputs["L1_ffn_wk"], np.float32))
    h["l1f_bk"] = _bias_cols(inputs["L1_ffn_bk"])
    h["l1f_wv"] = np.ascontiguousarray(np.asarray(inputs["L1_ffn_wv"], np.float32))
    h["l1f_bvrow"] = np.asarray(inputs["L1_ffn_bv"], np.float32).reshape(1, D).copy()
    h["l1f_lng"] = _bias_cols(inputs["L1_ffn_ln_g"])
    h["l1f_lnb"] = _bias_cols(inputs["L1_ffn_ln_b"])
    h["l1f_wvp"] = np.ascontiguousarray(np.asarray(inputs["L1_ffn_wvp"], np.float32))
    h["l1f_bvprow"] = np.asarray(inputs["L1_ffn_bvp"], np.float32).reshape(1, D).copy()
    h["l1f_wo"] = np.ascontiguousarray(np.asarray(inputs["L1_ffn_wo"], np.float32))
    h["l1f_bo"] = _bias_cols(inputs["L1_ffn_bo"])
    h["l1f_slot"] = np.ascontiguousarray(np.asarray(inputs["L1_ffn_slot_init"], np.float32))

    h["l2a_ln1g"] = _bias_cols(inputs["L2_attn_ln1_g"])
    h["l2a_ln1b"] = _bias_cols(inputs["L2_attn_ln1_b"])
    h["l2a_ln2g"] = _bias_cols(inputs["L2_attn_ln2_g"])
    h["l2a_ln2b"] = _bias_cols(inputs["L2_attn_ln2_b"])
    h["l2a_inW"] = np.ascontiguousarray(np.asarray(inputs["L2_attn_in_W"], np.float32))
    h["l2a_inb"] = _bias_cols(inputs["L2_attn_in_b"])
    dw = np.asarray(inputs["L2_attn_dw_W"], np.float32)
    h["l2a_dw"] = np.ascontiguousarray(
        np.stack([_bias_cols(dw[:, 0, k]) for k in range(3)]))
    h["l2a_dwb"] = _bias_cols(inputs["L2_attn_dw_b"])
    h["l2a_dtW"] = np.ascontiguousarray(np.asarray(inputs["L2_attn_dt_W"], np.float32))
    h["l2a_dtb"] = _bias_cols(inputs["L2_attn_dt_b"])
    h["l2a_bW"] = np.ascontiguousarray(np.asarray(inputs["L2_attn_b_W"], np.float32))
    h["l2a_bb"] = _bias_cols(inputs["L2_attn_b_b"])
    h["l2a_cW"] = np.ascontiguousarray(np.asarray(inputs["L2_attn_c_W"], np.float32))
    h["l2a_cb"] = _bias_cols(inputs["L2_attn_c_b"])
    h["l2a_alog"] = _bias_cols(inputs["L2_attn_a_log"])
    h["l2a_dskip"] = _bias_cols(inputs["L2_attn_d_skip"])
    h["l2a_outW"] = np.ascontiguousarray(np.asarray(inputs["L2_attn_out_W"], np.float32))
    h["l2a_outb"] = _bias_cols(inputs["L2_attn_out_b"])
    h["l2a_f1W"] = np.ascontiguousarray(np.asarray(inputs["L2_attn_f1_W"], np.float32))
    h["l2a_f1b"] = _bias_cols(inputs["L2_attn_f1_b"])
    h["l2a_f2W"] = np.ascontiguousarray(np.asarray(inputs["L2_attn_f2_W"], np.float32))
    h["l2a_f2b"] = _bias_cols(inputs["L2_attn_f2_b"])

    SC2 = np.float32(1.0 / np.sqrt(128.0))
    h["l2f_qT"] = np.ascontiguousarray(
        np.asarray(inputs["L2_ffn_q_w"], np.float32).T * SC2)
    h["l2f_kT"] = np.ascontiguousarray(np.asarray(inputs["L2_ffn_k_w"], np.float32).T)
    vw = np.asarray(inputs["L2_ffn_v_w"], np.float32)
    h["l2f_vwT"] = np.ascontiguousarray(vw.transpose(1, 0, 2).reshape(D, 64 * 128))
    h["l2f_ow"] = np.ascontiguousarray(
        np.asarray(inputs["L2_ffn_o_w"], np.float32).reshape(64 * 128, D))
    h["l2f_svT"] = np.ascontiguousarray(np.asarray(inputs["L2_ffn_sel_v_w"], np.float32).T)
    h["l2f_soT"] = np.ascontiguousarray(np.asarray(inputs["L2_ffn_sel_o_w"], np.float32).T)
    h["l2f_rs"] = np.asarray(inputs["L2_ffn_route_scale"], np.float32).reshape(1, 1).copy()

    h["l3a_wqT"] = np.ascontiguousarray(np.asarray(inputs["L3_attn_Wq"], np.float32).T)
    h["l3a_wkdT"] = np.ascontiguousarray(np.asarray(inputs["L3_attn_Wk_down"], np.float32).T)
    h["l3a_wvdT"] = np.ascontiguousarray(np.asarray(inputs["L3_attn_Wv_down"], np.float32).T)
    h["l3a_q2l"] = np.ascontiguousarray(
        (np.asarray(inputs["L3_attn_q_to_latent"], np.float32) *
         np.float32(0.125)).reshape(16 * 64, 64))
    h["l3a_vup"] = np.ascontiguousarray(
        np.asarray(inputs["L3_attn_v_up"], np.float32).reshape(16 * 64, 64))
    h["l3a_woT"] = np.ascontiguousarray(np.asarray(inputs["L3_attn_Wout"], np.float32).T)

    h["l3f_wgT"] = np.ascontiguousarray(np.asarray(inputs["L3_ffn_w_gate"], np.int32).T)
    h["l3f_wuT"] = np.ascontiguousarray(np.asarray(inputs["L3_ffn_w_up"], np.int32).T)
    h["l3f_wdT"] = np.ascontiguousarray(np.asarray(inputs["L3_ffn_w_down"], np.int32).T)

    cores = [{"x_f": np.ascontiguousarray(x[c].T)} for c in range(NCORES)]
    return h, cores


class Ctx:
    pass


def build_program():
    nc = bass.Bass("TRN2", target_bir_lowering=False)
    g = Ctx()
    g.nc = nc

    def inp(name, shape, dtype=F32):
        return Dram(nc, name, shape, dtype, kind="ExternalInput")

    g.x_f = inp("x_f", [D, T])
    g.l0_wqT = inp("l0_wqT", [D, D]); g.l0_wkT = inp("l0_wkT", [D, D])
    g.l0_wvT = inp("l0_wvT", [D, D]); g.l0_woT = inp("l0_woT", [D, D])
    g.l0_bq = inp("l0_bq", [128, 8]); g.l0_bk = inp("l0_bk", [128, 8])
    g.l0_bv_row = inp("l0_bv_row", [1, D]); g.l0_bo = inp("l0_bo", [128, 8])
    g.ropeCq = inp("ropeCq", [128, T]); g.ropeSq = inp("ropeSq", [128, T])
    g.ropeCk = inp("ropeCk", [128, T]); g.ropeSk = inp("ropeSk", [128, T])
    g.dmask = inp("dmask", [4, 128, 512])
    g.l0_win = inp("l0_win", [D, 4 * D]); g.l0_bin = inp("l0_bin", [128, 32])
    g.l0_wout = inp("l0_wout", [4 * D, D]); g.l0_bout = inp("l0_bout", [128, 8])
    g.ident = inp("ident", [128, 128])
    g.ones = inp("ones", [128, 1])
    g.cvec = inp("cvec", [128, 8])
    g.zeros64 = inp("zeros64", [1, 64])
    if N_BLOCKS >= 2:
        for d_ in ("fwd", "bwd"):
            setattr(g, f"l1a_wz_{d_}", inp(f"l1a_wz_{d_}", [4 * D, D]))
            setattr(g, f"l1a_wh_{d_}", inp(f"l1a_wh_{d_}", [4 * D, D]))
            setattr(g, f"l1a_bz_{d_}", inp(f"l1a_bz_{d_}", [128, 32]))
            setattr(g, f"l1a_bh_{d_}", inp(f"l1a_bh_{d_}", [128, 32]))
        g.l1a_fusionT = inp("l1a_fusionT", [2 * D, D])
        g.l1a_fb = inp("l1a_fb", [128, 8])
        g.l1a_g = inp("l1a_g", [128, 8]); g.l1a_b = inp("l1a_b", [128, 8])
        g.l1f_w1a = inp("l1f_w1a", [D, 512]); g.l1f_w1b = inp("l1f_w1b", [D, 512])
        g.l1f_b1 = inp("l1f_b1", [128, 4]); g.l1f_w2 = inp("l1f_w2", [512, 64])
        g.l1f_b2row = inp("l1f_b2row", [1, 64]); g.l1f_tau = inp("l1f_tau", [1, 1])
        g.l1f_wihT = inp("l1f_wihT", [D, 3 * D]); g.l1f_whhT = inp("l1f_whhT", [D, 3 * D])
        g.l1f_bih = inp("l1f_bih", [128, 24]); g.l1f_bhh = inp("l1f_bhh", [128, 24])
        g.l1f_wq = inp("l1f_wq", [D, 256]); g.l1f_bq = inp("l1f_bq", [128, 2])
        g.l1f_wk = inp("l1f_wk", [D, 256]); g.l1f_bk = inp("l1f_bk", [128, 2])
        g.l1f_wv = inp("l1f_wv", [D, D]); g.l1f_bvrow = inp("l1f_bvrow", [1, D])
        g.l1f_lng = inp("l1f_lng", [128, 8]); g.l1f_lnb = inp("l1f_lnb", [128, 8])
        g.l1f_wvp = inp("l1f_wvp", [D, D]); g.l1f_bvprow = inp("l1f_bvprow", [1, D])
        g.l1f_wo = inp("l1f_wo", [D, D]); g.l1f_bo = inp("l1f_bo", [128, 8])
        g.l1f_slot = inp("l1f_slot", [64, D])
    if N_BLOCKS >= 3:
        for nm, shp in [("l2a_ln1g", [128, 8]), ("l2a_ln1b", [128, 8]),
                        ("l2a_ln2g", [128, 8]), ("l2a_ln2b", [128, 8]),
                        ("l2a_inW", [D, 2 * D]), ("l2a_inb", [128, 16]),
                        ("l2a_dw", [3, 128, 8]), ("l2a_dwb", [128, 8]),
                        ("l2a_dtW", [D, D]), ("l2a_dtb", [128, 8]),
                        ("l2a_bW", [D, D]), ("l2a_bb", [128, 8]),
                        ("l2a_cW", [D, D]), ("l2a_cb", [128, 8]),
                        ("l2a_alog", [128, 8]), ("l2a_dskip", [128, 8]),
                        ("l2a_outW", [D, D]), ("l2a_outb", [128, 8]),
                        ("l2a_f1W", [D, 2 * D]), ("l2a_f1b", [128, 16]),
                        ("l2a_f2W", [2 * D, D]), ("l2a_f2b", [128, 8]),
                        ("l2f_qT", [D, D]), ("l2f_kT", [D, D]),
                        ("l2f_vwT", [D, 64 * 128]), ("l2f_ow", [64 * 128, D]),
                        ("l2f_svT", [D, 64]), ("l2f_soT", [D, 64]),
                        ("l2f_rs", [1, 1])]:
            setattr(g, nm, inp(nm, shp))
    if N_BLOCKS >= 4:
        for nm, shp, dt in [("l3a_wqT", [D, D], F32), ("l3a_wkdT", [D, 64], F32),
                            ("l3a_wvdT", [D, 64], F32), ("l3a_q2l", [16 * 64, 64], F32),
                            ("l3a_vup", [16 * 64, 64], F32), ("l3a_woT", [D, D], F32),
                            ("l3f_wgT", [D, 3 * D], I32), ("l3f_wuT", [D, 3 * D], I32),
                            ("l3f_wdT", [3 * D, D], I32)]:
            setattr(g, nm, inp(nm, shp, dt))

    g.out_f = Dram(nc, "out_f", [D, T], kind="ExternalOutput")

    g.xf = [Dram(nc, f"xf{i}", [D, T]) for i in range(2)]
    g.xnf = Dram(nc, "xnf", [D, T])
    g.sf = Dram(nc, "sf", [D, T])
    g.gf = Dram(nc, "gf", [D, T])
    g.qrope = Dram(nc, "qrope", [D, T])
    g.krope = Dram(nc, "krope", [D, T])
    g.vtok = Dram(nc, "vtok", [T, D])
    g.attnf = Dram(nc, "attnf", [D, T])
    g.xsimf = Dram(nc, "xsimf", [D, T])
    g.hidf = Dram(nc, "hidf", [4 * D, T])
    g.w_in_sim = Dram(nc, "w_in_sim", [D, 4 * D])
    g.w_out_sim = Dram(nc, "w_out_sim", [4 * D, D])
    g.rows = Dram(nc, "rows", [6, 4 * D])
    g.cc_in = Dram(nc, "cc_in", [1, 64])
    g.cc_out = Dram(nc, "cc_out", [1, 64])
    for i in range(8):
        setattr(g, f"sc{i}", Dram(nc, f"sc{i}", [D, T]))
    g.alpha_t = Dram(nc, "alpha_t", [T, 64])
    g.gv_t = Dram(nc, "gv_t", [T, 64])
    g.go_t = Dram(nc, "go_t", [T, 64])
    g.lv_t = Dram(nc, "lv_t", [T, 64])

    with nc.Block() as block:
        p = Prog(nc)
        g.p = p
        global LAST_PROG
        LAST_PROG = p
        g.WA = Arena(nc, "WA", 16384)
        g.WB = Arena(nc, "WB", 8192)
        g.XA = Arena(nc, "XA", 8192)
        g.XB = Arena(nc, "XB", 8192)
        g.YA = Arena(nc, "YA", 4096)
        g.SM = Arena(nc, "SM", 1024, gran=8)
        g.BB = Arena(nc, "BB", 4096, BF16)
        g.BI = Arena(nc, "BI", 2048, I32)
        g.ps = [Psum(nc, f"ps{i}") for i in range(8)]

        g.sb_ident = g.SM.v(0, 128)
        g.sb_ones = g.SM.v(128, 1)
        g.sb_cvec = g.SM.v(136, 8)
        p.dma("sy", g.sb_ident, g.ident.whole())
        p.dma("sy", g.sb_ones, g.ones.whole())
        p.dma("sy", g.sb_cvec, g.cvec.whole())

        emit_all(g)
        p.finalize(block)
    return nc


def dve_abs(g, eng, out, in_, tmp):
    p = g.p
    p.ts(eng, tmp, in_, -1.0, None, ALU.mult)
    p.tt(eng, out, tmp, in_, ALU.max)


def ident64(g):
    return V(g.sb_ident.ap[0:64, 0:64], g.sb_ident.keys)


def ccv(g, name, rows=128):
    v = g.SM.v(136 + CIDX[name], 1)
    return V(v.ap[0:rows, :], v.keys)


def round_even(g, eng, buf, tmp_f, tmp_f2, tmp_b):
    p = g.p
    C1 = 98304.0
    p.ts(eng, tmp_f, buf, C1, None, ALU.add)
    p.copy(eng, tmp_b, tmp_f)
    p.copy(eng, tmp_f2, tmp_b)
    p.ts(eng, tmp_f, tmp_b, -(C1 + 768.0), None, ALU.add)
    p.tt(eng, buf, buf, tmp_f, ALU.subtract)
    p.copy(eng, tmp_b, buf)
    p.tt(eng, tmp_f2, tmp_f2, tmp_b, ALU.add)
    p.ts(eng, tmp_f, tmp_b, -192.0, None, ALU.add)
    p.tt(eng, buf, buf, tmp_f, ALU.subtract)
    p.copy(eng, tmp_b, buf)
    p.tt(eng, tmp_f2, tmp_f2, tmp_b, ALU.add)
    p.ts(eng, buf, tmp_f2, -(C1 + 960.0), None, ALU.add)


def quantize_tile(g, eng, buf, scl, inv, lo, hi, tmp_f, tmp_f2, tmp_b):
    p = g.p
    p.ts(eng, buf, buf, scl, None, ALU.mult)
    round_even(g, eng, buf, tmp_f, tmp_f2, tmp_b)
    p.ts(eng, buf, buf, lo, hi, ALU.max, ALU.min)
    p.ts(eng, buf, buf, inv, None, ALU.mult)


def fold_partitions(g, buf, tmp, alu=ALU.max, width=128):
    p = g.p
    while width > 1:
        half = width // 2
        p.dma("ge", tmp[0:half, :], buf[half:width, :])
        p.tt("ve", buf[0:half, :], buf[0:half, :], tmp[0:half, :], alu)
        width = half
    return buf[0:1, :]


def bounce_rows(g, rowV, row_idx, col0, n, out128):
    p = g.p
    dst = g.rows.v(slice(row_idx, row_idx + 1), slice(col0, col0 + n),
                   key=(row_idx, col0))
    p.dma("se", dst, rowV)
    p.dma("sy", out128, V(dst.ap.broadcast_to([128, n]), dst.keys))


def emit_ln(g, xin, xout=None, gamma=None, beta=None, add=None):
    p = g.p
    if xout is None:
        xout = g.xnf
    for s in range(NSLAB):
        p.dma("sy" if s % 2 else "se", g.XA.v3(0, KD, SLAB), xin.slab3(0, KD, s))
        xs = [g.XA.v(c * SLAB, SLAB) for c in range(KD)]
        if add is not None:
            p.dma("ge", g.XB.v3(4096, KD, SLAB), add.slab3(0, KD, s))
            for c in range(KD):
                p.tt("ve" if c % 2 else "ge", xs[c], xs[c],
                     g.XB.v(4096 + c * SLAB, SLAB), ALU.add)
        ps_s = g.ps[0].v(0, SLAB, 0, 1)
        ps_q = g.ps[1].v(0, SLAB, 0, 1)
        sq = g.XB.v(0, SLAB)
        for c in range(KD):
            p.mm(ps_s, g.sb_ones, xs[c], start=(c == 0), stop=(c == KD - 1))
        for c in range(KD):
            p.tt("ve", sq, xs[c], xs[c], ALU.mult)
            p.mm(ps_q, g.sb_ones, sq, start=(c == 0), stop=(c == KD - 1))
        mean = g.XB.v(512, SLAB, 0, 1)
        msq = g.XB.v(1024, SLAB, 0, 1)
        var = g.XB.v(1536, SLAB, 0, 1)
        rstd = g.XB.v(2048, SLAB, 0, 1)
        mr = g.XB.v(2560, SLAB, 0, 1)
        p.ts("ve", mean, ps_s, 1.0 / D, None, ALU.mult)
        p.tt("ve", msq, mean, mean, ALU.mult)
        p.stt("ve", var, ps_q, 1.0 / D, msq, ALU.mult, ALU.subtract)
        sdt = g.XB.v(3584, SLAB, 0, 1)
        p.act(sdt, var, AF.Sqrt, bias=ccv(g, "eps_ln", 1))
        p.recip(rstd, sdt)
        p.tt("ve", mr, mean, rstd, ALU.mult)
        rstd_b = g.XB.v(3072, SLAB)
        mr_b = g.XB.v(3584, SLAB)
        bounce_rows(g, rstd, 0, s * SLAB, SLAB, rstd_b)
        bounce_rows(g, mr, 1, s * SLAB, SLAB, mr_b)
        for c in range(KD):
            t = g.XB.v(4096 + c * SLAB, SLAB)
            eng = "ve" if c % 2 else "ge"
            p.tt(eng, t, xs[c], rstd_b, ALU.mult)
            p.tt(eng, t, t, mr_b, ALU.subtract)
            if gamma is not None:
                p.ts(eng, t, t, V(gamma.ap[:, c:c + 1], gamma.keys),
                     V(beta.ap[:, c:c + 1], beta.keys), ALU.mult, ALU.add)
        p.dma("se", xout.slab3(0, KD, s), g.XB.v3(4096, KD, SLAB))


def emit_l0_attn(g):
    p = g.p

    def load_w(wdram, tag):
        out = []
        for c in range(KD):
            vdst = g.WA.v(c * D, D)
            p.dma("sy", vdst, wdram.v(slice(c * 128, c * 128 + 128),
                                      slice(0, D), key=(tag, c)))
            out.append(vdst)
        return out

    for which, wdram, bias_sb, Cd, Sd, outd in (
        ("q", g.l0_wqT, g.sb_bq, g.ropeCq, g.ropeSq, g.qrope),
        ("k", g.l0_wkT, g.sb_bk, g.ropeCk, g.ropeSk, g.krope),
    ):
        wch = load_w(wdram, which)
        for s in range(NSLAB):
            p.dma("sy", g.XA.v3(0, KD, SLAB), g.xnf.slab3(0, KD, s))
            xs = [g.XA.v(c * SLAB, SLAB) for c in range(KD)]
            ct = g.WB.v(4096, SLAB)
            st_ = g.WB.v(4608, SLAB)
            p.dma("sy", ct, Cd.v(slice(0, 128),
                                 slice(s * SLAB, (s + 1) * SLAB), key=s))
            p.dma("sy", st_, Sd.v(slice(0, 128),
                                  slice(s * SLAB, (s + 1) * SLAB), key=s))
            for j in range(KD):
                psb = g.ps[j % 4]
                for c in range(KD):
                    p.mm(psb.v(), V(wch[c].ap[:, j * 128:(j + 1) * 128],
                                    wch[c].keys),
                         xs[c], start=(c == 0), stop=(c == KD - 1))
                qt = g.XB.v(j * SLAB, SLAB)
                p.act(qt, psb.v(), AF.Identity,
                      bias=V(bias_sb.ap[:, j:j + 1], bias_sb.keys))
                sw = g.XB.v(4096 + (j % 4) * SLAB, SLAB)
                for b4 in range(4):
                    p.dma("ge", sw[b4 * 32:(b4 + 1) * 32, :],
                          qt[(b4 ^ 1) * 32:((b4 ^ 1) + 1) * 32, :])
                ro = g.YA.v(j * SLAB, SLAB)
                p.tt("ve", ro, qt, ct, ALU.mult)
                p.tt("ve", sw, sw, st_, ALU.mult)
                p.tt("ve", ro, ro, sw, ALU.add)
            p.dma("se", outd.slab3(0, KD, s), g.YA.v3(0, KD, SLAB))

    wv = load_w(g.l0_wvT, "v")
    bvb = g.WB.v(2048, D)
    bvsrc = g.l0_bv_row.whole()
    p.dma("sy", bvb, V(bvsrc.ap.broadcast_to([128, D]), bvsrc.keys))
    for tt_ in range(NTT):
        p.dma("sy", g.XA.v3(0, KD, 128),
              g.xnf.slab3(0, KD, tt_, ns=128, key_s=tt_ // 4))
        xs = [g.XA.v(c * 128, 128) for c in range(KD)]
        for js in range(2):
            psb = g.ps[4 + js]
            for c in range(KD):
                p.mm(psb.v(), xs[c],
                     V(wv[c].ap[:, js * SLAB:(js + 1) * SLAB], wv[c].keys),
                     start=(c == 0), stop=(c == KD - 1))
            vt = g.YA.v(2048 + js * SLAB, SLAB)
            p.tt("ve", vt, psb.v(),
                 V(bvb.ap[:, js * SLAB:(js + 1) * SLAB], bvb.keys), ALU.add)
        p.dma("se", g.vtok.v(slice(tt_ * 128, (tt_ + 1) * 128), slice(0, D),
                             key=(tt_,)), g.YA.v(2048, 2 * SLAB))

    mks = g.WB.v(0, 2048)
    dsrc = g.dmask.t.ap().transpose([1, 0, 2])
    p.dma("sy", V(mks.ap.rearrange("p (m c) -> p m c", c=512), mks.keys),
          V(dsrc, [("dmask",)]))
    for hp in range(8):
        qh2 = g.XA.v(0, T)
        kh2 = g.XA.v(4096, T)
        vh2 = g.XB.v(0, T)
        p.dma("sy", qh2, g.qrope.v(slice(hp * 128, (hp + 1) * 128), slice(0, T),
                                   key=[(hp, s) for s in range(NSLAB)]))
        p.dma("sy", kh2, g.krope.v(slice(hp * 128, (hp + 1) * 128), slice(0, T),
                                   key=[(hp, s) for s in range(NSLAB)]))
        vsrc = g.vtok.t.ap().rearrange("(a p) d -> p a d", p=128)
        vsrc = vsrc[:, :, hp * 128:(hp + 1) * 128]
        p.dma("sy", vh2, V(vsrc, [("vtok", t2) for t2 in range(NTT)]))
        for qt_ in range(NTT):
            hold2 = g.YA.v(0, 128)
            for h01 in range(2):
                acc = g.YA.v(512, 64)
                nm = [g.SM.v(512, 1), g.SM.v(520, 1)]
                lsum = g.SM.v(528, 1)
                alpha = g.SM.v(536, 1)
                rs = g.SM.v(544, 1)
                p.memset("ve", acc, 0.0)
                p.memset("ve", nm[0], 3.0e38)
                p.memset("ve", lsum, 0.0)
                nks = qt_ // 4 + 1
                for ks in range(nks):
                    pss = g.ps[0]
                    p.mm(pss.v(),
                         V(qh2.ap[h01 * 64:(h01 + 1) * 64,
                                  qt_ * 128:(qt_ + 1) * 128], qh2.keys),
                         V(kh2.ap[h01 * 64:(h01 + 1) * 64,
                                  ks * SLAB:(ks + 1) * SLAB], kh2.keys),
                         start=True, stop=True)
                    if ks == qt_ // 4:
                        p.tt("ve", pss.v(), pss.v(),
                             V(mks.ap[:, (qt_ % 4) * SLAB:
                                      ((qt_ % 4) + 1) * SLAB], mks.keys),
                             ALU.add)
                    nmo, nmn = nm[ks % 2], nm[(ks + 1) % 2]
                    nmt = g.SM.v(552, 1)
                    p.red(nmt, pss.v(), ALU.max, negate=True)
                    p.tt("ve", nmn, nmo, nmt, ALU.min)
                    p.act(alpha, nmo, AF.Exp, bias=nmn, scale=-1.0)
                    p_sb = g.XB.v(4096, SLAB)
                    p.act(p_sb, pss.v(), AF.Exp, bias=nmn, accum=rs)
                    p.stt("ve", lsum, lsum, alpha, rs, ALU.mult, ALU.add)
                    pst = g.ps[1]
                    for i4 in range(4):
                        p.transp(pst.v(i4 * 128, 128),
                                 V(p_sb.ap[:, i4 * 128:(i4 + 1) * 128],
                                   p_sb.keys), g.sb_ident)
                    pT = g.XB.v(4608, SLAB)
                    p.copy("se", pT, pst.v())
                    psv = g.ps[2]
                    for i4 in range(4):
                        kk = ks * 4 + i4
                        p.mm(psv.v(0, 64),
                             V(pT.ap[:, i4 * 128:(i4 + 1) * 128], pT.keys),
                             V(vh2.ap[:, kk * 128 + h01 * 64:
                                      kk * 128 + h01 * 64 + 64], vh2.keys),
                             start=(i4 == 0), stop=(i4 == 3))
                    p.stt("ve", acc, acc, alpha, psv.v(0, 64),
                          ALU.mult, ALU.add)
                linv = g.SM.v(560, 1)
                p.recip(linv, lsum)
                p.ts("ve", V(hold2.ap[:, h01 * 64:(h01 + 1) * 64], hold2.keys),
                     acc, linv, None, ALU.mult)
            ps3 = g.ps[3]
            p.transp(ps3.v(0, 128), hold2, g.sb_ident)
            outt = g.YA.v(256, 128)
            p.copy("se", outt, ps3.v(0, 128))
            p.dma("se", g.attnf.v(slice(hp * 128, (hp + 1) * 128),
                                  slice(qt_ * 128, (qt_ + 1) * 128),
                                  key=(hp, ("qt", qt_ // 4))), outt)

    wo = load_w(g.l0_woT, "o")
    for s in range(NSLAB):
        p.dma("sy", g.XA.v3(0, KD, SLAB),
              g.attnf.slab3(0, KD, s, key_s=("qt", s)))
        xs = [g.XA.v(c * SLAB, SLAB) for c in range(KD)]
        for j in range(KD):
            psb = g.ps[j % 4]
            for c in range(KD):
                p.mm(psb.v(), V(wo[c].ap[:, j * 128:(j + 1) * 128], wo[c].keys),
                     xs[c], start=(c == 0), stop=(c == KD - 1))
            ot = g.XB.v(j * SLAB, SLAB)
            p.act(ot, psb.v(), AF.Identity,
                  bias=V(g.sb_bo.ap[:, j:j + 1], g.sb_bo.keys))
        p.dma("se", g.sf.slab3(0, KD, s), g.XB.v3(0, KD, SLAB))


def emit_l0_ffn(g):
    p = g.p
    DFF = 4 * D

    def global_amax_allreduce(amx128, idx):
        col = g.SM.v(576 + idx * 64, 1)
        p.red(col, amx128, ALU.max)
        tmp = g.XB.v(3584, 1)
        one_ = fold_partitions(g, col, tmp)
        p.ts("ve", one_, one_, 1e-8, None, ALU.max)
        p.dma("ge", g.cc_in.whole(), g.zeros64.whole())
        ccw = g.cc_in.whole()
        p.dma("ge", V(ccw.ap[0:1, 0:1], ccw.keys), one_)
        p.collective_max(g.cc_in.whole(), g.cc_out.whole())
        cco = g.cc_out.whole()
        gmax = g.SM.v(584 + idx * 64, 1, 0, 1)
        p.dma("ge", gmax, V(cco.ap[0:1, 0:1], cco.keys))
        scl1 = g.SM.v(592 + idx * 64, 1, 0, 1)
        inv1 = g.SM.v(600 + idx * 64, 1, 0, 1)
        p.recip(scl1, gmax)
        p.ts("ve", scl1, scl1, 32767.0, None, ALU.mult)
        p.recip(inv1, scl1)
        sclb = g.SM.v(608 + idx * 64, 1)
        invb = g.SM.v(616 + idx * 64, 1)
        bounce_rows(g, scl1, 2, idx * 8, 1, sclb)
        bounce_rows(g, inv1, 2, idx * 8 + 4, 1, invb)
        return sclb, invb

    amx = g.XB.v(3072, SLAB)
    for s in range(NSLAB):
        p.dma("sy", g.XA.v3(0, KD, SLAB), g.xnf.slab3(0, KD, s))
        for c in range(KD):
            xc = g.XA.v(c * SLAB, SLAB)
            ab = g.XB.v(2560, SLAB)
            dve_abs(g, "ve", ab, xc, g.XB.v(2048, SLAB))
            if s == 0 and c == 0:
                p.copy("ve", amx, ab)
            else:
                p.tt("ve", amx, amx, ab, ALU.max)
    sclb, invb = global_amax_allreduce(amx, 0)
    for s in range(NSLAB):
        p.dma("sy", g.XA.v3(0, KD, SLAB), g.xnf.slab3(0, KD, s))
        for c in range(KD):
            xc = g.XA.v(c * SLAB, SLAB)
            eng = "ve" if c % 2 else "ge"
            tmp_f = g.XB.v((c % 4) * SLAB, SLAB)
            tmp_f2 = g.XB.v(2048 + (c % 4) * SLAB, SLAB)
            tmp_b = g.BB.v((c % 4) * SLAB, SLAB)
            quantize_tile(g, eng, xc, sclb, invb, -32768.0, 32767.0,
                          tmp_f, tmp_f2, tmp_b)
        p.dma("se", g.xsimf.slab3(0, KD, s), g.XA.v3(0, KD, SLAB))

    wamx = g.XB.v(0, DFF)
    for c in range(KD):
        wc = g.XA.v((c % 2) * DFF, DFF)
        p.dma("sy", wc, g.l0_win.v(slice(c * 128, c * 128 + 128),
                                   slice(0, DFF), key=("w", c)))
        ab = g.XB.v(DFF, DFF)
        dve_abs(g, "ve", ab, wc, g.YA.v(0, DFF))
        if c == 0:
            p.copy("ve", wamx, ab)
        else:
            p.tt("ve", wamx, wamx, ab, ALU.max)
    wtmp = g.YA.v(0, DFF)
    arow = fold_partitions(g, wamx, wtmp)
    p.ts("ve", arow, arow, 1e-8, None, ALU.max)
    srow = g.XB.v(DFF, DFF, 0, 1)
    p.recip(srow, arow)
    p.ts("ve", srow, srow, 127.0, None, ALU.mult)
    irow = g.YA.v(0, DFF, 0, 1)
    p.recip(irow, srow)
    dstS = g.rows.v(slice(3, 4), slice(0, DFF), key=(3, 0))
    p.dma("se", dstS, srow)
    dstI = g.rows.v(slice(4, 5), slice(0, DFF), key=(4, 0))
    p.dma("se", dstI, irow)
    sclwb = g.XB.v(0, DFF)
    invwb = g.XB.v(DFF, DFF)
    p.dma("sy", sclwb, V(dstS.ap.broadcast_to([128, DFF]), dstS.keys))
    p.dma("sy", invwb, V(dstI.ap.broadcast_to([128, DFF]), dstI.keys))
    for c in range(KD):
        wc = g.XA.v((c % 2) * DFF, DFF)
        p.dma("sy", wc, g.l0_win.v(slice(c * 128, c * 128 + 128),
                                   slice(0, DFF), key=("w", c)))
        eng = "ve" if c % 2 else "ge"
        p.tt(eng, wc, wc, sclwb, ALU.mult)
        tmp_f = g.YA.v(0, DFF)
        tmp_f2 = g.WB.v(4096, DFF)
        tmp_b = g.BB.v(0, DFF)
        round_even(g, eng, wc, tmp_f, tmp_f2, tmp_b)
        p.ts(eng, wc, wc, -128.0, 127.0, ALU.max, ALU.min)
        p.tt(eng, wc, wc, invwb, ALU.mult)
        p.dma("se", g.w_in_sim.v(slice(c * 128, c * 128 + 128),
                                 slice(0, DFF), key=("w", c)), wc)

    amx2 = g.WB.v(0, SLAB)
    first = [True]
    for fg in range(4):
        wg = []
        for c in range(KD):
            wgc = g.WA.v(c * D, D)
            p.dma("sy", wgc, g.w_in_sim.v(slice(c * 128, c * 128 + 128),
                                          slice(fg * D, (fg + 1) * D),
                                          key=("w", c)))
            wg.append(wgc)
        for s in range(NSLAB):
            p.dma("sy", g.XA.v3(0, KD, SLAB), g.xsimf.slab3(0, KD, s))
            xs = [g.XA.v(c * SLAB, SLAB) for c in range(KD)]
            for jf in range(KD):
                ffc = fg * 8 + jf
                psb = g.ps[jf]
                for c in range(KD):
                    p.mm(psb.v(), V(wg[c].ap[:, jf * 128:(jf + 1) * 128],
                                    wg[c].keys),
                         xs[c], start=(c == 0), stop=(c == KD - 1))
                hc = g.XB.v(jf * SLAB, SLAB)
                p.act(hc, psb.v(), AF.Gelu,
                      bias=V(g.sb_bin.ap[:, ffc:ffc + 1], g.sb_bin.keys))
                ab2 = g.WB.v(512, SLAB)
                dve_abs(g, "ve", ab2, hc, g.WB.v(1024, SLAB))
                if first[0]:
                    p.copy("ve", amx2, ab2)
                    first[0] = False
                else:
                    p.tt("ve", amx2, amx2, ab2, ALU.max)
            p.dma("se", g.hidf.slab3(fg * KD, KD, s), g.XB.v3(0, KD, SLAB))
    scl2, inv2 = global_amax_allreduce(amx2, 1)

    for c in range(4 * KD):
        wc = g.XA.v((c % 4) * D, D)
        p.dma("sy", wc, g.l0_wout.v(slice(c * 128, c * 128 + 128),
                                    slice(0, D), key=("w", c)))
        am = g.SM.v(704, 1)
        p.red(am, wc, ALU.max, absval=True)
        p.ts("ve", am, am, 1e-8, None, ALU.max)
        sc = g.SM.v(712, 1)
        iv = g.SM.v(720, 1)
        p.recip(sc, am)
        p.ts("ve", sc, sc, 127.0, None, ALU.mult)
        p.recip(iv, sc)
        eng = "ve" if c % 2 else "ge"
        p.ts(eng, wc, wc, sc, None, ALU.mult)
        tmp_f = g.XB.v((c % 4) * D, D)
        tmp_f2 = g.XB.v(4096 + (c % 4) * D, D)
        tmp_b = g.BB.v((c % 2) * D, D)
        round_even(g, eng, wc, tmp_f, tmp_f2, tmp_b)
        p.ts(eng, wc, wc, -128.0, 127.0, ALU.max, ALU.min)
        p.ts(eng, wc, wc, iv, None, ALU.mult)
        p.dma("se", g.w_out_sim.v(slice(c * 128, c * 128 + 128),
                                  slice(0, D), key=("w", c)), wc)

    for s in range(NSLAB):
        p.dma("sy", g.WA.v3(0, 4 * KD, SLAB), g.hidf.slab3(0, 4 * KD, s))
        hs = []
        for k in range(4 * KD):
            hc = g.WA.v(k * SLAB, SLAB)
            eng = "ve" if k % 2 else "ge"
            tmp_f = g.XB.v((k % 8) * SLAB, SLAB)
            tmp_f2 = g.XB.v(4096 + (k % 8) * SLAB, SLAB)
            tmp_b = g.BB.v((k % 8) * SLAB, SLAB)
            quantize_tile(g, eng, hc, scl2, inv2, -32768.0, 32767.0,
                          tmp_f, tmp_f2, tmp_b)
            hs.append(hc)
        for k in range(4 * KD):
            wk = g.XA.v((k % 2) * D, D)
            p.dma("sy", wk, g.w_out_sim.v(slice(k * 128, k * 128 + 128),
                                          slice(0, D), key=("w", k)))
            for j in range(KD):
                p.mm(g.ps[j].v(), V(wk.ap[:, j * 128:(j + 1) * 128], wk.keys),
                     hs[k], start=(k == 0), stop=(k == 4 * KD - 1))
        for j in range(KD):
            ot = g.XB.v(4096 + j * SLAB, SLAB)
            p.act(ot, g.ps[j].v(), AF.Silu,
                  bias=V(g.sb_bout.ap[:, j:j + 1], g.sb_bout.keys))
        p.dma("se", g.gf.slab3(0, KD, s), g.XB.v3(4096, KD, SLAB))



def emit_proj(g, xin, wd, wrow0, nkc, njc, outd, evac, in_keys=None):
    p = g.p
    wch = []
    ncols = njc * 128
    for c in range(nkc):
        vdst = g.WA.v(c * ncols, ncols)
        p.dma("sy", vdst, wd.v(slice(wrow0 + c * 128, wrow0 + (c + 1) * 128),
                               slice(0, ncols), key=("w", wrow0, c)))
        wch.append(vdst)
    for s in range(NSLAB):
        half = (s % 2) * (nkc * SLAB) if nkc <= 8 else 0
        p.dma("sy" if s % 2 else "se", g.XA.v3(half, nkc, SLAB),
              xin.slab3(0, nkc, s))
        xs = [g.XA.v(half + c * SLAB, SLAB) for c in range(nkc)]
        for j in range(njc):
            psb = g.ps[j % 4]
            for c in range(nkc):
                p.mm(psb.v(), V(wch[c].ap[:, j * 128:(j + 1) * 128], wch[c].keys),
                     xs[c], start=(c == 0), stop=(c == nkc - 1))
            evac(j, psb.v(), g.XB.v(j * SLAB, SLAB))
        p.dma("se" if s % 2 else "ge", outd.slab3(0, njc, s), g.XB.v3(0, njc, SLAB))


def ev_act(g, func, bias_sb, scale=1.0):
    def f(j, ps, dst):
        g.p.act(dst, ps, func, bias=V(bias_sb.ap[:, j:j + 1], bias_sb.keys),
                scale=scale)
    return f


def ev_softplus(g, bias_sb):
    def f(j, ps, dst):
        p = g.p
        pre = g.YA.v(0, SLAB)
        ab = g.YA.v(512, SLAB)
        lg = g.YA.v(1024, SLAB)
        p.act(pre, ps, AF.Identity, bias=V(bias_sb.ap[:, j:j + 1], bias_sb.keys))
        dve_abs(g, "ve", ab, pre, g.YA.v(1536, SLAB))
        p.act(ab, ab, AF.Exp, scale=-1.0)
        p.act(lg, ab, AF.Ln, bias=ccv(g, "one"))
        p.ts("ve", dst, pre, 0.0, None, ALU.max)
        p.tt("ve", dst, dst, lg, ALU.add)
    return f


def emit_reverse(g, srcd, dstd):
    p = g.p
    for s in range(NSLAB):
        p.dma("sy", g.XA.v3(0, KD, SLAB), srcd.slab3(0, KD, s))
        for c in range(KD):
            xc = g.XA.v(c * SLAB, SLAB)
            t = g.XB.v(c * SLAB, SLAB)
            p.copy("ve" if c % 2 else "ge", t, V(xc.ap[:, ::-1], xc.keys))
        p.dma("se", dstd.slab3(0, KD, NSLAB - 1 - s), g.XB.v3(0, KD, SLAB))


def emit_l1_attn(g):
    p = g.p
    g.sb_l1g = g.SM.v(256, 8)
    g.sb_l1b = g.SM.v(264, 8)
    p.dma("sy", g.sb_l1g, g.l1a_g.whole())
    p.dma("sy", g.sb_l1b, g.l1a_b.whole())
    emit_reverse(g, g.xnf, g.sc0)
    hfin = {}
    for d_ in ("fwd", "bwd"):
        bz_sb = g.SM.v(272, 32)
        bh_sb = g.SM.v(304, 32)
        p.dma("sy", bz_sb, getattr(g, f"l1a_bz_{d_}").whole())
        p.dma("sy", bh_sb, getattr(g, f"l1a_bh_{d_}").whole())
        hcur = g.xnf if d_ == "fwd" else g.sc0
        ping = [g.sc1, g.sc2] if d_ == "fwd" else [g.sc3, g.sc4]
        wz = getattr(g, f"l1a_wz_{d_}")
        wh = getattr(g, f"l1a_wh_{d_}")
        for i in range(4):
            emit_proj(g, hcur, wz, i * D, KD, KD, g.sc5,
                      ev_act(g, AF.Sigmoid, V(bz_sb.ap[:, i * 8:(i + 1) * 8],
                                              bz_sb.keys)))
            emit_proj(g, hcur, wh, i * D, KD, KD, g.sc6,
                      ev_act(g, AF.Tanh, V(bh_sb.ap[:, i * 8:(i + 1) * 8],
                                           bh_sb.keys)))
            for c in range(KD):
                zc = g.XA.v(0, T)
                hc = g.XA.v(4096, T)
                p.dma("sy", zc, g.sc5.v(slice(c * 128, (c + 1) * 128), slice(0, T),
                                        key=[(c, s) for s in range(NSLAB)]))
                p.dma("se", hc, g.sc6.v(slice(c * 128, (c + 1) * 128), slice(0, T),
                                        key=[(c, s) for s in range(NSLAB)]))
                p.tt("ve", hc, zc, hc, ALU.mult)
                p.act(zc, zc, AF.Identity, bias=ccv(g, "one"), scale=-1.0)
                p.ts("ve", zc, zc, 1e-8, 1.0 - 1e-8, ALU.max, ALU.min)
                sc = g.XB.v(0, T)
                p.scan(sc, zc, hc, 0.0)
                p.dma("ge", g.sc7.v(slice(c * 128, (c + 1) * 128), slice(0, T),
                                    key=[(c, s) for s in range(NSLAB)]), sc)
            nxt = ping[i % 2]
            emit_ln(g, g.sc7, nxt, gamma=g.sb_l1g, beta=g.sb_l1b,
                    add=(hcur if i > 0 else None))
            hcur = nxt
        hfin[d_] = hcur
    emit_reverse(g, hfin["bwd"], g.sc0)
    g.sb_fb = g.SM.v(272, 8)
    p.dma("sy", g.sb_fb, g.l1a_fb.whole())
    wch = []
    for c in range(16):
        vdst = g.WA.v(c * D, D)
        p.dma("sy", vdst, g.l1a_fusionT.v(slice(c * 128, (c + 1) * 128),
                                          slice(0, D), key=("w", c)))
        wch.append(vdst)
    for s in range(NSLAB):
        p.dma("sy", g.XA.v3(0, KD, SLAB), hfin["fwd"].slab3(0, KD, s))
        p.dma("se", g.XA.v3(4096, KD, SLAB), g.sc0.slab3(0, KD, s))
        xs = [g.XA.v(c * SLAB, SLAB) for c in range(16)]
        for j in range(KD):
            psb = g.ps[j % 4]
            for c in range(16):
                p.mm(psb.v(), V(wch[c].ap[:, j * 128:(j + 1) * 128], wch[c].keys),
                     xs[c], start=(c == 0), stop=(c == 15))
            p.act(g.XB.v(j * SLAB, SLAB), psb.v(), AF.Identity,
                  bias=V(g.sb_fb.ap[:, j:j + 1], g.sb_fb.keys))
        p.dma("ge", g.sf.slab3(0, KD, s), g.XB.v3(0, KD, SLAB))


def emit_l1_ffn(g):
    p = g.p
    sl = g.XB.v(0, D, 0, 64)
    p.dma("sy", sl, g.l1f_slot.whole())
    tmp = g.XB.v(1024, D)
    msum = fold_partitions(g, g.XB.v(0, D), tmp, alu=ALU.add, width=64)
    p.ts("ve", msum, msum, 1.0 / 64.0, None, ALU.mult)
    dstm = g.rows.v(slice(5, 6), slice(0, D), key=(5, 0))
    p.dma("se", dstm, msum)
    mcol = g.SM.v(336, 8)
    p.dma("sy", mcol, V(dstm.ap.rearrange("o (c p) -> (o p) c", p=128), dstm.keys),
          allow_slow_non_contiguous=True)
    hb = g.SM.v(344, 4)
    b1sb = g.SM.v(352, 4)
    p.dma("sy", b1sb, g.l1f_b1.whole())
    for jc in range(4):
        psb = g.ps[4]
        for c in range(KD):
            wv_ = g.XB.v(2048 + c * 128, 128)
            p.dma("sy", wv_, g.l1f_w1b.v(slice(c * 128, (c + 1) * 128),
                                         slice(jc * 128, (jc + 1) * 128),
                                         key=("w", c, jc)))
            p.mm(psb.v(0, 1), wv_, V(mcol.ap[:, c:c + 1], mcol.keys),
                 start=(c == 0), stop=(c == KD - 1))
        p.act(V(hb.ap[:, jc:jc + 1], hb.keys), psb.v(0, 1), AF.Identity,
              bias=V(b1sb.ap[:, jc:jc + 1], b1sb.keys))
    w1a_sb = []
    for c in range(KD):
        vdst = g.XB.v(4096 + c * 512, 512)
        p.dma("sy", vdst, g.l1f_w1a.v(slice(c * 128, (c + 1) * 128),
                                      slice(0, 512), key=("w", c)))
        w1a_sb.append(vdst)
    for s in range(NSLAB):
        p.dma("sy", g.XA.v3(0, KD, SLAB), g.xnf.slab3(0, KD, s))
        xs = [g.XA.v(c * SLAB, SLAB) for c in range(KD)]
        for j in range(4):
            psb = g.ps[j % 2]
            for c in range(KD):
                p.mm(psb.v(), V(w1a_sb[c].ap[:, j * 128:(j + 1) * 128],
                                w1a_sb[c].keys),
                     xs[c], start=(c == 0), stop=(c == KD - 1))
            p.act(g.WA.v(j * T + s * SLAB, SLAB), psb.v(), AF.Gelu,
                  bias=V(hb.ap[:, j:j + 1], hb.keys))
    w2sb = g.SM.v(360, 256, 0, 128)
    w2sb = g.SM.v(360, 256)
    for c in range(4):
        p.dma("sy", V(w2sb.ap[:, c * 64:(c + 1) * 64], w2sb.keys),
              g.l1f_w2.v(slice(c * 128, (c + 1) * 128), slice(0, 64),
                         key=("w", c)))
    b2b = g.SM.v(616, 64)
    b2w = g.l1f_b2row.whole()
    p.dma("sy", b2b, V(b2w.ap.broadcast_to([128, 64]), b2w.keys))
    tausb = g.SM.v(680, 1, 0, 1)
    p.dma("sy", tausb, g.l1f_tau.whole())
    dve_abs(g, "ve", tausb, tausb, g.SM.v(760, 1, 0, 1))
    p.ts("ve", tausb, tausb, 0.1, None, ALU.add)
    rtau1 = g.SM.v(688, 1, 0, 1)
    p.recip(rtau1, tausb)
    rtaub = g.SM.v(696, 1)
    bounce_rows(g, rtau1, 5, 2048, 1, rtaub)
    ps_si = [g.ps[5], g.ps[6]]
    ps_dn = g.ps[7]
    for tt_ in range(NTT):
        p.dma("sy", g.XA.v3(0, KD, 128),
              g.xnf.slab3(0, KD, tt_, ns=128, key_s=tt_ // 4))
        pst = g.ps[0]
        xnt = g.XB.v(0, 1024)
        for c in range(KD):
            p.transp(pst.v(c % 4 * 128, 128), g.XA.v(c * 128, 128), g.sb_ident)
            p.copy("se", V(xnt.ap[:, c * 128:(c + 1) * 128], xnt.keys),
                   pst.v(c % 4 * 128, 128))
        ps_lg = g.ps[1]
        for c in range(4):
            hch = g.WA.v(c * T + tt_ * 128, 128)
            p.mm(ps_lg.v(0, 64), hch,
                 V(w2sb.ap[:, c * 64:(c + 1) * 64], w2sb.keys),
                 start=(c == 0), stop=(c == 3))
        lg = g.XB.v(1024, 64)
        p.tt("ve", lg, ps_lg.v(0, 64), b2b, ALU.add)
        p.ts("ve", lg, lg, rtaub, None, ALU.mult)
        mx8 = g.SM.v(704, 8)
        p.max8(mx8, lg)
        nm1 = g.SM.v(712, 1)
        p.ts("ve", nm1, V(mx8.ap[:, 0:1], mx8.keys), -1.0, None, ALU.mult)
        sel = g.XB.v(1088, 64)
        p.ts("ve", sel, lg, V(mx8.ap[:, 7:8], mx8.keys), None, ALU.is_ge)
        ex = g.XB.v(1152, 64)
        p.act(ex, lg, AF.Exp, bias=nm1)
        p.tt("ve", ex, ex, sel, ALU.mult)
        ssum = g.SM.v(720, 1)
        p.red(ssum, ex, ALU.add)
        rsum = g.SM.v(728, 1)
        p.recip(rsum, ssum)
        alph = g.XB.v(1216, 64)
        p.ts("ve", alph, ex, rsum, None, ALU.mult)
        p.dma("se", g.alpha_t.v(slice(tt_ * 128, (tt_ + 1) * 128), slice(0, 64),
                                key=(tt_,)), alph)
        ps_at = g.ps[2]
        p.transp(ps_at.v(0, 128, 0, 64), alph, g.sb_ident)
        p.copy("se", V(g.YA.v(tt_ * 128, 128).ap[0:64, :], g.YA.v(tt_ * 128, 128).keys),
               ps_at.v(0, 128, 0, 64))
        for half in range(2):
            p.mm(ps_si[half].v(0, 512, 0, 64), alph,
                 V(xnt.ap[:, half * 512:(half + 1) * 512], xnt.keys),
                 start=(tt_ == 0), stop=(tt_ == NTT - 1))
        p.mm(ps_dn.v(0, 1, 0, 64), alph, g.sb_ones,
             start=(tt_ == 0), stop=(tt_ == NTT - 1))
    ss = g.XB.v(0, 1024, 0, 64)
    p.copy("se", V(ss.ap[:, 0:512], ss.keys), ps_si[0].v(0, 512, 0, 64))
    p.copy("se", V(ss.ap[:, 512:1024], ss.keys), ps_si[1].v(0, 512, 0, 64))
    dcol = g.SM.v(736, 1, 0, 64)
    p.act(dcol, ps_dn.v(0, 1, 0, 64), AF.Identity, bias=1e-8 if False else 0.0)
    p.ts("ve", dcol, dcol, 1e-8, None, ALU.add)
    rd = g.SM.v(744, 1, 0, 64)
    p.recip(rd, dcol)
    p.ts("ve", ss, ss, rd, None, ALU.mult)
    siT = g.XB.v(1024, 512)
    slT = g.XB.v(1536, 512)
    slin = g.XB.v(2048, 1024, 0, 64)
    p.dma("sy", slin, g.l1f_slot.whole())
    pst = g.ps[0]
    for c in range(KD):
        p.transp(pst.v(0, 64), V(ss.ap[:, c * 128:(c + 1) * 128], ss.keys),
                 ident64(g))
        p.copy("se", V(siT.ap[:, c * 64:(c + 1) * 64], siT.keys),
               pst.v(0, 64))
        p.transp(pst.v(64, 64),
                 V(slin.ap[:, c * 128:(c + 1) * 128], slin.keys), ident64(g))
        p.copy("se", V(slT.ap[:, c * 64:(c + 1) * 64], slT.keys),
               pst.v(64, 64))
    bih_sb = g.SM.v(752, 24)
    bhh_sb = g.SM.v(776, 24)
    p.dma("sy", bih_sb, g.l1f_bih.whole())
    p.dma("sy", bhh_sb, g.l1f_bhh.whole())
    gi = g.XB.v(2048 + 1024, 24 * 64)
    gh = g.XB.v(2048 + 1024 + 24 * 64, 24 * 64)
    for src_sb, wdr, bias, dst in ((siT, g.l1f_wihT, bih_sb, gi),
                                   (slT, g.l1f_whhT, bhh_sb, gh)):
        for m in range(24):
            psb = g.ps[1 + (m % 2)]
            for c in range(KD):
                wv_ = g.XA.v((c % 4) * 128, 128)
                p.dma("sy", wv_, wdr.v(slice(c * 128, (c + 1) * 128),
                                       slice(m * 128, (m + 1) * 128),
                                       key=("w", c, m)))
                p.mm(psb.v(0, 64), wv_,
                     V(src_sb.ap[:, c * 64:(c + 1) * 64], src_sb.keys),
                     start=(c == 0), stop=(c == KD - 1))
            p.act(V(dst.ap[:, m * 64:(m + 1) * 64], dst.keys), psb.v(0, 64),
                  AF.Identity, bias=V(bias.ap[:, m:m + 1], bias.keys))
    Snew = g.YA.v(0, 512)
    Snew = g.XB.v(2048 + 1024 + 48 * 64, 512)
    for c in range(KD):
        ir = V(gi.ap[:, c * 64:(c + 1) * 64], gi.keys)
        iz = V(gi.ap[:, (8 + c) * 64:(9 + c) * 64], gi.keys)
        inn = V(gi.ap[:, (16 + c) * 64:(17 + c) * 64], gi.keys)
        hr = V(gh.ap[:, c * 64:(c + 1) * 64], gh.keys)
        hz = V(gh.ap[:, (8 + c) * 64:(9 + c) * 64], gh.keys)
        hn = V(gh.ap[:, (16 + c) * 64:(17 + c) * 64], gh.keys)
        r_ = g.SM.v(800, 64)
        z_ = g.SM.v(864, 64)
        n_ = g.SM.v(928, 64)
        p.tt("ve", r_, ir, hr, ALU.add)
        p.act(r_, r_, AF.Sigmoid)
        p.tt("ve", z_, iz, hz, ALU.add)
        p.act(z_, z_, AF.Sigmoid)
        p.tt("ve", n_, r_, hn, ALU.mult)
        p.tt("ve", n_, n_, inn, ALU.add)
        p.act(n_, n_, AF.Tanh)
        sc_ = V(slT.ap[:, c * 64:(c + 1) * 64], slT.keys)
        zs = V(Snew.ap[:, c * 64:(c + 1) * 64], Snew.keys)
        p.tt("ve", r_, z_, sc_, ALU.mult)
        p.act(z_, z_, AF.Identity, bias=ccv(g, "one"), scale=-1.0)
        p.tt("ve", z_, z_, n_, ALU.mult)
        p.tt("ve", zs, z_, r_, ALU.add)
    bq_sb = g.SM.v(992, 2); bk_sb = g.SM.v(1000, 2)
    p.dma("sy", bq_sb, g.l1f_bq.whole())
    p.dma("sy", bk_sb, g.l1f_bk.whole())
    qf = g.SM.v(800, 128)
    kf = g.SM.v(0, 0) if False else None
    kf = g.XB.v(2048 + 1024 + 48 * 64 + 512, 128)
    for wdr, bsb, dst in ((g.l1f_wq, bq_sb, qf), (g.l1f_wk, bk_sb, kf)):
        for m in range(2):
            psb = g.ps[3]
            for c in range(KD):
                wv_ = g.XA.v(4096 + (c % 4) * 128, 128)
                p.dma("sy", wv_, wdr.v(slice(c * 128, (c + 1) * 128),
                                       slice(m * 128, (m + 1) * 128),
                                       key=("w", c, m)))
                p.mm(psb.v(0, 64), wv_,
                     V(Snew.ap[:, c * 64:(c + 1) * 64], Snew.keys),
                     start=(c == 0), stop=(c == KD - 1))
            p.act(V(dst.ap[:, m * 64:(m + 1) * 64], dst.keys), psb.v(0, 64),
                  AF.Identity, bias=V(bsb.ap[:, m:m + 1], bsb.keys))
    ps_sc = g.ps[4]
    for m in range(2):
        p.mm(ps_sc.v(0, 64, 0, 64), V(qf.ap[:, m * 64:(m + 1) * 64], qf.keys),
             V(kf.ap[:, m * 64:(m + 1) * 64], kf.keys),
             start=(m == 0), stop=(m == 1))
    att = g.SM.v(864, 64, 0, 64)
    p.act(att, ps_sc.v(0, 64, 0, 64), AF.Identity, scale=0.0625)
    nmx = g.SM.v(928, 1, 0, 64)
    p.red(nmx, att, ALU.max, negate=True)
    rs_ = g.SM.v(936, 1, 0, 64)
    p.act(att, att, AF.Exp, bias=nmx, accum=rs_)
    rr = g.SM.v(944, 1, 0, 64)
    p.recip(rr, rs_)
    p.ts("ve", att, att, rr, None, ALU.mult)
    ps_at2 = g.ps[5]
    p.transp(ps_at2.v(0, 64, 0, 64), att, ident64(g))
    attT = g.SM.v(864, 64, 0, 64)
    p.copy("se", attT, ps_at2.v(0, 64, 0, 64))
    vsm = g.XB.v(0, 1024, 0, 64)
    bvw = g.l1f_bvrow.whole()
    bvb64 = g.XB.v(1024, 1024, 0, 64)
    p.dma("sy", bvb64, V(bvw.ap.broadcast_to([64, D]), bvw.keys))
    for half in range(2):
        psb = g.ps[6]
        for c in range(KD):
            wv_ = g.XA.v((c % 4) * 512, 512)
            p.dma("sy", wv_, g.l1f_wv.v(slice(c * 128, (c + 1) * 128),
                                        slice(half * 512, (half + 1) * 512),
                                        key=("w", c, half)))
            p.mm(psb.v(0, 512, 0, 64),
                 V(Snew.ap[:, c * 64:(c + 1) * 64], Snew.keys), wv_,
                 start=(c == 0), stop=(c == KD - 1))
        p.tt("ve", V(vsm.ap[:, half * 512:(half + 1) * 512], vsm.keys),
             psb.v(0, 512, 0, 64),
             V(bvb64.ap[:, half * 512:(half + 1) * 512], bvb64.keys), ALU.add)
    yv = g.XB.v(2048, 512)
    for c in range(KD):
        psb = g.ps[7]
        p.mm(psb.v(0, 64), V(vsm.ap[:, c * 128:(c + 1) * 128], vsm.keys),
             attT, start=True, stop=True)
        p.tt("ve", V(yv.ap[:, c * 64:(c + 1) * 64], yv.keys), psb.v(0, 64),
             V(Snew.ap[:, c * 64:(c + 1) * 64], Snew.keys), ALU.add)
    ps_s = g.ps[0]; ps_q = g.ps[1]
    sqt = g.SM.v(864, 64)
    for c in range(KD):
        yc = V(yv.ap[:, c * 64:(c + 1) * 64], yv.keys)
        p.mm(ps_s.v(0, 64, 0, 1), g.sb_ones, yc, start=(c == 0), stop=(c == KD - 1))
        p.tt("ve", sqt, yc, yc, ALU.mult)
        p.mm(ps_q.v(0, 64, 0, 1), g.sb_ones, sqt, start=(c == 0), stop=(c == KD - 1))
    mrow = g.SM.v(928, 64, 0, 1)
    vrow = g.SM.v(864, 64, 0, 1)
    p.ts("ve", mrow, ps_s.v(0, 64, 0, 1), 1.0 / D, None, ALU.mult)
    p.ts("ve", vrow, ps_q.v(0, 64, 0, 1), 1.0 / D, None, ALU.mult)
    m2row = g.SM.v(736, 64, 0, 1)
    p.tt("ve", m2row, mrow, mrow, ALU.mult)
    p.tt("ve", vrow, vrow, m2row, ALU.subtract)
    p.act(vrow, vrow, AF.Sqrt, bias=ccv(g, "eps_ln", 1))
    p.recip(vrow, vrow)
    mr2 = g.SM.v(744, 64, 0, 1)
    p.tt("ve", mr2, mrow, vrow, ALU.mult)
    rb = g.XB.v(1024, 64)
    mb = g.XB.v(1088, 64)
    bounce_rows(g, vrow, 5, 1024, 64, rb)
    bounce_rows(g, mr2, 5, 1152, 64, mb)
    lng_sb = g.SM.v(736, 8); lnb_sb = g.SM.v(752, 8)
    p.dma("sy", lng_sb, g.l1f_lng.whole())
    p.dma("sy", lnb_sb, g.l1f_lnb.whole())
    rbf = V(rb.ap[0:128, :], rb.keys)
    mbf = V(mb.ap[0:128, :], mb.keys)
    for c in range(KD):
        yc = V(yv.ap[:, c * 64:(c + 1) * 64], yv.keys)
        p.tt("ve", yc, yc, rbf, ALU.mult)
        p.tt("ve", yc, yc, mbf, ALU.subtract)
        p.ts("ve", yc, yc, V(lng_sb.ap[:, c:c + 1], lng_sb.keys),
             V(lnb_sb.ap[:, c:c + 1], lnb_sb.keys), ALU.mult, ALU.add)
    if DEV_DUMP:
        p.dma("se", g.sc7.v(slice(0, 64), slice(0, 1024), key=("d", 0)), ss)
        p.dma("se", g.sc7.v(slice(128, 256), slice(0, 512), key=("d", 1)), siT)
        p.dma("se", g.sc7.v(slice(256, 384), slice(0, 1024), key=("d", 2)),
              V(gi.ap[:, 0:1024], gi.keys))
        p.dma("se", g.sc7.v(slice(384, 512), slice(0, 1024), key=("d", 3)),
              V(gh.ap[:, 0:1024], gh.keys))
        p.dma("se", g.sc7.v(slice(512, 640), slice(0, 512), key=("d", 4)), Snew)
        p.dma("se", g.sc7.v(slice(640, 704), slice(0, 1024), key=("d", 5)), vsm)
        p.dma("se", g.sc7.v(slice(704, 768), slice(0, 64), key=("d", 6)), att)
        p.dma("se", g.sc7.v(slice(768, 896), slice(0, 512), key=("d", 7)), yv)
    svm = g.XB.v(2560, 1024, 0, 64)
    bvpw = g.l1f_bvprow.whole()
    bvpb = g.XB.v(3584, 1024, 0, 64)
    p.dma("sy", bvpb, V(bvpw.ap.broadcast_to([64, D]), bvpw.keys))
    for half in range(2):
        psb = g.ps[2]
        for c in range(KD):
            wv_ = g.XA.v((c % 4) * 512, 512)
            p.dma("sy", wv_, g.l1f_wvp.v(slice(c * 128, (c + 1) * 128),
                                         slice(half * 512, (half + 1) * 512),
                                         key=("w", c, half)))
            p.mm(psb.v(0, 512, 0, 64),
                 V(yv.ap[:, c * 64:(c + 1) * 64], yv.keys), wv_,
                 start=(c == 0), stop=(c == KD - 1))
        p.tt("ve", V(svm.ap[:, half * 512:(half + 1) * 512], svm.keys),
             psb.v(0, 512, 0, 64),
             V(bvpb.ap[:, half * 512:(half + 1) * 512], bvpb.keys), ALU.add)
    if DEV_DUMP:
        p.dma("se", g.sc7.v(slice(896, 960), slice(0, 1024), key=("d", 8)), svm)
        p.dma("se", g.sc7.v(slice(960, 1024), slice(0, T), key=("d", 9)),
              V(g.YA.v(0, T).ap[0:64, :], g.YA.v(0, T).keys))
    g.sb_l1fbo = g.SM.v(736, 8)
    p.dma("sy", g.sb_l1fbo, g.l1f_bo.whole())
    wo_sb = []
    for c in range(KD):
        vdst = g.WA.v(c * D, D)
        p.dma("sy", vdst, g.l1f_wo.v(slice(c * 128, (c + 1) * 128),
                                     slice(0, D), key=("w", c)))
        wo_sb.append(vdst)
    for s in range(NSLAB):
        of = []
        for c in range(KD):
            psb = g.ps[3 + (c % 2)]
            p.mm(psb.v(),
                 V(svm.ap[:, c * 128:(c + 1) * 128], svm.keys),
                 V(g.YA.v(s * SLAB, SLAB).ap[0:64, :], g.YA.v(s * SLAB, SLAB).keys),
                 start=True, stop=True)
            oc = g.XA.v(c * SLAB, SLAB)
            p.copy("se", oc, psb.v())
            of.append(oc)
        for j in range(KD):
            psb = g.ps[5 + (j % 2)]
            for c in range(KD):
                p.mm(psb.v(), V(wo_sb[c].ap[:, j * 128:(j + 1) * 128],
                                wo_sb[c].keys),
                     of[c], start=(c == 0), stop=(c == KD - 1))
            p.act(g.XB.v(4096 + j * SLAB, SLAB), psb.v(), AF.Silu,
                  bias=V(g.sb_l1fbo.ap[:, j:j + 1], g.sb_l1fbo.keys))
        p.dma("se", g.gf.slab3(0, KD, s), g.XB.v3(4096, KD, SLAB))



def ev_copy(g):
    def f(j, ps, dst):
        g.p.copy("se", dst, ps)
    return f


def emit_addpass(g, a, b, outd, eng_q="se"):
    p = g.p
    for s in range(NSLAB):
        p.dma("sy", g.XA.v3(0, KD, SLAB), a.slab3(0, KD, s))
        p.dma("ge", g.XA.v3(4096, KD, SLAB), b.slab3(0, KD, s))
        for c in range(KD):
            p.tt("ve" if c % 2 else "ge", g.XA.v(c * SLAB, SLAB),
                 g.XA.v(c * SLAB, SLAB), g.XA.v(4096 + c * SLAB, SLAB), ALU.add)
        p.dma(eng_q, outd.slab3(0, KD, s), g.XA.v3(0, KD, SLAB))


def emit_l2_attn(g):
    p = g.p
    ln1g = g.SM.v(256, 8); ln1b = g.SM.v(264, 8)
    ln2g = g.SM.v(272, 8); ln2b = g.SM.v(280, 8)
    inb = g.SM.v(288, 16); dwb = g.SM.v(304, 8)
    dtb = g.SM.v(312, 8); bb_ = g.SM.v(320, 8); cb = g.SM.v(328, 8)
    alog = g.SM.v(336, 8); dskip = g.SM.v(344, 8); outb = g.SM.v(352, 8)
    f1b = g.SM.v(360, 16); f2b = g.SM.v(376, 8)
    dw0 = g.SM.v(384, 8); dw1 = g.SM.v(392, 8); dw2 = g.SM.v(400, 8)
    for sb, dr in ((ln1g, g.l2a_ln1g), (ln1b, g.l2a_ln1b), (ln2g, g.l2a_ln2g),
                   (ln2b, g.l2a_ln2b), (inb, g.l2a_inb), (dwb, g.l2a_dwb),
                   (dtb, g.l2a_dtb), (bb_, g.l2a_bb), (cb, g.l2a_cb),
                   (alog, g.l2a_alog), (dskip, g.l2a_dskip), (outb, g.l2a_outb),
                   (f1b, g.l2a_f1b), (f2b, g.l2a_f2b)):
        p.dma("sy", sb, dr.whole())
    for k3, sb in enumerate((dw0, dw1, dw2)):
        p.dma("sy", sb, g.l2a_dw.v(slice(k3, k3 + 1), key=k3))
    acol = g.SM.v(408, 8)
    p.act(acol, alog, AF.Exp)
    p.ts("ve", acol, acol, -1.0, None, ALU.mult)

    emit_ln(g, g.xnf, g.sc0, gamma=ln1g, beta=ln1b)
    emit_proj(g, g.sc0, g.l2a_inW, 0, KD, 16, g.hidf,
              ev_act(g, AF.Identity, inb))
    for c in range(KD):
        xp = g.XA.v((c % 2) * T, T)
        p.dma("sy", xp, g.hidf.v(slice(c * 128, (c + 1) * 128), slice(0, T),
                                 key=[(c, s) for s in range(NSLAB)]))
        mxc = g.XB.v((c % 2) * T, T)
        eng = "ve" if c % 2 else "ge"
        p.ts(eng, mxc, xp, V(dw1.ap[:, c:c + 1], dw1.keys),
             V(dwb.ap[:, c:c + 1], dwb.keys), ALU.mult, ALU.add)
        p.stt("ve", V(mxc.ap[:, 1:T], mxc.keys), V(xp.ap[:, 0:T - 1], xp.keys),
              V(dw0.ap[:, c:c + 1], dw0.keys), V(mxc.ap[:, 1:T], mxc.keys),
              ALU.mult, ALU.add)
        p.stt("ve", V(mxc.ap[:, 0:T - 1], mxc.keys), V(xp.ap[:, 1:T], xp.keys),
              V(dw2.ap[:, c:c + 1], dw2.keys), V(mxc.ap[:, 0:T - 1], mxc.keys),
              ALU.mult, ALU.add)
        p.dma("se", g.sc1.v(slice(c * 128, (c + 1) * 128), slice(0, T),
                            key=[(c, s) for s in range(NSLAB)]), mxc)
    emit_proj(g, g.sc1, g.l2a_dtW, 0, KD, KD, g.sc2, ev_softplus(g, dtb))
    emit_proj(g, g.sc1, g.l2a_bW, 0, KD, KD, g.sc3, ev_act(g, AF.Tanh, bb_))
    emit_proj(g, g.sc1, g.l2a_cW, 0, KD, KD, g.sc4, ev_act(g, AF.Tanh, cb))
    for c in range(KD):
        ks = [(c, s) for s in range(NSLAB)]
        dt = g.XA.v(0, T); bt = g.XA.v(4096, T)
        ct = g.XB.v(0, T); mxc = g.XB.v(4096, T); gtc = g.YA.v(0, T)
        p.dma("sy", dt, g.sc2.v(slice(c * 128, (c + 1) * 128), slice(0, T), key=ks))
        p.dma("se", bt, g.sc3.v(slice(c * 128, (c + 1) * 128), slice(0, T), key=ks))
        p.dma("ge", ct, g.sc4.v(slice(c * 128, (c + 1) * 128), slice(0, T), key=ks))
        p.dma("sy", mxc, g.sc1.v(slice(c * 128, (c + 1) * 128), slice(0, T), key=ks))
        p.dma("se", gtc, g.hidf.v(slice((8 + c) * 128, (9 + c) * 128), slice(0, T),
                                  key=[(8 + c, s) for s in range(NSLAB)]))
        decay = g.WB.v(0, T)
        p.act(decay, dt, AF.Exp, scale=V(acol.ap[:, c:c + 1], acol.keys))
        p.tt("ve", bt, bt, mxc, ALU.mult)
        p.tt("ve", bt, dt, bt, ALU.mult)
        st_ = g.WB.v(4096, T)
        p.scan(st_, decay, bt, 0.0)
        p.tt("ve", ct, ct, st_, ALU.mult)
        p.stt("ve", ct, mxc, V(dskip.ap[:, c:c + 1], dskip.keys), ct,
              ALU.mult, ALU.add)
        p.act(gtc, gtc, AF.Sigmoid)
        p.tt("ve", ct, ct, gtc, ALU.mult)
        p.dma("ge", g.sc5.v(slice(c * 128, (c + 1) * 128), slice(0, T), key=ks), ct)
    emit_proj(g, g.sc5, g.l2a_outW, 0, KD, KD, g.sc7,
              ev_act(g, AF.Identity, outb))
    emit_addpass(g, g.sc7, g.xnf, g.sc6)
    emit_ln(g, g.sc6, g.sc0, gamma=ln2g, beta=ln2b)
    emit_proj(g, g.sc0, g.l2a_f1W, 0, KD, 16, g.hidf,
              ev_act(g, AF.Gelu, f1b))
    emit_proj(g, g.hidf, g.l2a_f2W, 0, 16, KD, g.sc7,
              ev_act(g, AF.Identity, f2b))
    emit_addpass(g, g.sc7, g.sc6, g.sf)


def emit_l2_ffn(g):
    p = g.p
    rs1 = g.SM.v(416, 1, 0, 1)
    p.dma("sy", rs1, g.l2f_rs.whole())
    rsb = g.SM.v(424, 1)
    bounce_rows(g, rs1, 5, 2052, 1, rsb)
    svsb = []
    sosb = []
    for c in range(KD):
        a1 = g.WB.v(c * 64, 64)
        p.dma("sy", a1, g.l2f_svT.v(slice(c * 128, (c + 1) * 128), slice(0, 64),
                                    key=("wv", c)))
        svsb.append(a1)
        a2 = g.WB.v(512 + c * 64, 64)
        p.dma("sy", a2, g.l2f_soT.v(slice(c * 128, (c + 1) * 128), slice(0, 64),
                                    key=("wo", c)))
        sosb.append(a2)
    for tt_ in range(NTT):
        p.dma("sy", g.XA.v3(0, KD, 128),
              g.xnf.slab3(0, KD, tt_, ns=128, key_s=tt_ // 4))
        psv = g.ps[0]; pso = g.ps[1]
        for c in range(KD):
            xc = g.XA.v(c * 128, 128)
            p.mm(psv.v(0, 64), xc, svsb[c], start=(c == 0), stop=(c == KD - 1))
            p.mm(pso.v(0, 64), xc, sosb[c], start=(c == 0), stop=(c == KD - 1))
        for pgate, outd in ((psv, g.gv_t), (pso, g.go_t)):
            pr = g.XB.v(4096, 64)
            p.act(pr, pgate.v(0, 64), AF.Sigmoid)
            gt = g.XB.v(4160, 64)
            for hh in range(8):
                prh = V(pr.ap[:, hh * 8:(hh + 1) * 8], pr.keys)
                mx8 = g.SM.v(520, 8)
                p.max8(mx8, prh)
                ssum = g.SM.v(528, 1)
                p.tt("ve", ssum, V(mx8.ap[:, 0:1], mx8.keys),
                     V(mx8.ap[:, 1:2], mx8.keys), ALU.add)
                p.ts("ve", ssum, ssum, 1e-9, None, ALU.max)
                rr = g.SM.v(536, 1)
                p.recip(rr, ssum)
                p.ts("ve", rr, rr, rsb, None, ALU.mult)
                gh = V(gt.ap[:, hh * 8:(hh + 1) * 8], gt.keys)
                p.ts("ve", gh, prh, V(mx8.ap[:, 1:2], mx8.keys), None, ALU.is_ge)
                p.tt("ve", gh, gh, prh, ALU.mult)
                p.ts("ve", gh, gh, rr, None, ALU.mult)
            p.dma("se", outd.v(slice(tt_ * 128, (tt_ + 1) * 128), slice(0, 64),
                               key=(tt_,)), gt)
    emit_proj(g, g.xnf, g.l2f_qT, 0, KD, KD, g.sc0, ev_copy(g))
    emit_proj(g, g.xnf, g.l2f_kT, 0, KD, KD, g.sc1, ev_copy(g))
    for hh in range(8):
        wch = []
        for c in range(KD):
            vdst = g.WA.v(c * 1024, 1024)
            p.dma("sy", vdst, g.l2f_vwT.v(slice(c * 128, (c + 1) * 128),
                                          slice(hh * 1024, (hh + 1) * 1024),
                                          key=("w", c, hh)))
            wch.append(vdst)
        for tt_ in range(NTT):
            p.dma("sy", g.XA.v3(0, KD, 128),
                  g.xnf.slab3(0, KD, tt_, ns=128, key_s=tt_ // 4))
            gvt = g.XB.v(4096, 64)
            p.dma("se", gvt, g.gv_t.v(slice(tt_ * 128, (tt_ + 1) * 128),
                                      slice(0, 64), key=(tt_,)))
            vacc = g.XB.v(4224, 128)
            for eh in range(2):
                psb = g.ps[2 + eh]
                for c in range(KD):
                    p.mm(psb.v(), g.XA.v(c * 128, 128),
                         V(wch[c].ap[:, eh * 512:(eh + 1) * 512], wch[c].keys),
                         start=(c == 0), stop=(c == KD - 1))
                for e4 in range(4):
                    e = eh * 4 + e4
                    gcol = V(gvt.ap[:, hh * 8 + e:hh * 8 + e + 1], gvt.keys)
                    if e == 0:
                        p.ts("ve", vacc, psb.v(e4 * 128, 128), gcol, None, ALU.mult)
                    else:
                        p.stt("ve", vacc, psb.v(e4 * 128, 128), gcol, vacc,
                              ALU.mult, ALU.add)
            p.dma("ge", g.vtok.v(slice(tt_ * 128, (tt_ + 1) * 128),
                                 slice(hh * 128, (hh + 1) * 128),
                                 key=(tt_, hh)), vacc)
    for hh in range(8):
        qh = g.XA.v(0, T)
        kh = g.XA.v(4096, T)
        vh = g.XB.v(0, T)
        p.dma("sy", qh, g.sc0.v(slice(hh * 128, (hh + 1) * 128), slice(0, T),
                                key=[(hh, s) for s in range(NSLAB)]))
        p.dma("sy", kh, g.sc1.v(slice(hh * 128, (hh + 1) * 128), slice(0, T),
                                key=[(hh, s) for s in range(NSLAB)]))
        vsrc = g.vtok.t.ap().rearrange("(a p) d -> p a d", p=128)
        vsrc = vsrc[:, :, hh * 128:(hh + 1) * 128]
        p.dma("sy", vh, V(vsrc, [("vtok", t2, hh) for t2 in range(NTT)]))
        owch = []
        for e in range(8):
            vdst = g.WB.v(e * 512, 512)
            p.dma("sy", vdst, g.l2f_ow.v(
                slice(hh * 1024 + e * 128, hh * 1024 + (e + 1) * 128),
                slice(0, 512), key=("ow", hh, e, 0)))
            owch.append(vdst)
        owch2 = []
        for e in range(8):
            vdst = g.WB.v(4096 + e * 512, 512)
            p.dma("sy", vdst, g.l2f_ow.v(
                slice(hh * 1024 + e * 128, hh * 1024 + (e + 1) * 128),
                slice(512, 1024), key=("ow", hh, e, 1)))
            owch2.append(vdst)
        for qt_ in range(NTT):
            acc = g.YA.v(0, 128)
            nm = [g.SM.v(512, 1), g.SM.v(520, 1)]
            lsum = g.SM.v(528, 1)
            alpha = g.SM.v(536, 1)
            rs_ = g.SM.v(544, 1)
            p.memset("ve", acc, 0.0)
            p.memset("ve", nm[0], 3.0e38)
            p.memset("ve", lsum, 0.0)
            for ks in range(NSLAB):
                pss = g.ps[0]
                p.mm(pss.v(), V(qh.ap[:, qt_ * 128:(qt_ + 1) * 128], qh.keys),
                     V(kh.ap[:, ks * SLAB:(ks + 1) * SLAB], kh.keys),
                     start=True, stop=True)
                nmo, nmn = nm[ks % 2], nm[(ks + 1) % 2]
                nmt = g.SM.v(552, 1)
                p.red(nmt, pss.v(), ALU.max, negate=True)
                p.tt("ve", nmn, nmo, nmt, ALU.min)
                p.act(alpha, nmo, AF.Exp, bias=nmn, scale=-1.0)
                p_sb = g.XB.v(4096, SLAB)
                p.act(p_sb, pss.v(), AF.Exp, bias=nmn, accum=rs_)
                p.stt("ve", lsum, lsum, alpha, rs_, ALU.mult, ALU.add)
                pst = g.ps[1]
                for i4 in range(4):
                    p.transp(pst.v(i4 * 128, 128),
                             V(p_sb.ap[:, i4 * 128:(i4 + 1) * 128], p_sb.keys),
                             g.sb_ident)
                pT = g.XB.v(4608, SLAB)
                p.copy("se", pT, pst.v())
                psv2 = g.ps[2]
                for i4 in range(4):
                    kk = ks * 4 + i4
                    p.mm(psv2.v(0, 128),
                         V(pT.ap[:, i4 * 128:(i4 + 1) * 128], pT.keys),
                         V(vh.ap[:, kk * 128:(kk + 1) * 128], vh.keys),
                         start=(i4 == 0), stop=(i4 == 3))
                p.stt("ve", acc, acc, alpha, psv2.v(0, 128), ALU.mult, ALU.add)
            linv = g.SM.v(560, 1)
            p.recip(linv, lsum)
            p.ts("ve", acc, acc, linv, None, ALU.mult)
            got = g.XB.v(5120, 64)
            p.dma("se", got, g.go_t.v(slice(qt_ * 128, (qt_ + 1) * 128),
                                      slice(0, 64), key=(qt_,)))
            gtile = g.XB.v(5248, 1024)
            wgtT8 = g.XB.v(6272, 1024)
            for e in range(8):
                wgt = g.YA.v(128, 128)
                p.ts("ve", wgt, acc,
                     V(got.ap[:, hh * 8 + e:hh * 8 + e + 1], got.keys),
                     None, ALU.mult)
                pst2 = g.ps[3]
                p.transp(pst2.v(0, 128), wgt, g.sb_ident)
                p.copy("se", V(wgtT8.ap[:, e * 128:(e + 1) * 128], wgtT8.keys),
                       pst2.v(0, 128))
            for e in range(8):
                wgtT = V(wgtT8.ap[:, e * 128:(e + 1) * 128], wgtT8.keys)
                for dc in range(8):
                    ow = owch[e] if dc < 4 else owch2[e]
                    sl = dc % 4
                    p.mm(g.ps[dc].v(0, 128),
                         V(ow.ap[:, sl * 128:(sl + 1) * 128], ow.keys),
                         wgtT, start=(e == 0), stop=(e == 7))
            for dc in range(8):
                p.copy("se", V(gtile.ap[:, dc * 128:(dc + 1) * 128], gtile.keys),
                       g.ps[dc].v(0, 128))
            dst = g.sc2.slab3(0, KD, qt_, ns=128, key_s=qt_ // 4)
            if hh == 0:
                p.dma("ge", dst, g.XB.v3(5248, KD, 128))
            else:
                p.dma("ge", dst, g.XB.v3(5248, KD, 128), accum_op=ALU.add)
    for s in range(NSLAB):
        p.dma("sy", g.XA.v3(0, KD, SLAB), g.sc2.slab3(0, KD, s))
        for c in range(KD):
            p.act(g.XA.v(c * SLAB, SLAB), g.XA.v(c * SLAB, SLAB), AF.Silu)
        p.dma("se", g.gf.slab3(0, KD, s), g.XA.v3(0, KD, SLAB))


def emit_l3_attn(g):
    p = g.p
    emit_proj(g, g.xnf, g.l3a_wqT, 0, KD, KD, g.sc0, ev_copy(g))
    wkd = g.XB.v(0, 512)
    wvd = g.XB.v(512, 512)
    for c in range(KD):
        p.dma("sy", V(wkd.ap[:, c * 64:(c + 1) * 64], wkd.keys),
              g.l3a_wkdT.v(slice(c * 128, (c + 1) * 128), slice(0, 64),
                           key=("w", c)))
        p.dma("sy", V(wvd.ap[:, c * 64:(c + 1) * 64], wvd.keys),
              g.l3a_wvdT.v(slice(c * 128, (c + 1) * 128), slice(0, 64),
                           key=("w", c)))
    for s in range(NSLAB):
        p.dma("sy", g.XA.v3(0, KD, SLAB), g.xnf.slab3(0, KD, s))
        psb = g.ps[0]
        for c in range(KD):
            p.mm(psb.v(0, SLAB, 0, 64),
                 V(wkd.ap[:, c * 64:(c + 1) * 64], wkd.keys),
                 g.XA.v(c * SLAB, SLAB), start=(c == 0), stop=(c == KD - 1))
        lkc = g.XB.v(1024, SLAB, 0, 64)
        p.copy("se", lkc, psb.v(0, SLAB, 0, 64))
        p.dma("se", g.sc1.v(slice(0, 64), slice(s * SLAB, (s + 1) * SLAB),
                            key=("lk", s)), lkc)
    for tt_ in range(NTT):
        p.dma("sy", g.XA.v3(0, KD, 128),
              g.xnf.slab3(0, KD, tt_, ns=128, key_s=tt_ // 4))
        psb = g.ps[1]
        for c in range(KD):
            p.mm(psb.v(0, 64), g.XA.v(c * 128, 128),
                 V(wvd.ap[:, c * 64:(c + 1) * 64], wvd.keys),
                 start=(c == 0), stop=(c == KD - 1))
        lvc = g.XB.v(1536, 64)
        p.copy("se", lvc, psb.v(0, 64))
        p.dma("se", g.lv_t.v(slice(tt_ * 128, (tt_ + 1) * 128), slice(0, 64),
                             key=(tt_,)), lvc)
    lk_sb = g.XB.v(2048, T, 0, 64)
    p.dma("sy", lk_sb, g.sc1.v(slice(0, 64), slice(0, T),
                               key=[("lk", s) for s in range(NSLAB)]))
    lv_sb = g.XB.v(0, T // 2)
    lvsrc = g.lv_t.t.ap().rearrange("(a p) l -> p a l", p=128)
    p.dma("sy", lv_sb, V(lvsrc, [("lv_t", t2) for t2 in range(NTT)]))
    mks = g.WB.v(0, 2048)
    dsrc = g.dmask.t.ap().transpose([1, 0, 2])
    p.dma("sy", V(mks.ap.rearrange("p (m c) -> p m c", c=512), mks.keys),
          V(dsrc, [("dmask",)]))
    for hh in range(16):
        qh = g.XA.v(0, T, 0, 64)
        p.dma("sy", qh, g.sc0.v(slice(hh * 64, (hh + 1) * 64), slice(0, T),
                                key=[(hh // 2, s) for s in range(NSLAB)]))
        q2l = g.SM.v(256, 64, 0, 64)
        p.dma("sy", q2l, g.l3a_q2l.v(slice(hh * 64, (hh + 1) * 64),
                                     slice(0, 64), key=("w", hh)))
        vup = g.SM.v(320, 64, 0, 64)
        p.dma("sy", vup, g.l3a_vup.v(slice(hh * 64, (hh + 1) * 64),
                                     slice(0, 64), key=("w", hh)))
        qlat = g.YA.v(0, T, 0, 64)
        for s in range(NSLAB):
            psb = g.ps[0]
            p.mm(psb.v(0, SLAB, 0, 64), q2l,
                 V(qh.ap[:, s * SLAB:(s + 1) * SLAB], qh.keys),
                 start=True, stop=True)
            p.copy("se", V(qlat.ap[:, s * SLAB:(s + 1) * SLAB], qlat.keys),
                   psb.v(0, SLAB, 0, 64))
        for qt_ in range(NTT):
            acc = g.XA.v(4096, 64)
            nm = [g.SM.v(512, 1), g.SM.v(520, 1)]
            lsum = g.SM.v(528, 1)
            alpha = g.SM.v(536, 1)
            rs_ = g.SM.v(544, 1)
            p.memset("ve", acc, 0.0)
            p.memset("ve", nm[0], 3.0e38)
            p.memset("ve", lsum, 0.0)
            nks = qt_ // 4 + 1
            for ks in range(nks):
                pss = g.ps[1]
                p.mm(pss.v(),
                     V(qlat.ap[:, qt_ * 128:(qt_ + 1) * 128], qlat.keys),
                     V(lk_sb.ap[:, ks * SLAB:(ks + 1) * SLAB], lk_sb.keys),
                     start=True, stop=True)
                if ks == qt_ // 4:
                    p.tt("ve", pss.v(), pss.v(),
                         V(mks.ap[:, (qt_ % 4) * SLAB:((qt_ % 4) + 1) * SLAB],
                           mks.keys), ALU.add)
                nmo, nmn = nm[ks % 2], nm[(ks + 1) % 2]
                nmt = g.SM.v(552, 1)
                p.red(nmt, pss.v(), ALU.max, negate=True)
                p.tt("ve", nmn, nmo, nmt, ALU.min)
                p.act(alpha, nmo, AF.Exp, bias=nmn, scale=-1.0)
                p_sb = g.XA.v(4608, SLAB)
                p.act(p_sb, pss.v(), AF.Exp, bias=nmn, accum=rs_)
                p.stt("ve", lsum, lsum, alpha, rs_, ALU.mult, ALU.add)
                pst = g.ps[2]
                for i4 in range(4):
                    p.transp(pst.v(i4 * 128, 128),
                             V(p_sb.ap[:, i4 * 128:(i4 + 1) * 128], p_sb.keys),
                             g.sb_ident)
                pT = g.XA.v(5120, SLAB)
                p.copy("se", pT, pst.v())
                psv = g.ps[3]
                for i4 in range(4):
                    kk = ks * 4 + i4
                    p.mm(psv.v(0, 64),
                         V(pT.ap[:, i4 * 128:(i4 + 1) * 128], pT.keys),
                         V(lv_sb.ap[:, kk * 64:(kk + 1) * 64], lv_sb.keys),
                         start=(i4 == 0), stop=(i4 == 3))
                p.stt("ve", acc, acc, alpha, psv.v(0, 64), ALU.mult, ALU.add)
            linv = g.SM.v(560, 1)
            p.recip(linv, lsum)
            p.ts("ve", acc, acc, linv, None, ALU.mult)
            pst2 = g.ps[4]
            p.transp(pst2.v(0, 128, 0, 64), acc, g.sb_ident)
            ctxT = g.XA.v(5632, 128, 0, 64)
            p.copy("se", ctxT, pst2.v(0, 128, 0, 64))
            psh = g.ps[5]
            p.mm(psh.v(0, 128, 0, 64), vup, ctxT, start=True, stop=True)
            hp_ = g.XA.v(5760, 128, 0, 64)
            p.copy("se", hp_, psh.v(0, 128, 0, 64))
            p.dma("se", g.sc2.v(slice(hh * 64, (hh + 1) * 64),
                                slice(qt_ * 128, (qt_ + 1) * 128),
                                key=(hh // 2, qt_ // 4)), hp_)
    emit_proj(g, g.sc2, g.l3a_woT, 0, KD, KD, g.sf, ev_copy(g))


def emit_l3_ffn(g):
    p = g.p
    DH3 = 3 * D
    for jg in range(3):
        for w_i, wdr in ((0, g.l3f_wgT), (1, g.l3f_wuT)):
            for c in range(KD):
                iw = g.BI.v((c % 2) * 1024, 1024)
                p.dma("sy", iw, wdr.v(slice(c * 128, (c + 1) * 128),
                                      slice(jg * D, (jg + 1) * D),
                                      key=("w", c, jg)))
                p.act(g.WA.v(w_i * 8192 + c * 1024, 1024), iw, AF.Copy,
                      bias=0.0, scale=float(1.0 / 127.0))
        for s in range(NSLAB):
            p.dma("sy", g.XA.v3(0, KD, SLAB), g.xnf.slab3(0, KD, s))
            xs = [g.XA.v(c * SLAB, SLAB) for c in range(KD)]
            for j in range(KD):
                psg = g.ps[j % 2]
                psu = g.ps[2 + j % 2]
                for c in range(KD):
                    p.mm(psg.v(), V(g.WA.v(c * 1024, 1024).ap[:, j * 128:(j + 1) * 128],
                                    g.WA.v(c * 1024, 1024).keys),
                         xs[c], start=(c == 0), stop=(c == KD - 1))
                for c in range(KD):
                    p.mm(psu.v(), V(g.WA.v(8192 + c * 1024, 1024).ap[:, j * 128:(j + 1) * 128],
                                    g.WA.v(8192 + c * 1024, 1024).keys),
                         xs[c], start=(c == 0), stop=(c == KD - 1))
                hcell = g.XB.v(j * SLAB, SLAB)
                p.act(hcell, psg.v(), AF.Silu)
                p.tt("ve", hcell, hcell, psu.v(), ALU.mult)
            p.dma("se", g.hidf.slab3(jg * KD, KD, s), g.XB.v3(0, KD, SLAB))
    for s in range(NSLAB):
        p.dma("sy", g.WA.v3(0, 3 * KD, SLAB), g.hidf.slab3(0, 3 * KD, s))
        hs = [g.WA.v(k * SLAB, SLAB) for k in range(3 * KD)]
        for k in range(3 * KD):
            iw = g.BI.v((k % 2) * 1024, 1024)
            p.dma("sy", iw, g.l3f_wdT.v(slice(k * 128, (k + 1) * 128),
                                        slice(0, D), key=("w", k, s)))
            wk = g.XA.v((k % 2) * D, D)
            p.act(wk, iw, AF.Copy, bias=0.0, scale=float(1.0 / 127.0))
            for j in range(KD):
                p.mm(g.ps[j].v(), V(wk.ap[:, j * 128:(j + 1) * 128], wk.keys),
                     hs[k], start=(k == 0), stop=(k == 3 * KD - 1))
        for j in range(KD):
            ot = g.XB.v(4096 + j * SLAB, SLAB)
            p.act(ot, g.ps[j].v(), AF.Silu)
        p.dma("se", g.gf.slab3(0, KD, s), g.XB.v3(4096, KD, SLAB))


def emit_combine(g, xin, xout):
    p = g.p
    for s in range(NSLAB):
        p.dma("sy", g.XA.v3(0, KD, SLAB), xin.slab3(0, KD, s))
        p.dma("sy", g.XA.v3(4096, KD, SLAB), g.sf.slab3(0, KD, s))
        p.dma("ge", g.XB.v3(0, KD, SLAB), g.gf.slab3(0, KD, s))
        for c in range(KD):
            xc = g.XA.v(c * SLAB, SLAB)
            sc_ = g.XA.v(4096 + c * SLAB, SLAB)
            gc_ = g.XB.v(c * SLAB, SLAB)
            eng = "ve" if c % 2 else "ge"
            p.tt(eng, sc_, sc_, gc_, ALU.mult)
            p.tt(eng, sc_, sc_, xc, ALU.add)
        p.dma("se", xout.slab3(0, KD, s), g.XA.v3(4096, KD, SLAB))


def emit_all(g):
    p = g.p
    g.sb_bq = g.SM.v(256, 8)
    g.sb_bk = g.SM.v(264, 8)
    g.sb_bo = g.SM.v(272, 8)
    g.sb_bin = g.SM.v(280, 32)
    g.sb_bout = g.SM.v(312, 8)
    p.dma("sy", g.sb_bq, g.l0_bq.whole())
    p.dma("sy", g.sb_bk, g.l0_bk.whole())
    p.dma("sy", g.sb_bo, g.l0_bo.whole())
    p.dma("sy", g.sb_bin, g.l0_bin.whole())
    p.dma("sy", g.sb_bout, g.l0_bout.whole())

    xcur = g.x_f
    emitters = [(emit_l0_attn, emit_l0_ffn), (emit_l1_attn, emit_l1_ffn),
                (emit_l2_attn, emit_l2_ffn), (emit_l3_attn, emit_l3_ffn)]
    for blk in range(N_BLOCKS):
        emit_ln(g, xcur)
        emitters[blk][0](g)
        emitters[blk][1](g)
        last = (blk == N_BLOCKS - 1)
        tgt = g.out_f if (last and DEV_TAP is None) else g.xf[blk % 2]
        emit_combine(g, xcur, tgt)
        xcur = tgt

    if DEV_TAP == "alpha":
        for tt_ in range(NTT):
            av = g.XA.v(0, 64)
            p.dma("sy", av, g.alpha_t.v(slice(tt_ * 128, (tt_ + 1) * 128),
                                        slice(0, 64), key=(tt_,)))
            pst = g.ps[0]
            p.transp(pst.v(0, 128, 0, 64), av, g.sb_ident)
            ot = g.XB.v(0, 128, 0, 64)
            p.copy("se", ot, pst.v(0, 128, 0, 64))
            p.dma("se", g.out_f.v(slice(0, 64),
                                  slice(tt_ * 128, (tt_ + 1) * 128),
                                  key=("a", tt_)), ot)
        return
    if DEV_TAP is not None:
        srcd = (g.xf[int(DEV_TAP[2])] if DEV_TAP.startswith("xf")
                else getattr(g, DEV_TAP))
        for s in range(NSLAB):
            ks = ("qt", s) if DEV_TAP == "attnf" else None
            p.dma("sy", g.XA.v3(0, KD, SLAB), srcd.slab3(0, KD, s, key_s=ks))
            p.dma("se", g.out_f.slab3(0, KD, s), g.XA.v3(0, KD, SLAB))


def kernel(**inputs):
    h, cores = host_prepare(inputs)
    nc = build_program()
    in_maps = []
    for c in range(NCORES):
        m = dict(h)
        m.update(cores[c])
        in_maps.append(m)
    res = run_bass_kernel_spmd(nc, in_maps, core_ids=list(range(NCORES)))
    out = np.stack([np.ascontiguousarray(np.asarray(r["out_f"]).T)
                    for r in res.results])
    return out.astype(np.float32)
